# version 05
# speedup vs baseline: 1.2266x; 1.0131x over previous
import numpy as np
import ml_dtypes
from contextlib import ExitStack

import concourse.bass as bass
import concourse.mybir as mybir
from concourse.bass_utils import run_bass_kernel_spmd

F32 = mybir.dt.float32
BF16 = mybir.dt.bfloat16
U8 = mybir.dt.uint8
AF = mybir.ActivationFunctionType
ALU = mybir.AluOpType
AX = mybir.AxisListType

D = 1024
T = 4096
NCORES = 8

BF16NP = ml_dtypes.bfloat16


CONST_SPECS = {
    "c_ident_bf": ([128, 128], BF16),
    "c_ident_f32": ([128, 128], F32),
    "c_ones_row": ([1, 128], F32),
    "c_ones_col_bf": ([128, 1], BF16),
    "c_swpq": ([128, 128], F32),
    "c_diagmask": ([128, 896], BF16),
    "c_ut01": ([128, 128], F32),
    "c_sl01": ([128, 128], F32),
    "c_slbig": ([128, 128], F32),
    "c_invfrep": ([128, 512], F32),
    "c_invf16": ([32, 16], F32),
    "c_swpb": ([64, 64], BF16),
}
NFVALS = 64


def host_constants():
    c = {}
    c["c_ident_bf"] = np.eye(128, dtype=BF16NP)
    c["c_ident_f32"] = np.eye(128, dtype=np.float32)
    c["c_ones_row"] = np.ones((1, 128), dtype=np.float32)
    c["c_ones_col_bf"] = np.ones((128, 1), dtype=BF16NP)
    swp = np.zeros((64, 64), np.float32)
    for i in range(32):
        swp[2 * i, 2 * i + 1] = -1.0
        swp[2 * i + 1, 2 * i] = 1.0
    swpT = swp.T
    sw = np.zeros((128, 128), np.float32)
    sw[0:64, 0:64] = swpT
    sw[64:128, 64:128] = swpT
    c["c_swpq"] = sw
    s_i = np.arange(128)[:, None]
    u_i = np.arange(896)[None, :]
    c["c_diagmask"] = (s_i <= u_i - 384).astype(BF16NP)
    k_i = np.arange(128)[:, None]
    m_i = np.arange(128)[None, :]
    c["c_ut01"] = (k_i <= m_i).astype(np.float32)
    c["c_sl01"] = (k_i > m_i).astype(np.float32)
    c["c_slbig"] = (1e30 * (k_i > m_i)).astype(np.float32)
    invf = (1.0 / 10000.0 ** (np.arange(16) / 16.0)).astype(np.float32)
    c["c_invfrep"] = np.broadcast_to(np.tile(invf, 32)[None, :], (128, 512)).copy()
    c["c_invf16"] = np.broadcast_to(invf[None, :], (32, 16)).copy()
    swpb = np.zeros((64, 64), np.float32)
    for i in range(16):
        swpb[i, 16 + i] = -1.0
        swpb[16 + i, i] = 1.0
    c["c_swpb"] = swpb.T.astype(BF16NP)
    return c


def rope_tables():
    inv = 1.0 / 10000.0 ** (np.arange(0, 64, 2, dtype=np.float64) / 64)
    t = np.arange(T, dtype=np.float64)
    fr = np.outer(t, inv)
    c64 = np.repeat(np.cos(fr), 2, axis=1).T
    s64 = np.repeat(np.sin(fr), 2, axis=1).T
    c128 = np.concatenate([c64, c64], 0).astype(np.float32)
    s128 = np.concatenate([s64, s64], 0).astype(np.float32)
    return np.stack([0.125 * c128, 0.125 * s128, c128, s128]).astype(np.float32)


class KB:
    def __init__(self):
        self.nc = bass.Bass("TRN2", target_bir_lowering=False)
        self.stack = ExitStack()
        self.uid = 0
        self.const_sb = {}
        self.store_sem = None
        self.store_cnt = 0

    def name(self, p):
        self.uid += 1
        return f"{p}_{self.uid}"

    def sb(self, p, shape, dt, side="left"):
        return self.stack.enter_context(
            self.nc.sbuf_tensor(self.name(p), shape, dt, side=side))

    def sem(self, p="sem"):
        return self.stack.enter_context(self.nc.semaphore(self.name(p)))

    def din(self, nm, shape, dt):
        return self.nc.dram_tensor(nm, list(shape), dt, kind="ExternalInput").ap()

    def dout(self, nm, shape, dt):
        return self.nc.dram_tensor(nm, list(shape), dt, kind="ExternalOutput").ap()

    def dscr(self, nm, shape, dt):
        return self.nc.dram_tensor(self.name(nm), list(shape), dt).ap()

    def init(self):
        nc = self.nc
        self.store_sem = self.sem("g_store")
        self.ff_sem = self.sem("g_ff")
        self.ff_cnt = 0
        self.chain_init()
        self.fvals = {}
        self.fpool = self.sb("c_fpool_sb", [128, NFVALS], F32)
        for nm, (shape, dt) in CONST_SPECS.items():
            self.const_sb[nm] = self.sb(nm, shape, dt)
        ld = self.sem("c_ld")
        n = 0
        specs = list(CONST_SPECS.items())
        cin = {nm: self.din(nm, shape, dt) for nm, (shape, dt) in specs}
        fv_in = self.din("c_fvals", [128, NFVALS], F32)
        with nc.Block() as blk:
            @blk.sync
            def _(sync):
                nonlocal n
                for nm, _spec in specs:
                    sync.dma_start(self.const_sb[nm][:, :], cin[nm]).then_inc(ld, 16)
                    n += 1
                sync.dma_start(self.fpool[:, :], fv_in).then_inc(ld, 16)
                n += 1
                sync.wait_ge(ld, 16 * n)

    def chain_init(self):
        self.chsem = {"v": self.sem("chv"), "s": self.sem("chs")}
        self.chcnt = {"v": 0, "s": 0}
        self.lnsem = {n: self.sem("ln" + n)
                      for n in ("ld", "ld2", "s1", "s2", "t1", "t2", "t3")}
        self.lncnt = {n: 0 for n in ("ld", "ld2", "s1", "s2", "t1", "t2", "t3")}

    def step(self, key, eng, ins):
        self.chcnt[key] += 1
        ins.then_inc(self.chsem[key], 1)
        eng.wait_ge(self.chsem[key], self.chcnt[key])

    def fc(self, value):
        v = float(value)
        if v not in self.fvals:
            idx = len(self.fvals)
            assert idx < NFVALS, "float const pool exhausted"
            self.fvals[v] = idx
            self.nc.const_aps.register(F32, v, self.fpool[:, idx:idx + 1])
        return v

    def fvals_array(self):
        arr = np.zeros((128, NFVALS), dtype=np.float32)
        for v, i in self.fvals.items():
            arr[:, i] = v
        return arr

    def dma_ff(self, q, dst_ap, src_ap):
        q.dma_start(dst_ap, src_ap).then_inc(self.ff_sem, 16)
        self.ff_cnt += 16

    def ffw(self, eng):
        eng.wait_ge(self.ff_sem, self.ff_cnt)

    def store(self, q, dst_ap, src_ap):
        q.dma_start(dst_ap, src_ap).then_inc(self.store_sem, 16)
        self.store_cnt += 16

    def finalize(self):
        nc = self.nc
        cnt = self.store_cnt
        with nc.Block() as blk:
            @blk.sync
            def _(sync):
                sync.wait_ge(self.store_sem, cnt)
        self.stack.close()
        return nc


def xview(x_hbm):
    return x_hbm.rearrange("(c p) t -> p c t", p=128)


def emit_ln_slab(kb, st, x_sl, xn_bf, scr_bf, nsl, xsrc=None, prefetch=(),
                 ld_wait=None):
    nc = kb.nc
    onescol = kb.const_sb["c_ones_col_bf"]
    onesrow = kb.const_sb["c_ones_row"]
    nq = nsl // 512
    L, LC = kb.lnsem, kb.lncnt
    ps_sum = st.enter_context(nc.psum_tensor(kb.name("ps_sum"), [1, nsl], F32))
    ps_sq = st.enter_context(nc.psum_tensor(kb.name("ps_sq"), [1, nsl], F32))
    ps_mb = st.enter_context(nc.psum_tensor(kb.name("ps_mb"), [128, nsl], F32))
    ps_rb = st.enter_context(nc.psum_tensor(kb.name("ps_rb"), [128, nsl], F32))
    m_row = st.enter_context(nc.sbuf_tensor(kb.name("m_row"), [1, nsl], F32))
    v_row = st.enter_context(nc.sbuf_tensor(kb.name("v_row"), [1, nsl], F32))
    r_row = st.enter_context(nc.sbuf_tensor(kb.name("r_row"), [1, nsl], F32))
    tmp = st.enter_context(nc.sbuf_tensor(kb.name("ln_tmp"), [128, nsl], F32))
    scr2 = st.enter_context(nc.sbuf_tensor(kb.name("ln_scr2"), [128, 8, nsl], BF16))
    b = {n: LC[n] for n in LC}
    bv, bs = kb.chcnt["v"], kb.chcnt["s"]

    with nc.Block() as blk:
        if xsrc is not None or prefetch:
            @blk.sync
            def _(sync):
                kb.ffw(sync)
                if xsrc is not None:
                    sync.dma_start(x_sl[:, :, :], xsrc).then_inc(L["ld"], 16)
                for dst, srcap, snm in prefetch:
                    sync.dma_start(dst, srcap).then_inc(L[snm], 16)
            if xsrc is not None:
                LC["ld"] += 16
            for dst, srcap, snm in prefetch:
                LC[snm] += 16

        @blk.scalar
        def _(s):
            if xsrc is not None:
                s.wait_ge(L["ld"], b["ld"] + 16)
            elif ld_wait is not None:
                s.wait_ge(L[ld_wait[0]], ld_wait[1])
            kb.ffw(s)
            s.activation(scr_bf[:, :, :], x_sl[:, :, :], AF.Copy).then_inc(L["s1"], 1)
            s.activation(scr2[:, :, :], x_sl[:, :, :], AF.Square).then_inc(L["s2"], 1)
            s.wait_ge(kb.chsem["v"], bv + 4)
            kb.step("s", s, s.activation(v_row[:, :], v_row[:, :], AF.Ln, bias=kb.fc(1e-5)))
            kb.step("s", s, s.activation(r_row[:, :], v_row[:, :], AF.Exp, scale=kb.fc(-0.5)))

        @blk.tensor
        def _(t):
            t.wait_ge(L["s1"], b["s1"] + 1)
            for c in range(8):
                for h in range(nq):
                    t.matmul(ps_sum[0:1, h * 512:(h + 1) * 512], onescol[:, :],
                             scr_bf[:, c, h * 512:(h + 1) * 512],
                             start=(c == 0), stop=(c == 7))
            t.wait_ge(L["s2"], b["s2"] + 1)
            ins = None
            for c in range(8):
                for h in range(nq):
                    ins = t.matmul(ps_sq[0:1, h * 512:(h + 1) * 512], onescol[:, :],
                                   scr2[:, c, h * 512:(h + 1) * 512],
                                   start=(c == 0), stop=(c == 7))
            ins.then_inc(L["t1"], 1)
            t.wait_ge(kb.chsem["s"], bs + 2)
            for h in range(nq):
                sl = slice(h * 512, (h + 1) * 512)
                t.matmul(ps_mb[:, sl], onesrow[:, :], m_row[0:1, sl], start=True, stop=True)
                ins = t.matmul(ps_rb[:, sl], onesrow[:, :], r_row[0:1, sl], start=True, stop=True)
            ins.then_inc(L["t3"], 1)

        @blk.vector
        def _(v):
            v.wait_ge(L["t1"], b["t1"] + 1)
            kb.step("v", v, v.tensor_scalar_mul(m_row[:, :], ps_sum[:, :], kb.fc(1.0 / D)))
            kb.step("v", v, v.tensor_scalar_mul(v_row[:, :], ps_sq[:, :], kb.fc(1.0 / D)))
            kb.step("v", v, v.tensor_mul(r_row[:, :], m_row[:, :], m_row[:, :]))
            kb.step("v", v, v.tensor_sub(v_row[:, :], v_row[:, :], r_row[:, :]))
            v.wait_ge(L["t3"], b["t3"] + 1)
            kb.ffw(v)
            for c in range(8):
                kb.step("v", v, v.tensor_sub(tmp[:, :], x_sl[:, c, :], ps_mb[:, :]))
                kb.step("v", v, v.tensor_mul(xn_bf[:, c, :], tmp[:, :], ps_rb[:, :]))
    LC["s1"] += 1
    LC["s2"] += 1
    LC["t1"] += 1
    LC["t3"] += 1


def emit_scalar_cols2(kb, st, scalar_aps):
    nc = kb.nc
    onesrow = kb.const_sb["c_ones_row"]
    names = list(scalar_aps)
    n = len(names)
    row = st.enter_context(nc.sbuf_tensor(kb.name("sc_row"), [1, n], F32))
    allc = st.enter_context(nc.sbuf_tensor(kb.name("sc_cols"), [128, n], F32))
    ld = kb.sem("sc_ld")
    with nc.Block() as blk:
        @blk.sync
        def _(sync):
            for i, nm in enumerate(names):
                sync.dma_start(row[0:1, i:i + 1], scalar_aps[nm]).then_inc(ld, 16)
            sync.wait_ge(ld, 16 * n)
    with nc.psum_tensor(kb.name("sc_ps"), [128, n], F32) as ps:
        with nc.Block() as blk:
            @blk.tensor
            def _(t):
                t.matmul(ps[:, :], onesrow[:, :], row[:, :], start=True, stop=True)
        with nc.Block() as blk:
            @blk.vector
            def _(v):
                v.tensor_copy(allc[:, :], ps[:, :])
    return {nm: allc[:, i:i + 1] for i, nm in enumerate(names)}


def emit_l1ffn_preconvert(kb, k0_hbm, k1_hbm, k2_hbm, k0e_hbm, k1bf_hbm, k2bf_hbm):
    nc = kb.nc
    with ExitStack() as st:
        ta = st.enter_context(nc.sbuf_tensor(kb.name("pc_a"), [128, 4096], BF16))
        tb = st.enter_context(nc.sbuf_tensor(kb.name("pc_b"), [128, 4096], BF16))
        tc = st.enter_context(nc.sbuf_tensor(kb.name("pc_c"), [128, 4096], BF16))
        ld = kb.sem("pc_ld")
        stc = kb.sem("pc_st")
        nld = 0
        nst = 0
        for c in range(8):
            with nc.Block() as blk:
                @blk.gpsimd
                def _(g):
                    nonlocal nld
                    g.dma_start(ta[:, :], k0_hbm[c * 128:(c + 1) * 128, :]).then_inc(ld, 16)
                    g.dma_start(tb[:, :], k0_hbm[1024 + c * 128:1024 + (c + 1) * 128, :]).then_inc(ld, 16)
                    nld += 32
                    g.wait_ge(ld, nld)
            with nc.Block() as blk:
                @blk.vector
                def _(v):
                    v.tensor_sub(tc[:, :], ta[:, :], tb[:, :])
            with nc.Block() as blk:
                @blk.sync
                def _(sync):
                    nonlocal nst
                    sync.dma_start(k0e_hbm[c * 128:(c + 1) * 128, :], tc[:, :]).then_inc(stc, 16)
                    nst += 16
                    sync.wait_ge(stc, nst)
        for src, dst, nchunks, w in ((k1_hbm, k1bf_hbm, 32, 4096), (k2_hbm, k2bf_hbm, 32, 1024)):
            for c in range(nchunks):
                with nc.Block() as blk:
                    @blk.gpsimd
                    def _(g):
                        nonlocal nld
                        g.dma_start(ta[:, 0:w], src[c * 128:(c + 1) * 128, :]).then_inc(ld, 16)
                        nld += 16
                        g.wait_ge(ld, nld)
                with nc.Block() as blk:
                    @blk.scalar
                    def _(s):
                        s.activation(tc[:, 0:w], ta[:, 0:w], AF.Copy, bias=kb.fc(-1.0), scale=kb.fc(2.0))
                with nc.Block() as blk:
                    @blk.sync
                    def _(sync):
                        nonlocal nst
                        sync.dma_start(dst[c * 128:(c + 1) * 128, :], tc[:, 0:w]).then_inc(stc, 16)
                        nst += 16
                        sync.wait_ge(stc, nst)


class L1FFNSems:
    def __init__(self, kb):
        names = ["ldx", "ldk0", "ldk1", "ldk1b", "ldk2", "mm1", "ev1", "mm2",
                 "ev2", "mm3", "ev3"]
        self.s = {n: kb.sem("l1f_" + n) for n in names}
        self.c = {n: 0 for n in names}


def emit_l1ffn(kb, ffs, x_in_hbm, x_out_hbm, k0e_hbm, k1bf_hbm, k2bf_hbm, scols):
    nc = kb.nc
    S, C = ffs.s, ffs.c
    NSL = 1024
    xin_v = xview(x_in_hbm)
    xout_v = xview(x_out_hbm)
    k0e_v = k0e_hbm.rearrange("(c p) w -> p c w", p=128)
    k1_v = k1bf_hbm.rearrange("(c p) m -> p c m", p=128)
    k2_v = k2bf_hbm.rearrange("(c p) m -> p c m", p=128)

    for isl in range(T // NSL):
        tsl = slice(isl * NSL, (isl + 1) * NSL)
        with ExitStack() as st_xn:
            xn = st_xn.enter_context(nc.sbuf_tensor(kb.name("xn"), [128, 8, NSL], BF16))
            with ExitStack() as st_ln:
                x_sl = st_ln.enter_context(nc.sbuf_tensor(kb.name("x_sl"), [128, 8, NSL], F32))
                scr = st_ln.enter_context(nc.sbuf_tensor(kb.name("scr"), [128, 8, NSL], BF16))
                emit_ln_slab(kb, st_ln, x_sl, xn, scr, NSL, xsrc=xin_v[:, :, tsl])
            with ExitStack() as st_a2:
                a2 = st_a2.enter_context(
                    nc.sbuf_tensor(kb.name("a2"), [128, 32, NSL], BF16, side="right"))
                with ExitStack() as st_a1:
                    a1 = st_a1.enter_context(nc.sbuf_tensor(kb.name("a1"), [128, 32, NSL], BF16))
                    with ExitStack() as st_k0:
                        k0e = st_k0.enter_context(nc.sbuf_tensor(kb.name("k0e"), [128, 8, 2048], BF16))
                        gph = 16 * (NSL // 512)
                        with ExitStack() as st_ps, nc.Block() as blk:
                            psl = [st_ps.enter_context(nc.psum_tensor(kb.name("p1"), [128, 512], F32))
                                   for _ in range(4)]
                            mm0, ev0, ld0 = C["mm1"], C["ev1"], C["ldk0"]

                            @blk.sync
                            def _(sync):
                                for wh in range(2):
                                    if wh == 1:
                                        sync.wait_ge(S["mm1"], mm0 + gph)
                                    sync.dma_start(k0e[:, :, :],
                                                   k0e_v[:, :, wh * 2048:(wh + 1) * 2048]
                                                   ).then_inc(S["ldk0"], 16)

                            @blk.tensor
                            def _(t):
                                g = 0
                                for wh in range(2):
                                    t.wait_ge(S["ldk0"], ld0 + 16 * (wh + 1))
                                    for w in range(16):
                                        for h in range(NSL // 512):
                                            if g >= 4:
                                                t.wait_ge(S["ev1"], ev0 + g - 3)
                                            for k in range(8):
                                                ins = t.matmul(psl[g % 4][:, :],
                                                               k0e[:, k, w * 128:(w + 1) * 128],
                                                               xn[:, k, h * 512:(h + 1) * 512],
                                                               start=(k == 0), stop=(k == 7))
                                            ins.then_inc(S["mm1"], 1)
                                            g += 1

                            @blk.scalar
                            def _(s):
                                g = 0
                                for wh in range(2):
                                    for w in range(16):
                                        for h in range(NSL // 512):
                                            s.wait_ge(S["mm1"], mm0 + g + 1)
                                            s.activation(a1[:, wh * 16 + w, h * 512:(h + 1) * 512],
                                                         psl[g % 4][:, :], AF.Relu,
                                                         scale=scols["s0x2"]).then_inc(S["ev1"], 1)
                                            g += 1
                            C["ldk0"] += 32
                            C["mm1"] += 64
                            C["ev1"] += 64
                    with ExitStack() as st_k1:
                        k1b = [st_k1.enter_context(
                                   nc.sbuf_tensor(kb.name("k1b"), [128, 32, 256], BF16, side="right"))
                               for _ in range(2)]
                        npc = 16
                        gpp = 2 * (NSL // 512)
                        with ExitStack() as st_ps, nc.Block() as blk:
                            psl = [st_ps.enter_context(nc.psum_tensor(kb.name("p2"), [128, 512], F32))
                                   for _ in range(4)]
                            mm0, ev0 = C["mm2"], C["ev2"]
                            ld0 = [C["ldk1"], C["ldk1b"]]
                            k1sem = [S["ldk1"], S["ldk1b"]]

                            @blk.sync
                            def _(sync):
                                for pc in range(npc):
                                    if pc >= 2:
                                        sync.wait_ge(S["mm2"], mm0 + gpp * (pc - 1))
                                    sync.dma_start(k1b[pc % 2][:, :, :],
                                                   k1_v[:, :, pc * 256:(pc + 1) * 256]
                                                   ).then_inc(k1sem[pc % 2], 16)

                            @blk.tensor
                            def _(t):
                                g = 0
                                for pc in range(npc):
                                    t.wait_ge(k1sem[pc % 2], ld0[pc % 2] + 16 * (pc // 2 + 1))
                                    for mc in range(2):
                                        for h in range(NSL // 512):
                                            if g >= 4:
                                                t.wait_ge(S["ev2"], ev0 + g - 3)
                                            for k in range(32):
                                                ins = t.matmul(psl[g % 4][:, :],
                                                               k1b[pc % 2][:, k, mc * 128:(mc + 1) * 128],
                                                               a1[:, k, h * 512:(h + 1) * 512],
                                                               start=(k == 0), stop=(k == 31))
                                            ins.then_inc(S["mm2"], 1)
                                            g += 1

                            @blk.scalar
                            def _(s):
                                g = 0
                                for pc in range(npc):
                                    for mc in range(2):
                                        for h in range(NSL // 512):
                                            s.wait_ge(S["mm2"], mm0 + g + 1)
                                            s.activation(a2[:, pc * 2 + mc, h * 512:(h + 1) * 512],
                                                         psl[g % 4][:, :], AF.Relu,
                                                         scale=scols["s1x2"]).then_inc(S["ev2"], 1)
                                            g += 1
                            C["ldk1"] += 16 * (npc // 2)
                            C["ldk1b"] += 16 * (npc // 2)
                            C["mm2"] += npc * gpp
                            C["ev2"] += npc * gpp
                with ExitStack() as st_m3:
                    k2sb = st_m3.enter_context(nc.sbuf_tensor(kb.name("k2sb"), [128, 32, 1024], BF16))
                    x_sl2 = st_m3.enter_context(nc.sbuf_tensor(kb.name("x_sl2"), [128, 8, NSL], F32))
                    x_out = x_sl2
                    with ExitStack() as st_ps, nc.Block() as blk:
                        psl = [st_ps.enter_context(nc.psum_tensor(kb.name("p3"), [128, 512], F32))
                               for _ in range(4)]
                        mm0, ev0, ld0 = C["mm3"], C["ev3"], C["ldk2"]

                        @blk.sync
                        def _(sync):
                            sync.dma_start(k2sb[:, :, :], k2_v).then_inc(S["ldk2"], 16)
                            sync.dma_start(x_sl2[:, :, :], xin_v[:, :, tsl]).then_inc(S["ldk2"], 16)
                            sync.wait_ge(S["ev3"], ev0 + 8 * (NSL // 512))
                            kb.store(sync, xout_v[:, :, tsl], x_out[:, :, :])

                        @blk.tensor
                        def _(t):
                            t.wait_ge(S["ldk2"], ld0 + 32)
                            g = 0
                            for dout in range(8):
                                for h in range(NSL // 512):
                                    if g >= 4:
                                        t.wait_ge(S["ev3"], ev0 + g - 3)
                                    for k in range(32):
                                        ins = t.matmul(psl[g % 4][:, :],
                                                       k2sb[:, k, dout * 128:(dout + 1) * 128],
                                                       a2[:, k, h * 512:(h + 1) * 512],
                                                       start=(k == 0), stop=(k == 31))
                                    ins.then_inc(S["mm3"], 1)
                                    g += 1

                        @blk.vector
                        def _(v):
                            v.wait_ge(S["ldk2"], ld0 + 32)
                            g = 0
                            for dout in range(8):
                                for h in range(NSL // 512):
                                    v.wait_ge(S["mm3"], mm0 + g + 1)
                                    sl = slice(h * 512, (h + 1) * 512)
                                    v.scalar_tensor_tensor(
                                        x_out[:, dout, sl], psl[g % 4][:, :],
                                        scols["s2h"], x_sl2[:, dout, sl],
                                        ALU.mult, ALU.add).then_inc(S["ev3"], 1)
                                    g += 1
                        C["ldk2"] += 32
                        C["mm3"] += 16
                        C["ev3"] += 16


def emit_xin_copy(kb, x_ext, xres):
    nc = kb.nc
    sem = kb.sem("xc")
    n = 0
    xi = xview(x_ext)
    xo = xview(xres)
    with ExitStack() as st:
        big = st.enter_context(nc.sbuf_tensor(kb.name("xcopy"), [128, 8, T // 2], F32))
        for h in range(2):
            tsl = slice(h * (T // 2), (h + 1) * (T // 2))
            with nc.Block() as blk:
                @blk.sync
                def _(sync):
                    nonlocal n
                    sync.dma_start(big[:, :, :], xi[:, :, tsl]).then_inc(sem, 16)
                    n += 16
                    sync.wait_ge(sem, n)
                    sync.dma_start(xo[:, :, tsl], big[:, :, :]).then_inc(sem, 16)
                    n += 16
                    sync.wait_ge(sem, n)


def emit_moe_preconvert(kb, w1_hbm, w2_hbm, w1bf_hbm, w2bf_hbm):
    nc = kb.nc
    w1v = w1_hbm.rearrange("(c p) m -> p c m", p=128)
    w1bv = w1bf_hbm.rearrange("(c p) m -> p c m", p=128)
    w2v = w2_hbm.rearrange("(c p) d -> p c d", p=128)
    w2bv = w2bf_hbm.rearrange("(c p) d -> p c d", p=128)
    with ExitStack() as st:
        buf = st.enter_context(nc.sbuf_tensor(kb.name("pcw"), [128, 8192], BF16))
        ld = kb.sem("pcw_ld")
        stc = kb.sem("pcw_st")
        nld = nst = 0
        for c in range(8):
            with nc.Block() as blk:
                @blk.gpsimd
                def _(g):
                    nonlocal nld
                    g.dma_start(buf[:, :], w1v[:, c, :]).then_inc(ld, 16)
                    nld += 16
                    g.wait_ge(ld, nld)
            with nc.Block() as blk:
                @blk.sync
                def _(sync):
                    nonlocal nst
                    sync.dma_start(w1bv[:, c, :], buf[:, :]).then_inc(stc, 16)
                    nst += 16
                    sync.wait_ge(stc, nst)
        bview = buf[:, :].rearrange("p (c d) -> p c d", c=8)
        for g8 in range(8):
            with nc.Block() as blk:
                @blk.gpsimd
                def _(g):
                    nonlocal nld
                    g.dma_start(bview, w2v[:, g8 * 8:(g8 + 1) * 8, :]).then_inc(ld, 16)
                    nld += 16
                    g.wait_ge(ld, nld)
            with nc.Block() as blk:
                @blk.sync
                def _(sync):
                    nonlocal nst
                    sync.dma_start(w2bv[:, g8 * 8:(g8 + 1) * 8, :], bview).then_inc(stc, 16)
                    nst += 16
                    sync.wait_ge(stc, nst)


class MoESems:
    def __init__(self, kb):
        names = ["ldx", "ldw1", "ldw1b", "ldw2", "ldw2b", "ldr", "mm1", "rel",
                 "va", "gb", "mm2", "ev2", "lg"]
        self.s = {n: kb.sem("moe_" + n) for n in names}
        self.c = {n: 0 for n in names}


def emit_moe(kb, ms, x_in_hbm, x_out_hbm, w1bf_hbm, w2bf_hbm, rw_hbm, dbg=None, parts="all"):
    nc = kb.nc
    S, C = ms.s, ms.c
    NSL = 512
    ident = kb.const_sb["c_ident_f32"]
    onesrow = kb.const_sb["c_ones_row"]
    xin_v = xview(x_in_hbm)
    xout_v = xview(x_out_hbm)
    w1bv = w1bf_hbm.rearrange("(c p) m -> p c m", p=128)
    w2bv = w2bf_hbm.rearrange("(c p) d -> p c d", p=128)

    g_scr = kb.dscr("g_scr", [8, NSL], F32)
    with ExitStack() as st_call:
        rw_sb = st_call.enter_context(nc.sbuf_tensor(kb.name("rw"), [8, 1024], F32))
        rwT = st_call.enter_context(nc.sbuf_tensor(kb.name("rwT"), [128, 8, 8], BF16))
        with nc.Block() as blk:
            @blk.sync
            def _(sync):
                sync.dma_start(rw_sb[:, :], rw_hbm).then_inc(S["ldr"], 16)
                C["ldr"] += 16
                sync.wait_ge(S["ldr"], C["ldr"])
        with ExitStack() as st_ps:
            ps_rt = st_ps.enter_context(nc.psum_tensor(kb.name("ps_rt"), [128, 8, 8], F32))
            with nc.Block() as blk:
                @blk.tensor
                def _(t):
                    for c in range(8):
                        t.matmul(ps_rt[:, c, :], rw_sb[0:8, c * 128:(c + 1) * 128],
                                 ident[0:8, 0:8], start=True, stop=True)
            with nc.Block() as blk2:
                @blk2.scalar
                def _(s):
                    s.activation(rwT[:, :, :], ps_rt[:, :, :], AF.Copy)

        for isl in range(T // NSL):
            tsl = slice(isl * NSL, (isl + 1) * NSL)
            with ExitStack() as st_sl:
                xn = st_sl.enter_context(nc.sbuf_tensor(kb.name("xn"), [128, 8, NSL], BF16))
                x_sl = st_sl.enter_context(nc.sbuf_tensor(kb.name("x_sl"), [128, 8, NSL], F32))
                gT = st_sl.enter_context(nc.sbuf_tensor(kb.name("gT"), [1, 8, NSL], F32))
                with ExitStack() as st_ln:
                    scr = st_ln.enter_context(nc.sbuf_tensor(kb.name("scr"), [128, 8, NSL], BF16))
                    emit_ln_slab(kb, st_ln, x_sl, xn, scr, NSL, xsrc=xin_v[:, :, tsl])
                if parts == "noln":
                    continue
                skip_router = parts in ("norouter",)
                with ExitStack() as st_rt:
                    probs = st_rt.enter_context(nc.sbuf_tensor(kb.name("probs"), [128, 4, 8], F32))
                    ps_lg = st_rt.enter_context(nc.psum_tensor(kb.name("ps_lg"), [128, 4, 8], F32))
                    with nc.Block() as blk:
                        @blk.tensor
                        def _(t):
                            for tg in range(4):
                                for c in range(8):
                                    t.matmul(ps_lg[:, tg, :],
                                             xn[:, c, tg * 128:(tg + 1) * 128],
                                             rwT[:, c, :], start=(c == 0), stop=(c == 7))
                    with nc.Block() as blk:
                        @blk.scalar
                        def _(s):
                            s.activation(probs[:, :, :], ps_lg[:, :, :], AF.Sigmoid)
                    g_tile = st_rt.enter_context(nc.sbuf_tensor(kb.name("gt2"), [128, 4, 8], F32))
                    cols = st_rt.enter_context(nc.sbuf_tensor(kb.name("tcol"), [128, 6], F32))
                    m1, m2, den, rcp, w1c, w2c = [cols[:, i:i + 1] for i in range(6)]
                    eq1 = st_rt.enter_context(nc.sbuf_tensor(kb.name("eq1"), [128, 8], F32))
                    eq2 = st_rt.enter_context(nc.sbuf_tensor(kb.name("eq2"), [128, 8], F32))
                    p2 = st_rt.enter_context(nc.sbuf_tensor(kb.name("p2t"), [128, 8], F32))
                    with nc.Block() as blk:
                        @blk.vector
                        def _(v):
                            for tg in range(4):
                                pr = probs[:, tg, :]
                                kb.step("v", v, v.reduce_max(m1, pr, axis=AX.X))
                                kb.step("v", v, v.tensor_scalar(eq1[:, :], pr, m1, None, ALU.is_equal))
                                kb.step("v", v, v.scalar_tensor_tensor(p2[:, :], eq1[:, :], kb.fc(-1e30), pr,
                                                       ALU.mult, ALU.add))
                                kb.step("v", v, v.reduce_max(m2, p2[:, :], axis=AX.X))
                                kb.step("v", v, v.tensor_scalar(eq2[:, :], p2[:, :], m2, None, ALU.is_equal))
                                kb.step("v", v, v.tensor_add(den, m1, m2))
                                kb.step("v", v, v.tensor_scalar(den, den, kb.fc(1e-20), 2.0, ALU.add, ALU.mult))
                                kb.step("v", v, v.reciprocal(rcp, den))
                                kb.step("v", v, v.tensor_mul(w1c, m1, rcp))
                                kb.step("v", v, v.tensor_mul(w2c, m2, rcp))
                                kb.step("v", v, v.tensor_scalar(g_tile[:, tg, :], eq2[:, :], w2c, None, ALU.mult))
                                kb.step("v", v, v.scalar_tensor_tensor(g_tile[:, tg, :], eq1[:, :], w1c,
                                                       g_tile[:, tg, :], ALU.mult, ALU.add))
                    gts = st_rt.enter_context(nc.sbuf_tensor(kb.name("gts"), [8, 4, 128], F32))
                    with ExitStack() as st_pt:
                        ps_gt = st_pt.enter_context(nc.psum_tensor(kb.name("ps_gt"), [8, 4, 128], F32))
                        with nc.Block() as blk:
                            @blk.tensor
                            def _(t):
                                for tg in range(4):
                                    t.transpose(ps_gt[0:8, tg, :], g_tile[:, tg, 0:8], ident[:, :])
                        with nc.Block() as blk:
                            @blk.vector
                            def _(v):
                                v.tensor_copy(gts[:, :, :], ps_gt[0:8, :, :])
                    with nc.Block() as blk:
                        @blk.sync
                        def _(sync):
                            sync.dma_start(g_scr, gts[:, :, :].rearrange("e tg t -> e (tg t)")
                                           ).then_inc(S["lg"], 16)
                            C["lg"] += 16
                            sync.wait_ge(S["lg"], C["lg"])
                            sync.dma_start(gT[0:1, :, :],
                                           g_scr.rearrange("e t -> (e t)").rearrange("(o f) -> o f", o=1).rearrange("o (e t) -> o e t", e=8)
                                           ).then_inc(S["lg"], 16)
                            C["lg"] += 16
                            sync.wait_ge(S["lg"], C["lg"])
                    if dbg is not None and isl == 0:
                        with nc.Block() as blk:
                            @blk.sync
                            def _(sync):
                                kb.store(sync, dbg["probs"], probs[:, :, :].rearrange("p tg e -> p (tg e)"))
                                kb.store(sync, dbg["gT"], gT[0:1, :, :].rearrange("o e t -> o (e t)"))
                if skip_router:
                    pass
                if parts == "now1":
                    continue
                a_all = st_sl.enter_context(nc.sbuf_tensor(kb.name("a_all"), [128, 64, NSL], BF16))
                w1b = [st_sl.enter_context(nc.sbuf_tensor(kb.name("w1b"), [128, 8, 1024], BF16, side="right"))
                       for _ in range(2)]
                w2b = [st_sl.enter_context(nc.sbuf_tensor(kb.name("w2b"), [128, 8, 1024], BF16, side="right"))
                       for _ in range(2)]
                rbuf = st_sl.enter_context(nc.sbuf_tensor(kb.name("rbuf"), [128, NSL], F32))
                sqb = [st_sl.enter_context(nc.sbuf_tensor(kb.name("sqb"), [128, NSL], F32))
                       for _ in range(2)]
                with ExitStack() as st_ps, nc.Block() as blk:
                    hps = [st_ps.enter_context(nc.psum_tensor(kb.name("hps"), [128, NSL], F32))
                           for _ in range(2)]
                    gbps = [st_ps.enter_context(nc.psum_tensor(kb.name("gbps"), [128, NSL], F32))
                            for _ in range(2)]
                    b_ldw1 = [C["ldw1"], C["ldw1b"]]
                    b_mm1, b_rel, b_va, b_gb = (C["mm1"], C["rel"],
                                                C["va"], C["gb"])
                    w1sem = [S["ldw1"], S["ldw1b"]]

                    @blk.sync
                    def _(sync):
                        for e in range(8):
                            if e >= 2:
                                sync.wait_ge(S["mm1"], b_mm1 + 8 * (e - 1))
                            sync.dma_start(w1b[e % 2][:, :, :],
                                           w1bv[:, :, e * 1024:(e + 1) * 1024]
                                           ).then_inc(w1sem[e % 2], 16)

                    @blk.tensor
                    def _(t):
                        for e in range(8):
                            t.wait_ge(w1sem[e % 2], b_ldw1[e % 2] + 16 * (e // 2 + 1))
                            if e >= 2:
                                t.wait_ge(S["va"], b_va + 8 * (e - 1))
                            t.matmul(gbps[e % 2][:, :], onesrow[:, :], gT[0:1, e, :],
                                     start=True, stop=True).then_inc(S["gb"], 1)
                            for w in range(8):
                                g = e * 8 + w
                                if g >= 2:
                                    t.wait_ge(S["rel"], b_rel + g - 1)
                                for k in range(8):
                                    ins = t.matmul(hps[g % 2][:, :],
                                                   w1b[e % 2][:, k, w * 128:(w + 1) * 128],
                                                   xn[:, k, :],
                                                   start=(k == 0), stop=(k == 7))
                                ins.then_inc(S["mm1"], 1)

                    @blk.scalar
                    def _(s):
                        for e in range(8):
                            for w in range(8):
                                g = e * 8 + w
                                s.wait_ge(S["mm1"], b_mm1 + g + 1)
                                if g >= 2:
                                    s.wait_ge(S["va"], b_va + g - 1)
                                kb.step("s", s, s.activation(rbuf[:, :], hps[g % 2][:, :], AF.Relu))
                                s.activation(sqb[g % 2][:, :], rbuf[:, :],
                                             AF.Square).then_inc(S["rel"], 1)

                    @blk.vector
                    def _(v):
                        for e in range(8):
                            v.wait_ge(S["gb"], b_gb + e + 1)
                            for w in range(8):
                                g = e * 8 + w
                                v.wait_ge(S["rel"], b_rel + g + 1)
                                v.tensor_mul(a_all[:, g, :], sqb[g % 2][:, :],
                                             gbps[e % 2][:, :]).then_inc(S["va"], 1)
                    C["ldw1"] += 16 * 4
                    C["ldw1b"] += 16 * 4
                    C["gb"] += 8
                    C["mm1"] += 64
                    C["rel"] += 64
                    C["va"] += 64
                if dbg is not None and isl == 0:
                    with nc.Block() as blk:
                        @blk.sync
                        def _(sync):
                            kb.store(sync, dbg["xn"], xn[:, :, :].rearrange("p c t -> p (c t)"))
                            kb.store(sync, dbg["a0"], a_all[:, 0:8, :].rearrange("p c t -> p (c t)"))
                if parts == "now2":
                    continue
                with ExitStack() as st_ps, nc.Block() as blk:
                    ops = st_ps.enter_context(nc.psum_tensor(kb.name("ops"), [128, 8, NSL], F32))
                    b_ldw2 = [C["ldw2"], C["ldw2b"]]
                    b_mm2, b_ev2 = C["mm2"], C["ev2"]
                    w2sem = [S["ldw2"], S["ldw2b"]]
                    do_add = parts in ("all", "w2nostore")
                    do_store = parts == "all"

                    @blk.sync
                    def _(sync):
                        for e in range(8):
                            if e >= 2:
                                sync.wait_ge(S["mm2"], b_mm2 + 8 * (e - 1))
                            sync.dma_start(w2b[e % 2][:, :, :],
                                           w2bv[:, e * 8:(e + 1) * 8, :]
                                           ).then_inc(w2sem[e % 2], 16)

                    @blk.tensor
                    def _(t):
                        for e in range(8):
                            t.wait_ge(w2sem[e % 2], b_ldw2[e % 2] + 16 * (e // 2 + 1))
                            for dout in range(8):
                                for k in range(8):
                                    ins = t.matmul(ops[:, dout, :],
                                                   w2b[e % 2][:, k, dout * 128:(dout + 1) * 128],
                                                   a_all[:, e * 8 + k, :],
                                                   start=(e == 0 and k == 0),
                                                   stop=(e == 7 and k == 7))
                                if dout == 7:
                                    ins.then_inc(S["mm2"], 8)

                    @blk.vector
                    def _(v):
                        if do_add:
                            v.wait_ge(S["mm2"], b_mm2 + 64)
                            for dout in range(8):
                                v.tensor_add(x_sl[:, dout, :], x_sl[:, dout, :],
                                             ops[:, dout, :]).then_inc(S["ev2"], 1)
                    C["ldw2"] += 16 * 4
                    C["ldw2b"] += 16 * 4
                    C["mm2"] += 64
                    C["ev2"] += 8 if do_add else 0
                if do_store:
                    with nc.Block() as blk:
                        @blk.sync
                        def _(sync):
                            kb.store(sync, xout_v[:, :, tsl], x_sl[:, :, :])


def build_moe_nc(debug=False, parts="all"):
    kb = KB()
    x_in = kb.din("x", [D, T], F32)
    w1 = kb.din("w1", [1024, 8192], F32)
    w2 = kb.din("w2", [8192, 1024], F32)
    rw = kb.din("rw", [8, 1024], F32)
    y = kb.dout("y", [D, T], F32)
    w1bf = kb.dscr("w1bf", [1024, 8192], BF16)
    w2bf = kb.dscr("w2bf", [8192, 1024], BF16)
    xres = kb.dscr("xres", [D, T], F32)
    dbg = None
    if debug:
        dbg = {"probs": kb.dout("d_probs", [128, 32], F32),
               "gT": kb.dout("d_gT", [1, 8 * 512], F32),
               "xn": kb.dout("d_xn", [128, 8 * 512], BF16),
               "a0": kb.dout("d_a0", [128, 8 * 512], BF16)}
    kb.init()
    emit_xin_copy(kb, x_in, xres)
    if parts in ("all", "pc"):
        emit_moe_preconvert(kb, w1, w2, w1bf, w2bf)
    if parts in ("all", "moe", "norouter", "now2", "now1"):
        ms = MoESems(kb)
        emit_moe(kb, ms, xres, y, w1bf, w2bf, rw, dbg=dbg, parts=parts)
    nc = kb.finalize()
    return nc, {"c_fvals": kb.fvals_array()}


def emit_attn_preconvert(kb, wqkv_hbm, wout_hbm, rope_hbm, wqkv_bf, wout_bf, rope_scr):
    nc = kb.nc
    qv = wqkv_hbm.rearrange("(c p) m -> p c m", p=128)
    qbv = wqkv_bf.rearrange("(c p) m -> p c m", p=128)
    ov = wout_hbm.rearrange("(c p) m -> p c m", p=128)
    obv = wout_bf.rearrange("(c p) m -> p c m", p=128)
    with ExitStack() as st:
        buf = st.enter_context(nc.sbuf_tensor(kb.name("pca"), [128, 8, 3072], BF16))
        fbuf = st.enter_context(nc.sbuf_tensor(kb.name("pcf"), [128, 4096], F32))
        ld = kb.sem("pca_ld")
        stc = kb.sem("pca_st")
        nld = nst = 0
        with nc.Block() as blk:
            @blk.gpsimd
            def _(g):
                nonlocal nld
                g.dma_start(buf[:, :, :], qv).then_inc(ld, 16)
                nld += 16
                g.wait_ge(ld, nld)
        with nc.Block() as blk:
            @blk.sync
            def _(sync):
                nonlocal nst
                sync.dma_start(qbv, buf[:, :, :]).then_inc(stc, 16)
                nst += 16
                sync.wait_ge(stc, nst)
        with nc.Block() as blk:
            @blk.gpsimd
            def _(g):
                nonlocal nld
                g.dma_start(buf[:, 0:8, 0:1024], ov).then_inc(ld, 16)
                nld += 16
                g.wait_ge(ld, nld)
        with nc.Block() as blk:
            @blk.sync
            def _(sync):
                nonlocal nst
                sync.dma_start(obv, buf[:, 0:8, 0:1024]).then_inc(stc, 16)
                nst += 16
                sync.wait_ge(stc, nst)
        for tb in range(4):
            with nc.Block() as blk:
                @blk.gpsimd
                def _(g):
                    nonlocal nld
                    g.dma_start(fbuf[:, :], rope_hbm[tb, :, :]).then_inc(ld, 16)
                    nld += 16
                    g.wait_ge(ld, nld)
            with nc.Block() as blk:
                @blk.sync
                def _(sync):
                    nonlocal nst
                    sync.dma_start(rope_scr[tb, :, :], fbuf[:, :]).then_inc(stc, 16)
                    nst += 16
                    sync.wait_ge(stc, nst)


class AttnSems:
    def __init__(self, kb):
        names = ["ldx", "ldxn", "ldxnb", "ldq", "ldqb", "ldw", "ldtab", "ldtabb",
                 "mm", "sev", "tswp",
                 "vrope", "ttr", "vcp", "smm", "sexp", "vmask", "tpv", "vrcp",
                 "trb", "vfin", "mmo", "evo"]
        self.s = {n: kb.sem("at_" + n) for n in names}
        self.c = {n: 0 for n in names}


def emit_attn(kb, asx, x_in_hbm, x_out_hbm, wqkv_bf, wout_bf, rope_scr,
              xn_scr, q_scr):
    nc = kb.nc
    S, C = asx.s, asx.c
    NSL = 512
    nslabs = T // NSL
    ident_bf = kb.const_sb["c_ident_bf"]
    onesrow = kb.const_sb["c_ones_row"]
    swpq = kb.const_sb["c_swpq"]
    dmask = kb.const_sb["c_diagmask"]
    xin_v = xview(x_in_hbm)
    xout_v = xview(x_out_hbm)
    xn_v = xn_scr.rearrange("(c p) t -> p c t", p=128)
    qbv = wqkv_bf.rearrange("(c p) m -> p c m", p=128)
    obv = wout_bf.rearrange("(c p) m -> p c m", p=128)
    q_v = q_scr.rearrange("pr p t -> p pr t")

    with ExitStack() as st_x2:
        x_sl2 = [st_x2.enter_context(nc.sbuf_tensor(kb.name("x_sl"), [128, 8, NSL], F32))
                 for _ in range(2)]
        base_ld = {"ld": kb.lncnt["ld"], "ld2": kb.lncnt["ld2"]}
        lpn = lambda s: "ld" if s % 2 == 0 else "ld2"
        for isl in range(nslabs):
            tsl = slice(isl * NSL, (isl + 1) * NSL)
            with ExitStack() as st:
                xn = st.enter_context(nc.sbuf_tensor(kb.name("xn"), [128, 8, NSL], BF16))
                scr = st.enter_context(nc.sbuf_tensor(kb.name("scr"), [128, 8, NSL], BF16))
                pf = []
                if isl + 1 < nslabs:
                    nts = slice((isl + 1) * NSL, (isl + 2) * NSL)
                    pf.append((x_sl2[(isl + 1) % 2][:, :, :], xin_v[:, :, nts],
                               lpn(isl + 1)))
                if isl == 0:
                    emit_ln_slab(kb, st, x_sl2[0], xn, scr, NSL,
                                 xsrc=xin_v[:, :, tsl], prefetch=pf)
                else:
                    emit_ln_slab(kb, st, x_sl2[isl % 2], xn, scr, NSL, prefetch=pf,
                                 ld_wait=(lpn(isl),
                                          base_ld[lpn(isl)] + 16 * (isl // 2 + 1)))
                with nc.Block() as blk:
                    @blk.sync
                    def _(sync):
                        kb.dma_ff(sync, xn_v[:, :, tsl], xn[:, :, :])

    with ExitStack() as st_kv:
        k_sb = st_kv.enter_context(nc.sbuf_tensor(kb.name("k_sb"), [128, 8, T], BF16, side="right"))
        vp_sb = st_kv.enter_context(nc.sbuf_tensor(kb.name("vp_sb"), [128, 32, 1040], BF16, side="right"))
        with nc.Block() as blk:
            @blk.vector
            def _(v):
                ones_view = vp_sb[:, :, :].rearrange("p st (h c) -> p st h c", c=65)[:, :, :, 64:65]
                v.memset(ones_view, 1.0)

        for phase in ("q", "k", "v"):
            with ExitStack() as st_ph:
                wpart = st_ph.enter_context(nc.sbuf_tensor(kb.name("wp"), [128, 8, 1024], BF16))
                xn2 = [st_ph.enter_context(nc.sbuf_tensor(kb.name("xns"), [128, 8, NSL], BF16))
                       for _ in range(2)]
                if phase in ("q", "k"):
                    tabs2 = [st_ph.enter_context(nc.sbuf_tensor(kb.name("tabs"), [128, 2, NSL], F32))
                             for _ in range(2)]
                part0 = {"q": 0, "k": 1, "v": 2}[phase]
                ti = {"q": 0, "k": 2, "v": 0}[phase]
                b_xn0 = {"ldxn": C["ldxn"], "ldxnb": C["ldxnb"]}
                b_tab0 = {"ldtab": C["ldtab"], "ldtabb": C["ldtabb"]}
                with nc.Block() as blk:
                    @blk.sync
                    def _(sync):
                        kb.ffw(sync)
                        sync.dma_start(wpart[:, :, :],
                                       qbv[:, :, part0 * 1024:(part0 + 1) * 1024]
                                       ).then_inc(S["ldw"], 16)
                        C["ldw"] += 16
                for isl in range(nslabs):
                    tsl = slice(isl * NSL, (isl + 1) * NSL)
                    xn = xn2[isl % 2]
                    if phase in ("q", "k"):
                        tabs = tabs2[isl % 2]
                    with ExitStack() as st_sl:
                        raw = [st_sl.enter_context(nc.sbuf_tensor(kb.name("raw"), [128, NSL], F32))
                               for _ in range(2)]
                        if phase in ("q", "k"):
                            t1b = st_sl.enter_context(nc.sbuf_tensor(kb.name("t1b"), [128, NSL], F32))
                            t2b = st_sl.enter_context(nc.sbuf_tensor(kb.name("t2b"), [128, NSL], F32))
                        if phase == "q":
                            qtmp = st_sl.enter_context(nc.sbuf_tensor(kb.name("qtmp"), [128, 8, NSL], BF16))
                        if phase == "v":
                            vtmp = [st_sl.enter_context(nc.sbuf_tensor(kb.name("vtmp"), [128, NSL], BF16))
                                    for _ in range(2)]
                        with ExitStack() as st_ps, nc.Block() as blk:
                            mps = [st_ps.enter_context(nc.psum_tensor(kb.name("mps"), [128, NSL], F32))
                                   for _ in range(2)]
                            if phase in ("q", "k"):
                                wps = [st_ps.enter_context(nc.psum_tensor(kb.name("wps"), [128, NSL], F32))
                                       for _ in range(2)]
                            else:
                                tps = [st_ps.enter_context(nc.psum_tensor(kb.name("tps"), [128, 128], BF16))
                                       for _ in range(2)]
                            b = {n: C[n] for n in ("ldxn", "ldtab", "mm", "sev",
                                                   "tswp", "vrope", "ttr", "vcp")}
                            ldw_now = C["ldw"]

                            @blk.sync
                            def _(sync):
                                kb.ffw(sync)
                                for lsl in ([isl, isl + 1] if isl == 0 else [isl + 1]):
                                    if lsl >= nslabs:
                                        continue
                                    lts = slice(lsl * NSL, (lsl + 1) * NSL)
                                    xs = "ldxn" if lsl % 2 == 0 else "ldxnb"
                                    sync.dma_start(xn2[lsl % 2][:, :, :],
                                                   xn_v[:, :, lts]).then_inc(S[xs], 16)
                                    C[xs] += 16
                                    if phase in ("q", "k"):
                                        ts_ = "ldtab" if lsl % 2 == 0 else "ldtabb"
                                        sync.dma_start(tabs2[lsl % 2][:, :, :],
                                                       rope_scr[ti:ti + 2, :, lts].rearrange("tb p t -> p tb t")
                                                       ).then_inc(S[ts_], 16)
                                        C[ts_] += 16

                            @blk.tensor
                            def _(t):
                                t.wait_ge(S["ldw"], ldw_now)
                                xs = "ldxn" if isl % 2 == 0 else "ldxnb"
                                t.wait_ge(S[xs], b_xn0[xs] + 16 * (isl // 2 + 1))
                                for pr in range(8):
                                    if pr >= 2:
                                        t.wait_ge(S["sev"], b["sev"] + pr - 1)
                                    for k in range(8):
                                        ins = t.matmul(mps[pr % 2][:, :],
                                                       wpart[:, k, pr * 128:(pr + 1) * 128],
                                                       xn[:, k, :],
                                                       start=(k == 0), stop=(k == 7))
                                    ins.then_inc(S["mm"], 1)
                                    if phase in ("q", "k"):
                                        t.wait_ge(S["sev"], b["sev"] + pr + 1)
                                        if pr >= 2:
                                            t.wait_ge(S["vrope"], b["vrope"] + pr - 1)
                                        t.matmul(wps[pr % 2][:, :], swpq[:, :], raw[pr % 2][:, :],
                                                 start=True, stop=True).then_inc(S["tswp"], 1)
                                    else:
                                        t.wait_ge(S["sev"], b["sev"] + pr + 1)
                                        for c4 in range(4):
                                            g = pr * 4 + c4
                                            if g >= 2:
                                                t.wait_ge(S["vcp"], b["vcp"] + 2 * (g - 1))
                                            t.transpose(tps[c4 % 2][:, :],
                                                        vtmp[pr % 2][:, c4 * 128:(c4 + 1) * 128],
                                                        ident_bf[:, :]).then_inc(S["ttr"], 1)

                            @blk.scalar
                            def _(s):
                                for pr in range(8):
                                    s.wait_ge(S["mm"], b["mm"] + pr + 1)
                                    if phase in ("q", "k"):
                                        if pr >= 2:
                                            s.wait_ge(S["tswp"], b["tswp"] + pr - 1)
                                            s.wait_ge(S["vrope"], b["vrope"] + pr - 1)
                                        s.activation(raw[pr % 2][:, :], mps[pr % 2][:, :],
                                                     AF.Copy).then_inc(S["sev"], 1)
                                    else:
                                        if pr >= 2:
                                            s.wait_ge(S["ttr"], b["ttr"] + 4 * (pr - 1))
                                        s.activation(vtmp[pr % 2][:, :], mps[pr % 2][:, :],
                                                     AF.Copy).then_inc(S["sev"], 1)

                            @blk.vector
                            def _(v):
                                kb.ffw(v)
                                if phase in ("q", "k"):
                                    ts_ = "ldtab" if isl % 2 == 0 else "ldtabb"
                                    v.wait_ge(S[ts_], b_tab0[ts_] + 16 * (isl // 2 + 1))
                                    for pr in range(8):
                                        v.wait_ge(S["tswp"], b["tswp"] + pr + 1)
                                        if pr >= 1:
                                            v.wait_ge(S["vrope"], b["vrope"] + pr)
                                        kb.step("v", v, v.tensor_mul(t1b[:, :], raw[pr % 2][:, :],
                                                                     tabs[:, 0, :]))
                                        kb.step("v", v, v.tensor_mul(t2b[:, :], wps[pr % 2][:, :],
                                                                     tabs[:, 1, :]))
                                        dst = qtmp[:, pr, :] if phase == "q" else k_sb[:, pr, tsl]
                                        v.tensor_add(dst, t1b[:, :], t2b[:, :]).then_inc(S["vrope"], 1)
                                else:
                                    for pr in range(8):
                                        for c4 in range(4):
                                            v.wait_ge(S["ttr"], b["ttr"] + pr * 4 + c4 + 1)
                                            stile = isl * 4 + c4
                                            hA, hB = 2 * pr, 2 * pr + 1
                                            v.tensor_copy(vp_sb[:, stile, hA * 65:hA * 65 + 64],
                                                          tps[c4 % 2][:, 0:64]).then_inc(S["vcp"], 1)
                                            v.tensor_copy(vp_sb[:, stile, hB * 65:hB * 65 + 64],
                                                          tps[c4 % 2][:, 64:128]).then_inc(S["vcp"], 1)

                            C["mm"] += 8
                            C["sev"] += 8
                            if phase in ("q", "k"):
                                C["tswp"] += 8
                                C["vrope"] += 8
                            else:
                                C["ttr"] += 32
                                C["vcp"] += 64
                        if phase == "q":
                            with nc.Block() as blk:
                                @blk.sync
                                def _(sync):
                                    kb.dma_ff(sync, q_v[:, :, tsl], qtmp[:, :, :])

        with ExitStack() as st_o:
            wout_sb = st_o.enter_context(nc.sbuf_tensor(kb.name("wout"), [128, 8, 1024], BF16))
            q_sl2 = [st_o.enter_context(nc.sbuf_tensor(kb.name("q_sl"), [128, 8, NSL], BF16))
                     for _ in range(2)]
            b_ldq0 = {"ldq": C["ldq"], "ldqb": C["ldqb"]}
            with nc.Block() as blk:
                @blk.sync
                def _(sync):
                    kb.ffw(sync)
                    sync.dma_start(wout_sb[:, :, :], obv).then_inc(S["ldw"], 16)
                    C["ldw"] += 16
            for isl in range(nslabs):
                tsl = slice(isl * NSL, (isl + 1) * NSL)
                jmax = 4 * (isl + 1)
                ndiag = min(4, jmax)
                jd0 = jmax - 4
                q_sl = q_sl2[isl % 2]
                with ExitStack() as st_sl:
                    x_sl = st_sl.enter_context(nc.sbuf_tensor(kb.name("x_slB"), [128, 8, NSL], F32))
                    out_sl = st_sl.enter_context(nc.sbuf_tensor(kb.name("out_sl"), [128, 8, NSL], BF16))
                    pt = st_sl.enter_context(nc.sbuf_tensor(kb.name("pt"), [128, 4, NSL], BF16))
                    rbsb = st_sl.enter_context(nc.sbuf_tensor(kb.name("rbsb"), [64, 2, NSL], F32))
                    rcps = st_sl.enter_context(nc.sbuf_tensor(kb.name("rcps"), [1, 2, NSL], F32))
                    with ExitStack() as st_ps, nc.Block() as blk:
                        spsb = [st_ps.enter_context(nc.psum_tensor(kb.name("sps"), [128, 2, NSL], F32))
                                for _ in range(2)]
                        oA = st_ps.enter_context(nc.psum_tensor(kb.name("oA"), [65, NSL], F32))
                        oB = st_ps.enter_context(nc.psum_tensor(kb.name("oB"), [65, NSL], F32))
                        rbA = st_ps.enter_context(nc.psum_tensor(kb.name("rbA"), [64, NSL], F32))
                        rbB = st_ps.enter_context(nc.psum_tensor(kb.name("rbB"), [64, NSL], F32))
                        b = {n: C[n] for n in ("smm", "sexp", "vmask", "tpv",
                                               "vrcp", "trb", "vfin", "ldq", "ldx")}

                        @blk.sync
                        def _(sync):
                            kb.ffw(sync)
                            for lsl in ([isl, isl + 1] if isl == 0 else [isl + 1]):
                                if lsl >= nslabs:
                                    continue
                                lts = slice(lsl * NSL, (lsl + 1) * NSL)
                                qs = "ldq" if lsl % 2 == 0 else "ldqb"
                                sync.dma_start(q_sl2[lsl % 2][:, :, :],
                                               q_v[:, :, lts]).then_inc(S[qs], 16)
                                C[qs] += 16
                            sync.wait_ge(kb.store_sem, kb.store_cnt)
                            sync.dma_start(x_sl[:, :, :], xin_v[:, :, tsl]).then_inc(S["ldx"], 16)
                        C["ldx"] += 16

                        def pt_ap(j, half):
                            return pt[:, (j % 2) * 2 + half, :]

                        def emit_pv(t, pr, j):
                            hA, hB = 2 * pr, 2 * pr + 1
                            jj = pr * jmax + j
                            if j == 0 and pr >= 1:
                                t.wait_ge(S["vfin"], b["vfin"] + pr)
                            if j >= jd0:
                                t.wait_ge(S["vmask"], b["vmask"] + pr * ndiag + (j - jd0) + 1)
                            else:
                                t.wait_ge(S["sexp"], b["sexp"] + jj + 1)
                            t.matmul(oA[:, :], vp_sb[:, j, hA * 65:(hA + 1) * 65],
                                     pt_ap(j, 0), start=(j == 0), stop=(j == jmax - 1))
                            t.matmul(oB[:, :], vp_sb[:, j, hB * 65:(hB + 1) * 65],
                                     pt_ap(j, 1), start=(j == 0),
                                     stop=(j == jmax - 1)).then_inc(S["tpv"], 1)

                        def emit_rb(t, p):
                            t.wait_ge(S["vrcp"], b["vrcp"] + p + 1)
                            t.matmul(rbA[:, :], onesrow[0:1, 0:64], rcps[0:1, 0, :],
                                     start=True, stop=True)
                            t.matmul(rbB[:, :], onesrow[0:1, 0:64], rcps[0:1, 1, :],
                                     start=True, stop=True).then_inc(S["trb"], 1)

                        @blk.tensor
                        def _(t):
                            qs = "ldq" if isl % 2 == 0 else "ldqb"
                            t.wait_ge(S[qs], b_ldq0[qs] + 16 * (isl // 2 + 1))
                            for pr in range(8):
                                for j in range(jmax):
                                    jj = pr * jmax + j
                                    if jj >= 2:
                                        t.wait_ge(S["sexp"], b["sexp"] + jj - 1)
                                    t.matmul(spsb[j % 2][:, 0, :],
                                             k_sb[0:64, pr, j * 128:(j + 1) * 128],
                                             q_sl[0:64, pr, :],
                                             start=True, stop=True,
                                             tile_position=(0, 0))
                                    t.matmul(spsb[j % 2][:, 1, :],
                                             k_sb[64:128, pr, j * 128:(j + 1) * 128],
                                             q_sl[64:128, pr, :],
                                             start=True, stop=True,
                                             tile_position=(64, 0)).then_inc(S["smm"], 1)
                                    if j == 1 and pr >= 1:
                                        emit_rb(t, pr - 1)
                                    if j >= 1:
                                        emit_pv(t, pr, j - 1)
                                emit_pv(t, pr, jmax - 1)
                            emit_rb(t, 7)

                        @blk.scalar
                        def _(s):
                            for pr in range(8):
                                for j in range(jmax):
                                    jj = pr * jmax + j
                                    s.wait_ge(S["smm"], b["smm"] + jj + 1)
                                    if jj >= 2:
                                        s.wait_ge(S["tpv"], b["tpv"] + jj - 1)
                                    s.activation(pt[:, (j % 2) * 2:(j % 2) * 2 + 2, :],
                                                 spsb[j % 2][:, :, :],
                                                 AF.Exp, bias=kb.fc(-24.0)).then_inc(S["sexp"], 1)

                        @blk.vector
                        def _(v):
                            for pr in range(8):
                                for jd in range(ndiag):
                                    j = jd0 + jd
                                    jj = pr * jmax + j
                                    v.wait_ge(S["sexp"], b["sexp"] + jj + 1)
                                    off = 384 - 128 * jd
                                    v.tensor_mul(pt_ap(j, 0), pt_ap(j, 0),
                                                 dmask[:, off:off + NSL])
                                    v.tensor_mul(pt_ap(j, 1), pt_ap(j, 1),
                                                 dmask[:, off:off + NSL]).then_inc(S["vmask"], 1)
                                v.wait_ge(S["tpv"], b["tpv"] + (pr + 1) * jmax)
                                kb.step("v", v, v.reciprocal(rcps[0:1, 0, :], oA[64:65, :]))
                                kb.step("v", v,
                                        v.reciprocal(rcps[0:1, 1, :], oB[64:65, :]))
                                v.engine_nop().then_inc(S["vrcp"], 1)
                                v.wait_ge(S["trb"], b["trb"] + pr + 1)
                                kb.step("v", v, v.tensor_copy(rbsb[:, 0, :], rbA[:, :]))
                                kb.step("v", v, v.tensor_copy(rbsb[:, 1, :], rbB[:, :]))
                                v.tensor_mul(out_sl[0:64, pr, :], oA[0:64, :], rbsb[:, 0, :])
                                v.tensor_mul(out_sl[64:128, pr, :], oB[0:64, :],
                                             rbsb[:, 1, :]).then_inc(S["vfin"], 1)

                        C["smm"] += 8 * jmax
                        C["sexp"] += 8 * jmax
                        C["vmask"] += 8 * ndiag
                        C["tpv"] += 8 * jmax
                        C["vrcp"] += 8
                        C["trb"] += 8
                        C["vfin"] += 8
                    with ExitStack() as st_ps, nc.Block() as blk:
                        ops = [st_ps.enter_context(nc.psum_tensor(kb.name("opj"), [128, NSL], F32))
                               for _ in range(2)]
                        b_mmo, b_evo = C["mmo"], C["evo"]
                        ldw_o = C["ldw"]

                        @blk.tensor
                        def _(t):
                            t.wait_ge(S["ldw"], ldw_o)
                            for dout in range(8):
                                if dout >= 2:
                                    t.wait_ge(S["evo"], b_evo + dout - 1)
                                for k in range(8):
                                    ins = t.matmul(ops[dout % 2][:, :],
                                                   wout_sb[:, k, dout * 128:(dout + 1) * 128],
                                                   out_sl[:, k, :],
                                                   start=(k == 0), stop=(k == 7))
                                ins.then_inc(S["mmo"], 1)

                        @blk.vector
                        def _(v):
                            v.wait_ge(S["ldx"], b["ldx"] + 16)
                            for dout in range(8):
                                v.wait_ge(S["mmo"], b_mmo + dout + 1)
                                v.tensor_add(x_sl[:, dout, :], x_sl[:, dout, :],
                                             ops[dout % 2][:, :]).then_inc(S["evo"], 1)
                        C["mmo"] += 8
                        C["evo"] += 8
                    with nc.Block() as blk:
                        @blk.sync
                        def _(sync):
                            kb.store(sync, xout_v[:, :, tsl], x_sl[:, :, :])


def build_attn_nc():
    kb = KB()
    x_in = kb.din("x", [D, T], F32)
    wqkv = kb.din("wqkv", [1024, 3072], F32)
    wout = kb.din("wout", [1024, 1024], F32)
    rope = kb.din("c_rope", [4, 128, T], F32)
    y = kb.dout("y", [D, T], F32)
    xres = kb.dscr("xres", [D, T], F32)
    wqkv_bf = kb.dscr("wqkvbf", [1024, 3072], BF16)
    wout_bf = kb.dscr("woutbf", [1024, 1024], BF16)
    rope_scr = kb.dscr("ropescr", [4, 128, T], F32)
    xn_scr = kb.dscr("xnscr", [D, T], BF16)
    q_scr = kb.dscr("qscr", [8, 128, T], BF16)
    kb.init()
    emit_xin_copy(kb, x_in, xres)
    emit_attn_preconvert(kb, wqkv, wout, rope, wqkv_bf, wout_bf, rope_scr)
    asx = AttnSems(kb)
    emit_attn(kb, asx, xres, y, wqkv_bf, wout_bf, rope_scr, xn_scr, q_scr)
    nc = kb.finalize()
    return nc, {"c_fvals": kb.fvals_array(), "c_rope": rope_tables()}


def emit_ssm_preconvert(kb, win_hbm, wout_hbm, nw_hbm, win_bf, wout2_bf):
    nc = kb.nc
    wiv = win_hbm.rearrange("(c p) m -> p c m", p=128)
    wibv = win_bf.rearrange("(c p) m -> p c m", p=128)
    with ExitStack() as st:
        buf = st.enter_context(nc.sbuf_tensor(kb.name("pcs"), [128, 4256], BF16))
        fb = st.enter_context(nc.sbuf_tensor(kb.name("pcsf"), [128, 1024], F32))
        nwc = st.enter_context(nc.sbuf_tensor(kb.name("nwc"), [128, 1], F32))
        ob = st.enter_context(nc.sbuf_tensor(kb.name("pcso"), [128, 1024], BF16))
        ld = kb.sem("pcs_ld")
        stc = kb.sem("pcs_st")
        nld = nst = 0
        for c in range(8):
            with nc.Block() as blk:
                @blk.gpsimd
                def _(g):
                    nonlocal nld
                    g.dma_start(buf[:, :], wiv[:, c, :]).then_inc(ld, 16)
                    nld += 16
                    g.wait_ge(ld, nld)
            with nc.Block() as blk:
                @blk.sync
                def _(sync):
                    nonlocal nst
                    sync.dma_start(wibv[:, c, :], buf[:, :]).then_inc(stc, 16)
                    nst += 16
                    sync.wait_ge(stc, nst)
        for c in range(16):
            with nc.Block() as blk:
                @blk.gpsimd
                def _(g):
                    nonlocal nld
                    g.dma_start(fb[:, :], wout_hbm[c * 128:(c + 1) * 128, :]).then_inc(ld, 16)
                    g.dma_start(nwc[:, :], nw_hbm[c * 128:(c + 1) * 128, :]).then_inc(ld, 16)
                    nld += 32
                    g.wait_ge(ld, nld)
            with nc.Block() as blk:
                @blk.vector
                def _(v):
                    v.tensor_scalar(ob[:, :], fb[:, :], nwc[:, :], None, ALU.mult)
            with nc.Block() as blk:
                @blk.sync
                def _(sync):
                    nonlocal nst
                    sync.dma_start(wout2_bf[c * 128:(c + 1) * 128, :], ob[:, :]).then_inc(stc, 16)
                    nst += 16
                    sync.wait_ge(stc, nst)


class SSMSems:
    def __init__(self, kb):
        names = ["ldx", "ldxn", "ldw", "ldbc", "ldxs", "ldz", "ldf", "p1mm",
                 "p1ev", "tmsk", "tg", "sl", "slh", "vlu", "vma", "vsd", "typ", "vmg",
                 "tsw", "tsa", "vst", "ttr", "sfe", "mmo", "evo", "gat"]
        self.s = {n: kb.sem("ss_" + n) for n in names}
        self.c = {n: 0 for n in names}


def emit_ssm(kb, sx, x_in_hbm, x_out_hbm, win_bf, wout2_bf, dtb_hbm, alog_hbm,
             dsk_hbm, xn_scr, z_scr, xs_scr, bcd_scr, psf_scr, nch=32, php=9):
    nc = kb.nc
    S, C = sx.s, sx.c
    NSL = 512
    nslabs = T // NSL
    NCH = 32
    ident = kb.const_sb["c_ident_f32"]
    ident_bf = kb.const_sb["c_ident_bf"]
    onesrow = kb.const_sb["c_ones_row"]
    ut01 = kb.const_sb["c_ut01"]
    sl01 = kb.const_sb["c_sl01"]
    invfrep = kb.const_sb["c_invfrep"]
    slbig = kb.const_sb["c_slbig"]
    invf16 = kb.const_sb["c_invf16"]
    swpb = kb.const_sb["c_swpb"]
    xin_v = xview(x_in_hbm)
    xout_v = xview(x_out_hbm)
    xn_v = xn_scr.rearrange("(c p) t -> p c t", p=128)
    wibv = win_bf.rearrange("(c p) m -> p c m", p=128)
    wo2v = wout2_bf.rearrange("(c p) m -> p c m", p=128)
    z_v = z_scr.rearrange("(sl p) m -> p sl m", p=128)
    xs_v = xs_scr.rearrange("(sl p) m -> p sl m", p=128)
    bcd_v = bcd_scr.rearrange("(sl p) m -> p sl m", p=128)
    MAGIC = 12582912.0
    TWO_PI = float(2.0 * np.pi)
    HALF_PI = float(np.pi / 2.0)

    with ExitStack() as st_x2:
        x_sl2 = [st_x2.enter_context(nc.sbuf_tensor(kb.name("x_sl"), [128, 8, NSL], F32))
                 for _ in range(2)]
        base_ld = {"ld": kb.lncnt["ld"], "ld2": kb.lncnt["ld2"]}
        lpn = lambda s: "ld" if s % 2 == 0 else "ld2"
        for isl in range(nslabs):
            tsl = slice(isl * NSL, (isl + 1) * NSL)
            with ExitStack() as st:
                xn = st.enter_context(nc.sbuf_tensor(kb.name("xn"), [128, 8, NSL], BF16))
                scr = st.enter_context(nc.sbuf_tensor(kb.name("scr"), [128, 8, NSL], BF16))
                pf = []
                if isl + 1 < nslabs:
                    nts = slice((isl + 1) * NSL, (isl + 2) * NSL)
                    pf.append((x_sl2[(isl + 1) % 2][:, :, :], xin_v[:, :, nts],
                               lpn(isl + 1)))
                if isl == 0:
                    emit_ln_slab(kb, st, x_sl2[0], xn, scr, NSL,
                                 xsrc=xin_v[:, :, tsl], prefetch=pf)
                else:
                    emit_ln_slab(kb, st, x_sl2[isl % 2], xn, scr, NSL, prefetch=pf,
                                 ld_wait=(lpn(isl),
                                          base_ld[lpn(isl)] + 16 * (isl // 2 + 1)))
                with nc.Block() as blk:
                    @blk.sync
                    def _(sync):
                        kb.dma_ff(sync, xn_v[:, :, tsl], xn[:, :, :])

    with ExitStack() as st_w:
        wi = st_w.enter_context(nc.sbuf_tensor(kb.name("wi"), [128, 8, 4256], BF16))
        with nc.Block() as blk:
            @blk.sync
            def _(sync):
                kb.ffw(sync)
                sync.dma_start(wi[:, :, :], wibv).then_inc(S["ldw"], 16)
                C["ldw"] += 16
        for isl in range(nslabs):
            tsl = slice(isl * NSL, (isl + 1) * NSL)
            with ExitStack() as st_sl:
                xn = st_sl.enter_context(nc.sbuf_tensor(kb.name("xns"), [128, 8, NSL], BF16))
                z_sl = st_sl.enter_context(nc.sbuf_tensor(kb.name("z_sl"), [128, 4, 2048], BF16))
                xs_sl = st_sl.enter_context(nc.sbuf_tensor(kb.name("xs_sl"), [128, 4, 2048], BF16))
                bc_sl = st_sl.enter_context(nc.sbuf_tensor(kb.name("bc_sl"), [128, 4, 160], F32))
                with ExitStack() as st_ps, nc.Block() as blk:
                    pps = [st_ps.enter_context(nc.psum_tensor(kb.name("pps"), [128, 512], F32))
                           for _ in range(4)]
                    b_mm, b_ev, b_ldxn = C["p1mm"], C["p1ev"], C["ldxn"]
                    ldw_now = C["ldw"]

                    @blk.sync
                    def _(sync):
                        kb.ffw(sync)
                        sync.dma_start(xn[:, :, :], xn_v[:, :, tsl]).then_inc(S["ldxn"], 16)
                    C["ldxn"] += 16

                    @blk.tensor
                    def _(t):
                        t.wait_ge(S["ldw"], ldw_now)
                        t.wait_ge(S["ldxn"], b_ldxn + 16)
                        g = 0
                        for tg in range(4):
                            for cc in range(9):
                                ncols = 512 if cc < 8 else 160
                                if g >= 4:
                                    t.wait_ge(S["p1ev"], b_ev + g - 3)
                                for k in range(8):
                                    ins = t.matmul(
                                        pps[g % 4][:, 0:ncols],
                                        xn[:, k, tg * 128:(tg + 1) * 128],
                                        wi[:, k, cc * 512:cc * 512 + ncols],
                                        start=(k == 0), stop=(k == 7))
                                ins.then_inc(S["p1mm"], 1)
                                g += 1

                    @blk.scalar
                    def _(s):
                        kb.ffw(s)
                        g = 0
                        for tg in range(4):
                            for cc in range(9):
                                s.wait_ge(S["p1mm"], b_mm + g + 1)
                                p = pps[g % 4]
                                if cc < 4:
                                    s.activation(z_sl[:, tg, cc * 512:(cc + 1) * 512],
                                                 p[:, 0:512], AF.Copy).then_inc(S["p1ev"], 1)
                                elif cc < 8:
                                    s.activation(xs_sl[:, tg, (cc - 4) * 512:(cc - 3) * 512],
                                                 p[:, 0:512], AF.Copy).then_inc(S["p1ev"], 1)
                                else:
                                    for base in (0, 64):
                                        inr = p[:, base:base + 32].rearrange(
                                            "p (i two) -> p two i", two=2)
                                        outr = bc_sl[:, tg, base:base + 32].rearrange(
                                            "p (two i) -> p two i", two=2)
                                        s.activation(outr, inr, AF.Copy)
                                        s.activation(bc_sl[:, tg, base + 32:base + 64],
                                                     p[:, base + 32:base + 64], AF.Copy)
                                    s.activation(bc_sl[:, tg, 128:160], p[:, 128:160],
                                                 AF.Copy).then_inc(S["p1ev"], 1)
                                g += 1
                    C["p1mm"] += 36
                    C["p1ev"] += 36
                with nc.Block() as blk:
                    @blk.sync
                    def _(sync):
                        kb.dma_ff(sync, z_v[:, isl * 4:(isl + 1) * 4, :], z_sl[:, :, :])
                        kb.dma_ff(sync, xs_v[:, isl * 4:(isl + 1) * 4, :], xs_sl[:, :, :])
                        kb.dma_ff(sync, bcd_v[:, isl * 4:(isl + 1) * 4, :], bc_sl[:, :, :])

    with ExitStack() as st_p2:
        sb = lambda nm, shp, dt, side="left": st_p2.enter_context(
            nc.sbuf_tensor(kb.name(nm), shp, dt, side=side))
        state = sb("state", [64, 2048], F32)
        state_bf = sb("statebf", [128, 2048], BF16)
        wo2 = sb("wo2", [128, 16, 1024], BF16, side="right")
        dtb_b = sb("dtb_b", [128, 32], F32)
        a_b = sb("a_b", [128, 32], F32)
        DI = sb("DI", [128, 32, 128], BF16, side="right")
        rows = sb("rows", [1, 128], F32)
        y_stage = sb("ystage", [128, 4, 2048], F32, side="right")
        bcd = sb("bcd", [128, 160], F32)
        xs_c = sb("xs_c", [128, 2048], BF16)
        dt_t = sb("dt_t", [128, 32], F32)
        tA = sb("tA", [128, 32], F32)
        tB = sb("tB", [128, 32], F32)
        psi = sb("psi", [128, 32], F32)
        psihat = sb("psihat", [128, 32], F32)
        lam = sb("lam", [128, 32], F32)
        bdec = sb("bdec", [128, 32], F32)
        phi = sb("phi", [128, 32], F32)
        psf = sb("psf", [32, 128], F32)
        psrow = sb("psrow", [1, 2048], F32)
        drow = sb("drow", [1, 64], F32)
        bt_f = sb("bt_f", [128, 2048], F32)
        ct_f = sb("ct_f", [128, 2048], F32)
        bt_bf = sb("bt_bf", [128, 2048], BF16)
        ct_bf = sb("ct_bf", [128, 2048], BF16)
        bhat = sb("bhat", [128, 2048], BF16)
        btf_feat = sb("btff", [128, 16, 128], BF16)
        ctf_feat = sb("ctff", [128, 16, 128], BF16)
        lm = sb("lm", [128, 2, 128], F32)
        lmu = sb("lmu", [128, 2, 128], F32)
        sgf = sb("sgf", [128, 2, 128], F32)
        spp = sb("spp", [128, 32, 128], BF16)
        tp16 = sb("tp16", [32, 128], F32)
        ccol_t = sb("ccol_t", [64, 32], F32)
        scol_t = sb("scol_t", [64, 32], F32)
        dec64 = sb("dec64", [64, 32], F32)
        u_bf = sb("u_bf", [64, 2048], BF16)
        y_mg = ct_f
        gsq = bt_f
        z_g = sb("z_g", [128, 4, 2048], BF16, side="right")
        x_g = sb("x_g", [128, 8, NSL], F32, side="right")

        def rep_ap(tile, pcount, pstride, nmid, last_count):
            return bass.AP(tile[:, :].tensor, 0,
                           [[pstride, pcount], [1, nmid], [0, last_count]])

        with nc.Block() as blk:
            @blk.sync
            def _(sync):
                kb.ffw(sync)
                sync.dma_start(rows[0:1, 0:32], dtb_hbm).then_inc(S["ldf"], 16)
                sync.dma_start(rows[0:1, 32:64], alog_hbm).then_inc(S["ldf"], 16)
                sync.dma_start(rows[0:1, 64:96], dsk_hbm).then_inc(S["ldf"], 16)
                sync.dma_start(wo2[:, :, :], wo2v).then_inc(S["ldw"], 16)
                C["ldf"] += 48
                C["ldw"] += 16

            @blk.vector
            def _(v):
                v.memset(state[:, :], 0.0)
                v.memset(state_bf[:, :], 0.0)
        ldw_wo2 = C["ldw"]
        with nc.Block() as blk:
            @blk.scalar
            def _(s):
                s.wait_ge(S["ldf"], C["ldf"])
                s.activation(rows[0:1, 32:64], rows[0:1, 32:64], AF.Exp)
        with ExitStack() as st_ps:
            pb = st_ps.enter_context(nc.psum_tensor(kb.name("pb"), [128, 96], F32))
            with nc.Block() as blk:
                @blk.tensor
                def _(t):
                    t.matmul(pb[:, :], onesrow[:, :], rows[0:1, 0:96], start=True, stop=True)
            with nc.Block() as blk:
                @blk.vector
                def _(v):
                    v.tensor_copy(dtb_b[:, :], pb[:, 0:32])
                    v.tensor_copy(a_b[:, :], pb[:, 32:64])
                    for h in range(32):
                        v.tensor_scalar(DI[:, h, :], ident_bf[:, :], pb[:, 64 + h:65 + h],
                                        None, ALU.mult)

        for ch in range(nch):
            zg = ch // 4
            cg = ch % 4
            with ExitStack() as st_ps:
                pphi = st_ps.enter_context(nc.psum_tensor(kb.name("pphi"), [128, 32], F32))
                ppsi = st_ps.enter_context(nc.psum_tensor(kb.name("ppsi"), [128, 32], F32))
                pprv = st_ps.enter_context(nc.psum_tensor(kb.name("pprv"), [128, 32], F32))
                ppft = st_ps.enter_context(nc.psum_tensor(kb.name("ppft"), [32, 128], F32))
                pft2 = st_ps.enter_context(nc.psum_tensor(kb.name("pft2"), [32, 128], F32))
                pcc = st_ps.enter_context(nc.psum_tensor(kb.name("pcc"), [128, 32], F32))
                pdd = st_ps.enter_context(nc.psum_tensor(kb.name("pdd"), [64, 32], F32))
                b_ldbc = C["ldbc"]
                bv0, bs0 = kb.chcnt["v"], kb.chcnt["s"]
                b_t2 = C["typ"]
                b_ff = kb.ff_cnt

                with nc.Block() as blk:
                    @blk.sync
                    def _(sync):
                        kb.ffw(sync)
                        sync.dma_start(bcd[:, :], bcd_v[:, ch, :]).then_inc(S["ldbc"], 16)
                        sync.dma_start(xs_c[:, :], xs_v[:, ch, :]).then_inc(S["ldbc"], 16)
                        if cg == 0:
                            sync.dma_start(z_g[:, :, :], z_v[:, zg * 4:(zg + 1) * 4, :]
                                           ).then_inc(S["ldbc"], 16)
                            sync.wait_ge(kb.store_sem, kb.store_cnt)
                            sync.dma_start(x_g[:, :, :],
                                           xin_v[:, :, zg * 512:(zg + 1) * 512]
                                           ).then_inc(S["ldbc"], 16)
                        sync.wait_ge(kb.chsem["v"], bv0 + 10)
                        kb.dma_ff(sync, psf_scr[0:32, :], psf[:, :])
                        sync.wait_ge(kb.ff_sem, kb.ff_cnt)
                        psfl = psf_scr.rearrange("a b -> (a b)").rearrange(
                            "(o f) -> o f", o=1)
                        sync.dma_start(psrow[0:1, :], psfl[0:1, 0:2048]
                                       ).then_inc(S["slh"], 16)
                    C["ldbc"] += 32 + (32 if cg == 0 else 0)
                    tot_ldbc = C["ldbc"]
                    C["slh"] += 16

                    @blk.vector
                    def _(v):
                        v.wait_ge(S["ldbc"], tot_ldbc)
                        kb.step("v", v, v.tensor_add(tA[:, :], bcd[:, 128:160], dtb_b[:, :]))
                        v.wait_ge(kb.chsem["s"], bs0 + 3)
                        kb.step("v", v, v.tensor_scalar_max(tA[:, :], tA[:, :], 0.0))
                        kb.step("v", v, v.tensor_add(dt_t[:, :], tA[:, :], tB[:, :]))
                        kb.step("v", v, v.tensor_mul(tA[:, :], dt_t[:, :], a_b[:, :]))
                        v.wait_ge(S["typ"], b_t2 + 1)
                        kb.step("v", v, v.tensor_copy(phi[:, :], pphi[:, :]))
                        kb.step("v", v, v.tensor_copy(psi[:, :], ppsi[:, :]))
                        v.wait_ge(kb.chsem["s"], bs0 + 4)
                        kb.step("v", v, v.tensor_add(psihat[:, :], ppsi[:, :], tB[:, :]))
                        kb.step("v", v, v.tensor_sub(bdec[:, :], tB[:, :], pprv[:, :]))
                        kb.step("v", v, v.tensor_add(drow[0:1, 0:32], pprv[0:1, :], tA[0:1, :]))
                        v.wait_ge(S["typ"], b_t2 + 2)
                        v.wait_ge(kb.ff_sem, b_ff)
                        kb.step("v", v, v.tensor_copy(psf[:, :], ppft[0:32, :]))
                        kb.step("v", v, v.tensor_scalar(tp16[:, 0:16], invf16[:, :],
                                                        pft2[0:32, 127:128], None, ALU.mult))
                        kb.step("v", v, v.tensor_scalar_mul(tp16[:, 96:112], tp16[:, 0:16],
                                                            kb.fc(1.0 / TWO_PI)))
                        kb.step("v", v, v.tensor_scalar_add(tp16[:, 96:112], tp16[:, 96:112],
                                                            kb.fc(MAGIC)))
                        kb.step("v", v, v.tensor_scalar_sub(tp16[:, 96:112], tp16[:, 96:112],
                                                            kb.fc(MAGIC)))
                        kb.step("v", v, v.scalar_tensor_tensor(tp16[:, 0:16], tp16[:, 96:112],
                                                               kb.fc(-TWO_PI), tp16[:, 0:16],
                                                               ALU.mult, ALU.add))
                        v.wait_ge(kb.chsem["s"], bs0 + 10)
                        kb.step("v", v, v.tensor_copy(tp16[:, 16:32], tp16[:, 0:16]))
                        kb.step("v", v, v.tensor_copy(tp16[:, 80:96], tp16[:, 64:80]))
                        kb.step("v", v, v.memset(tp16[:, 32:64], 1.0))
                        kb.step("v", v, v.memset(tp16[:, 96:128], 0.0))
                        v.wait_ge(S["typ"], b_t2 + 3)
                        kb.step("v", v, v.tensor_copy(ccol_t[:, :], pcc[0:64, :]))
                        kb.step("v", v, v.tensor_copy(scol_t[:, :], pcc[64:128, :]))
                        kb.step("v", v, v.tensor_copy(dec64[:, :], pdd[:, :]))

                    @blk.scalar
                    def _(s):
                        s.wait_ge(kb.chsem["v"], bv0 + 1)
                        kb.step("s", s, s.activation(tB[:, :], tA[:, :], AF.Abs))
                        kb.step("s", s, s.activation(tB[:, :], tB[:, :], AF.Exp, scale=-1.0))
                        kb.step("s", s, s.activation(tB[:, :], tB[:, :], AF.Ln, bias=1.0))
                        s.wait_ge(kb.chsem["v"], bv0 + 3)
                        kb.step("s", s, s.activation(tB[:, :], dt_t[:, :], AF.Ln))
                        s.wait_ge(kb.chsem["v"], bv0 + 8)
                        kb.step("s", s, s.activation(bdec[:, :], bdec[:, :], AF.Exp))
                        kb.step("s", s, s.activation(lam[:, :], psi[:, :], AF.Exp, scale=-1.0))
                        s.wait_ge(kb.chsem["v"], bv0 + 9)
                        kb.step("s", s, s.activation(drow[0:1, 0:32], drow[0:1, 0:32],
                                                     AF.Exp, scale=-1.0))
                        s.wait_ge(kb.chsem["v"], bv0 + 15)
                        kb.step("s", s, s.activation(tp16[:, 64:80], tp16[:, 0:16], AF.Sin))
                        kb.step("s", s, s.activation(tp16[:, 96:112], tp16[:, 0:16], AF.Abs))
                        kb.step("s", s, s.activation(tp16[:, 0:16], tp16[:, 96:112], AF.Sin,
                                                     scale=-1.0, bias=kb.fc(HALF_PI)))

                    @blk.tensor
                    def _(t):
                        t.wait_ge(kb.chsem["v"], bv0 + 4)
                        t.matmul(pphi[:, :], ut01[:, :], dt_t[:, :], start=True, stop=True)
                        t.matmul(ppsi[:, :], ut01[:, :], tA[:, :], start=True, stop=True)
                        t.matmul(pprv[:, :], sl01[:, :], tA[:, :],
                                 start=True, stop=True).then_inc(S["typ"], 1)
                        t.wait_ge(kb.chsem["v"], bv0 + 6)
                        t.transpose(ppft[:, :], psi[:, :], ident[:, :])
                        t.transpose(pft2[:, :], phi[:, :],
                                    ident[:, :]).then_inc(S["typ"], 1)
                        t.wait_ge(kb.chsem["v"], bv0 + 19)
                        t.wait_ge(kb.chsem["s"], bs0 + 7)
                        t.transpose(pcc[:, :], tp16[:, :], ident[0:32, 0:32])
                        t.matmul(pdd[:, :], onesrow[0:1, 0:64], drow[0:1, 0:32],
                                 start=True, stop=True).then_inc(S["typ"], 1)
                    C["typ"] += 3

            if php <= 1:
                continue
            st_ph3 = ExitStack()
            angs = st_ph3.enter_context(nc.sbuf_tensor(kb.name("angs"), [128, 512], F32))
            ca = st_ph3.enter_context(nc.sbuf_tensor(kb.name("ca"), [128, 512], F32))
            sa2 = st_ph3.enter_context(nc.sbuf_tensor(kb.name("sa2"), [128, 512], F32))
            vt1 = st_ph3.enter_context(nc.sbuf_tensor(kb.name("vt1"), [128, 512], F32))
            with nc.Block() as blk:
                @blk.vector
                def _(v):
                    ph_b = rep_ap(phi, 128, 32, 32, 16)
                    kb.step("v", v, v.tensor_mul(angs[:, :], ph_b, invfrep[:, :]))
                    kb.step("v", v, v.tensor_scalar_mul(vt1[:, :], angs[:, :],
                                                        kb.fc(1.0 / TWO_PI)))
                    kb.step("v", v, v.tensor_scalar_add(vt1[:, :], vt1[:, :], kb.fc(MAGIC)))
                    kb.step("v", v, v.tensor_scalar_sub(vt1[:, :], vt1[:, :], kb.fc(MAGIC)))
                    kb.step("v", v, v.scalar_tensor_tensor(angs[:, :], vt1[:, :],
                                                           kb.fc(-TWO_PI), angs[:, :],
                                                           ALU.mult, ALU.add))

                @blk.scalar
                def _(s):
                    s.wait_ge(kb.chsem["v"], kb.chcnt["v"])
                    kb.step("s", s, s.activation(sa2[:, :], angs[:, :], AF.Sin))
                    kb.step("s", s, s.activation(ca[:, :], angs[:, :], AF.Abs))
                    kb.step("s", s, s.activation(ca[:, :], ca[:, :], AF.Sin,
                                                 scale=-1.0, bias=kb.fc(HALF_PI)))
            with nc.Block() as blk:
                @blk.vector
                def _(v):
                    th = bcd[:, :].tensor
                    for base, dstf in ((0, bt_f), (64, ct_f)):
                        Be = bass.AP(th, base, [[160, 128], [0, 32], [1, 16]])
                        Bo = bass.AP(th, base + 16, [[160, 128], [0, 32], [1, 16]])
                        Br = bass.AP(th, base + 32, [[160, 128], [0, 32], [1, 32]])
                        dv = dstf[:, :].rearrange("p (h n) -> p h n", n=64)
                        dste, dsto, dstr = dv[:, :, 0:16], dv[:, :, 16:32], dv[:, :, 32:64]
                        kb.step("v", v, v.tensor_mul(vt1[:, :], Be, ca[:, :]))
                        kb.step("v", v, v.tensor_mul(angs[:, :], Bo, sa2[:, :]))
                        kb.step("v", v, v.tensor_add(dste, vt1[:, :], angs[:, :]))
                        kb.step("v", v, v.tensor_mul(vt1[:, :], Bo, ca[:, :]))
                        kb.step("v", v, v.tensor_mul(angs[:, :], Be, sa2[:, :]))
                        kb.step("v", v, v.tensor_sub(dsto, vt1[:, :], angs[:, :]))
                        kb.step("v", v, v.tensor_copy(dstr, Br))
                    bd_rep = rep_ap(bdec, 128, 32, 32, 64)
                    kb.step("v", v, v.tensor_copy(bt_bf[:, :], bt_f[:, :]))
                    kb.step("v", v, v.tensor_copy(ct_bf[:, :], ct_f[:, :]))
                    kb.step("v", v, v.tensor_mul(
                        bhat[:, :].rearrange("p (h n) -> p h n", n=64),
                        bt_f[:, :].rearrange("p (h n) -> p h n", n=64), bd_rep))
            st_ph3.close()
            with ExitStack() as st_ps:
                ptr = [st_ps.enter_context(nc.psum_tensor(kb.name("ptr"), [128, 128], BF16))
                       for _ in range(6)]
                b_ttr, b_sfe = C["ttr"], C["sfe"]
                with nc.Block() as blk:
                    @blk.tensor
                    def _(t):
                        g = 0
                        for srcb, dstf in ((bt_bf, btf_feat), (ct_bf, ctf_feat)):
                            for pt in range(16):
                                if g >= 6:
                                    t.wait_ge(S["sfe"], b_sfe + g - 5)
                                t.transpose(ptr[g % 6][:, :],
                                            srcb[:, (g % 16) * 128:(g % 16 + 1) * 128],
                                            ident_bf[:, :]).then_inc(S["ttr"], 1)
                                g += 1

                    @blk.scalar
                    def _(s):
                        g = 0
                        for srcb, dstf in ((bt_bf, btf_feat), (ct_bf, ctf_feat)):
                            for pt in range(16):
                                s.wait_ge(S["ttr"], b_ttr + g + 1)
                                s.activation(dstf[:, pt, :], ptr[g % 6][:, :],
                                             AF.Copy).then_inc(S["sfe"], 1)
                                g += 1
                    C["ttr"] += 32
                    C["sfe"] += 32

            if php <= 2:
                continue
            with ExitStack() as st_ps:
                pm = [st_ps.enter_context(nc.psum_tensor(kb.name("pm"), [128, 128], F32))
                      for _ in range(2)]
                pg = [st_ps.enter_context(nc.psum_tensor(kb.name("pg"), [128, 128], F32))
                      for _ in range(2)]
                b_tmsk, b_tg, b_sl2, b_vlu = C["tmsk"], C["tg"], C["sl"], C["vlu"]
                b_slh, b_vma = C["slh"], C["vma"]
                with nc.Block() as blk:
                    @blk.sync
                    def _(sync):
                        sync.wait_ge(S["tmsk"], b_tmsk + 16)
                        psfl = psf_scr.rearrange("a b -> (a b)").rearrange(
                            "(o f) -> o f", o=1)
                        sync.dma_start(psrow[0:1, :], psfl[0:1, 2048:4096]
                                       ).then_inc(S["slh"], 16)
                    C["slh"] += 16

                    @blk.tensor
                    def _(t):
                        for h in range(32):
                            t.wait_ge(S["slh"], b_slh - 16 + (16 if h < 16 else 32))
                            if h >= 2:
                                t.wait_ge(S["vma"], b_vma + h - 1)
                            t.matmul(pm[h % 2][:, :], onesrow[:, :],
                                     psrow[0:1, (h % 16) * 128:(h % 16 + 1) * 128],
                                     start=True, stop=True).then_inc(S["tmsk"], 1)
                            if h >= 2:
                                t.wait_ge(S["vlu"], b_vlu + h - 1)
                            pt_i = h // 2
                            r0 = (h % 2) * 64
                            t.matmul(pg[h % 2][:, :],
                                     btf_feat[r0:r0 + 64, pt_i, :],
                                     ctf_feat[r0:r0 + 64, pt_i, :],
                                     start=True, stop=True).then_inc(S["tg"], 1)

                    @blk.scalar
                    def _(s):
                        for h in range(32):
                            s.wait_ge(S["vma"], b_vma + h + 1)
                            if h >= 2:
                                s.wait_ge(S["vlu"], b_vlu + h - 1)
                            s.activation(lm[:, h % 2, :], lmu[:, h % 2, :], AF.Exp,
                                         scale=-1.0, bias=psihat[:, h:h + 1]
                                         ).then_inc(S["sl"], 1)

                    @blk.vector
                    def _(v):
                        for h in range(33):
                            if h < 32:
                                v.wait_ge(S["tmsk"], b_tmsk + h + 1)
                                if h >= 2:
                                    v.wait_ge(S["sl"], b_sl2 + h - 1)
                                v.tensor_add(lmu[:, h % 2, :], pm[h % 2][:, :],
                                             slbig[:, :]).then_inc(S["vma"], 1)
                            if h >= 1:
                                hp = h - 1
                                v.wait_ge(S["sl"], b_sl2 + hp + 1)
                                v.wait_ge(S["tg"], b_tg + hp + 1)
                                kb.step("v", v, v.tensor_mul(sgf[:, hp % 2, :], pg[hp % 2][:, :],
                                                             lm[:, hp % 2, :]))
                                v.tensor_add(spp[:, hp, :], sgf[:, hp % 2, :],
                                             DI[:, hp, :]).then_inc(S["vlu"], 1)
                C["tmsk"] += 32
                C["tg"] += 32
                C["sl"] += 32
                C["vma"] += 32
                C["vlu"] += 32

            if php <= 3:
                continue
            with ExitStack() as st_ps:
                yin = st_ps.enter_context(nc.psum_tensor(kb.name("yin"), [128, 2048], F32))
                yint = st_ps.enter_context(nc.psum_tensor(kb.name("yint"), [128, 2048], F32))
                with nc.Block() as blk:
                    @blk.tensor
                    def _(t):
                        for pt in range(16):
                            hA, hB = 2 * pt, 2 * pt + 1
                            t.matmul(yint[:, hA * 64:(hA + 1) * 64],
                                     ctf_feat[0:64, pt, :],
                                     state_bf[0:64, hA * 64:(hA + 1) * 64],
                                     start=True, stop=True, tile_position=(0, 0))
                            t.matmul(yin[:, hA * 64:(hA + 1) * 64],
                                     spp[:, hA, :], xs_c[:, hA * 64:(hA + 1) * 64],
                                     start=True, stop=True)
                            t.matmul(yint[:, hB * 64:(hB + 1) * 64],
                                     ctf_feat[64:128, pt, :],
                                     state_bf[64:128, hB * 64:(hB + 1) * 64],
                                     start=True, stop=True, tile_position=(64, 0))
                            t.matmul(yin[:, hB * 64:(hB + 1) * 64],
                                     spp[:, hB, :], xs_c[:, hB * 64:(hB + 1) * 64],
                                     start=True, stop=True)
                with nc.Block() as blk:
                    @blk.vector
                    def _(v):
                        lam_rep = rep_ap(lam, 128, 32, 32, 64)
                        kb.step("v", v, v.tensor_mul(
                            y_mg[:, :].rearrange("p (h n) -> p h n", n=64),
                            yint[:, :].rearrange("p (h n) -> p h n", n=64), lam_rep))
                        kb.step("v", v, v.tensor_add(y_stage[:, cg, :], y_mg[:, :],
                                                     yin[:, :]))
            if php <= 4:
                continue
            st_ph6 = ExitStack()
            u_all = st_ph6.enter_context(nc.sbuf_tensor(kb.name("u_all"), [64, 2048], F32))
            w1s = st_ph6.enter_context(nc.sbuf_tensor(kb.name("w1s"), [64, 2048], F32))
            with ExitStack() as st_ps:
                psa = st_ps.enter_context(nc.psum_tensor(kb.name("psa"), [64, 2048], F32))
                with nc.Block() as blk:
                    @blk.tensor
                    def _(t):
                        for h in range(32):
                            t.matmul(psa[:, h * 64:(h + 1) * 64],
                                     bhat[:, h * 64:(h + 1) * 64],
                                     xs_c[:, h * 64:(h + 1) * 64],
                                     start=True, stop=True)
                with nc.Block() as blk:
                    @blk.vector
                    def _(v):
                        dec_rep = rep_ap(dec64, 64, 32, 32, 64)
                        kb.step("v", v, v.tensor_mul(
                            u_all[:, :].rearrange("p (h n) -> p h n", n=64),
                            state[:, :].rearrange("p (h n) -> p h n", n=64), dec_rep))
                        kb.step("v", v, v.tensor_add(u_all[:, :], u_all[:, :], psa[:, :]))
                        kb.step("v", v, v.tensor_copy(u_bf[:, :], u_all[:, :]))
            with ExitStack() as st_ps:
                psw = st_ps.enter_context(nc.psum_tensor(kb.name("psw"), [64, 2048], F32))
                with nc.Block() as blk:
                    @blk.tensor
                    def _(t):
                        for h in range(32):
                            t.matmul(psw[:, h * 64:(h + 1) * 64], swpb[:, :],
                                     u_bf[:, h * 64:(h + 1) * 64], start=True, stop=True)
                with nc.Block() as blk:
                    @blk.vector
                    def _(v):
                        cc_rep = rep_ap(ccol_t, 64, 32, 32, 64)
                        sc_rep = rep_ap(scol_t, 64, 32, 32, 64)
                        kb.step("v", v, v.tensor_mul(
                            w1s[:, :].rearrange("p (h n) -> p h n", n=64),
                            u_all[:, :].rearrange("p (h n) -> p h n", n=64), cc_rep))
                        kb.step("v", v, v.tensor_mul(
                            u_all[:, :].rearrange("p (h n) -> p h n", n=64),
                            psw[:, :].rearrange("p (h n) -> p h n", n=64), sc_rep))
                        kb.step("v", v, v.tensor_add(state[:, :], w1s[:, :], u_all[:, :]))
                        kb.step("v", v, v.tensor_copy(state_bf[0:64, :], state[:, :]))
                        kb.step("v", v, v.tensor_copy(state_bf[64:128, :], state[:, :]))
            st_ph6.close()

            if cg == 3:
                with ExitStack() as st_ps:
                    gz = st_ps.enter_context(nc.sbuf_tensor(kb.name("gz"), [128, 2048], F32))
                    gn_bf = st_ps.enter_context(nc.sbuf_tensor(kb.name("gn_bf"), [128, 2048], BF16))
                    gnf = st_ps.enter_context(nc.sbuf_tensor(kb.name("gnf"), [128, 16, 128], BF16))
                    rcol = st_ps.enter_context(nc.sbuf_tensor(kb.name("rcol"), [128, 2], F32))
                    ptg = [st_ps.enter_context(nc.psum_tensor(kb.name("ptg"), [128, 128], BF16))
                           for _ in range(4)]
                    pout = [st_ps.enter_context(nc.psum_tensor(kb.name("pout"), [128, 128], F32))
                            for _ in range(2)]
                    for cc2 in range(4):
                        with nc.Block() as blk:
                            @blk.scalar
                            def _(s):
                                s.wait_ge(S["ldbc"], C["ldbc"])
                                kb.step("s", s, s.activation(gz[:, :], z_g[:, cc2, :], AF.Silu))

                            @blk.vector
                            def _(v):
                                v.wait_ge(kb.chsem["s"], kb.chcnt["s"])
                                kb.step("v", v, v.tensor_mul(gz[:, :], gz[:, :],
                                                             y_stage[:, cc2, :]))
                        with nc.Block() as blk:
                            @blk.scalar
                            def _(s):
                                kb.step("s", s, s.activation(gsq[:, :], gz[:, :], AF.Square))

                            @blk.vector
                            def _(v):
                                v.wait_ge(kb.chsem["s"], kb.chcnt["s"])
                                kb.step("v", v, v.reduce_sum(rcol[:, 0:1], gsq[:, :],
                                                             axis=AX.X))
                        with nc.Block() as blk:
                            @blk.scalar
                            def _(s):
                                kb.step("s", s, s.activation(rcol[:, 1:2], rcol[:, 0:1], AF.Ln,
                                                             scale=kb.fc(1.0 / 2048.0),
                                                             bias=kb.fc(1e-5)))
                                kb.step("s", s, s.activation(rcol[:, 1:2], rcol[:, 1:2], AF.Exp,
                                                             scale=kb.fc(-0.5)))
                                kb.step("s", s, s.activation(gn_bf[:, :], gz[:, :], AF.Copy,
                                                             scale=rcol[:, 1:2]))
                        with nc.Block() as blk:
                            b_ttr2, b_sfe2 = C["ttr"], C["sfe"]

                            @blk.tensor
                            def _(t):
                                for pt in range(16):
                                    if pt >= 4:
                                        t.wait_ge(S["sfe"], b_sfe2 + pt - 3)
                                    t.transpose(ptg[pt % 4][:, :],
                                                gn_bf[:, pt * 128:(pt + 1) * 128],
                                                ident_bf[:, :]).then_inc(S["ttr"], 1)

                            @blk.scalar
                            def _(s):
                                for pt in range(16):
                                    s.wait_ge(S["ttr"], b_ttr2 + pt + 1)
                                    s.activation(gnf[:, pt, :], ptg[pt % 4][:, :],
                                                 AF.Copy).then_inc(S["sfe"], 1)
                            C["ttr"] += 16
                            C["sfe"] += 16
                        with nc.Block() as blk:
                            b_mmo, b_evo = C["mmo"], C["evo"]

                            @blk.tensor
                            def _(t):
                                t.wait_ge(S["ldw"], ldw_wo2)
                                for dout in range(8):
                                    if dout >= 2:
                                        t.wait_ge(S["evo"], b_evo + dout - 1)
                                    for k in range(16):
                                        ins = t.matmul(pout[dout % 2][:, :],
                                                       wo2[:, k, dout * 128:(dout + 1) * 128],
                                                       gnf[:, k, :],
                                                       start=(k == 0), stop=(k == 15))
                                    ins.then_inc(S["mmo"], 1)

                            @blk.vector
                            def _(v):
                                for dout in range(8):
                                    v.wait_ge(S["mmo"], b_mmo + dout + 1)
                                    v.tensor_add(x_g[:, dout, cc2 * 128:(cc2 + 1) * 128],
                                                 x_g[:, dout, cc2 * 128:(cc2 + 1) * 128],
                                                 pout[dout % 2][:, :]).then_inc(S["evo"], 1)
                            C["mmo"] += 8
                            C["evo"] += 8
                    with nc.Block() as blk:
                        @blk.sync
                        def _(sync):
                            kb.store(sync, xout_v[:, :, zg * 512:(zg + 1) * 512],
                                     x_g[:, :, :])


def build_ssm_nc(nch=32, php=9):
    kb = KB()
    x_in = kb.din("x", [D, T], F32)
    win = kb.din("win", [1024, 4256], F32)
    wout = kb.din("wout", [2048, 1024], F32)
    dtb = kb.din("dtb", [1, 32], F32)
    alog = kb.din("alog", [1, 32], F32)
    dsk = kb.din("dsk", [1, 32], F32)
    nw = kb.din("nw", [2048, 1], F32)
    y = kb.dout("y", [D, T], F32)
    xres = kb.dscr("xres", [D, T], F32)
    win_bf = kb.dscr("winbf", [1024, 4256], BF16)
    wout2_bf = kb.dscr("wout2bf", [2048, 1024], BF16)
    xn_scr = kb.dscr("xnscr", [D, T], BF16)
    z_scr = kb.dscr("zscr", [T, 2048], BF16)
    xs_scr = kb.dscr("xsscr", [T, 2048], BF16)
    bcd_scr = kb.dscr("bcdscr", [T, 160], F32)
    psf_scr = kb.dscr("psfscr", [32, 128], F32)
    kb.init()
    emit_xin_copy(kb, x_in, xres)
    emit_ssm_preconvert(kb, win, wout, nw, win_bf, wout2_bf)
    sx = SSMSems(kb)
    emit_ssm(kb, sx, xres, y, win_bf, wout2_bf, dtb, alog, dsk,
             xn_scr, z_scr, xs_scr, bcd_scr, psf_scr, nch=nch, php=php)
    nc = kb.finalize()
    return nc, {"c_fvals": kb.fvals_array()}


def build_l1ffn_nc():
    kb = KB()
    x_in = kb.din("x", [D, T], F32)
    k0 = kb.din("k0", [2048, 4096], U8)
    k1 = kb.din("k1", [4096, 4096], U8)
    k2 = kb.din("k2", [4096, 1024], U8)
    s0 = kb.din("s0", [1, 1], F32)
    s1 = kb.din("s1", [1, 1], F32)
    s2 = kb.din("s2", [1, 1], F32)
    y = kb.dout("y", [D, T], F32)
    k0e = kb.dscr("k0e", [1024, 4096], BF16)
    k1bf = kb.dscr("k1bf", [4096, 4096], BF16)
    k2bf = kb.dscr("k2bf", [4096, 1024], BF16)
    xres = kb.dscr("xres", [D, T], F32)
    kb.init()
    emit_xin_copy(kb, x_in, xres)
    st = kb.stack
    raw = emit_scalar_cols2(kb, st, {"s0": s0, "s1": s1, "s2": s2})
    fold = kb.sb("fold", [128, 3], F32)
    with kb.nc.Block() as blk:
        @blk.vector
        def _(v):
            v.tensor_scalar_mul(fold[:, 0:1], raw["s0"], kb.fc(4.0))
            v.tensor_scalar_mul(fold[:, 1:2], raw["s1"], kb.fc(2.0))
            v.tensor_scalar_mul(fold[:, 2:3], raw["s2"], kb.fc(0.5))
    scols = {"s0x2": fold[:, 0:1], "s1x2": fold[:, 1:2], "s2h": fold[:, 2:3]}
    emit_l1ffn_preconvert(kb, k0, k1, k2, k0e, k1bf, k2bf)
    ffs = L1FFNSems(kb)
    emit_l1ffn(kb, ffs, xres, y, k0e, k1bf, k2bf, scols)
    nc = kb.finalize()
    return nc, {"c_fvals": kb.fvals_array()}


def build_full_nc():
    kb = KB()
    x_in = kb.din("x", [D, T], F32)
    w_qkv = kb.din("w_qkv", [1024, 3072], F32)
    w_out = kb.din("w_out", [1024, 1024], F32)
    rw = kb.din("rw", [8, 1024], F32)
    w1 = kb.din("w1", [1024, 8192], F32)
    w2 = kb.din("w2", [8192, 1024], F32)
    win = kb.din("win", [1024, 4256], F32)
    wout = kb.din("wout", [2048, 1024], F32)
    dtb = kb.din("dtb", [1, 32], F32)
    alog = kb.din("alog", [1, 32], F32)
    dsk = kb.din("dsk", [1, 32], F32)
    nw = kb.din("nw", [2048, 1], F32)
    k0 = kb.din("k0", [2048, 4096], U8)
    k1 = kb.din("k1", [4096, 4096], U8)
    k2 = kb.din("k2", [4096, 1024], U8)
    s0 = kb.din("s0", [1, 1], F32)
    s1 = kb.din("s1", [1, 1], F32)
    s2 = kb.din("s2", [1, 1], F32)
    rope = kb.din("c_rope", [4, 128, T], F32)
    y = kb.dout("y", [D, T], F32)
    xres = kb.dscr("xres", [D, T], F32)
    w1bf = kb.dscr("w1bf", [1024, 8192], BF16)
    w2bf = kb.dscr("w2bf", [8192, 1024], BF16)
    wqkv_bf = kb.dscr("wqkvbf", [1024, 3072], BF16)
    wout_bf = kb.dscr("woutbf", [1024, 1024], BF16)
    rope_scr = kb.dscr("ropescr", [4, 128, T], F32)
    k0e = kb.dscr("k0e", [1024, 4096], BF16)
    k1bf = kb.dscr("k1bf", [4096, 4096], BF16)
    k2bf = kb.dscr("k2bf", [4096, 1024], BF16)
    win_bf = kb.dscr("winbf", [1024, 4256], BF16)
    wout2_bf = kb.dscr("wout2bf", [2048, 1024], BF16)
    xn_scr = kb.dscr("xnscr", [D, T], BF16)
    q_scr = kb.dscr("qscr", [8, 128, T], BF16)
    z_scr = kb.dscr("zscr", [T, 2048], BF16)
    xs_scr = kb.dscr("xsscr", [T, 2048], BF16)
    bcd_scr = kb.dscr("bcdscr", [T, 160], F32)
    psf_scr = kb.dscr("psfscr", [32, 128], F32)

    kb.init()
    emit_xin_copy(kb, x_in, xres)
    emit_moe_preconvert(kb, w1, w2, w1bf, w2bf)
    emit_attn_preconvert(kb, w_qkv, w_out, rope, wqkv_bf, wout_bf, rope_scr)
    emit_l1ffn_preconvert(kb, k0, k1, k2, k0e, k1bf, k2bf)
    emit_ssm_preconvert(kb, win, wout, nw, win_bf, wout2_bf)
    st = kb.stack
    raw = emit_scalar_cols2(kb, st, {"s0": s0, "s1": s1, "s2": s2})
    fold = kb.sb("fold", [128, 3], F32)
    with kb.nc.Block() as blk:
        @blk.vector
        def _(v):
            v.tensor_scalar_mul(fold[:, 0:1], raw["s0"], kb.fc(4.0))
            v.tensor_scalar_mul(fold[:, 1:2], raw["s1"], kb.fc(2.0))
            v.tensor_scalar_mul(fold[:, 2:3], raw["s2"], kb.fc(0.5))
    scols = {"s0x2": fold[:, 0:1], "s1x2": fold[:, 1:2], "s2h": fold[:, 2:3]}

    ms = MoESems(kb)
    asx = AttnSems(kb)
    ffs = L1FFNSems(kb)
    sx = SSMSems(kb)
    emit_moe(kb, ms, xres, xres, w1bf, w2bf, rw)
    emit_attn(kb, asx, xres, xres, wqkv_bf, wout_bf, rope_scr, xn_scr, q_scr)
    emit_moe(kb, ms, xres, xres, w1bf, w2bf, rw)
    emit_l1ffn(kb, ffs, xres, xres, k0e, k1bf, k2bf, scols)
    emit_ssm(kb, sx, xres, xres, win_bf, wout2_bf, dtb, alog, dsk,
             xn_scr, z_scr, xs_scr, bcd_scr, psf_scr)
    emit_l1ffn(kb, ffs, xres, y, k0e, k1bf, k2bf, scols)
    nc = kb.finalize()
    return nc, {"c_fvals": kb.fvals_array(), "c_rope": rope_tables()}


def kernel(**inputs):
    consts = host_constants()
    nc, extra = build_full_nc()
    consts.update(extra)
    x = np.asarray(inputs["L0_attn_x"])
    in_maps = []
    for b in range(NCORES):
        m = dict(consts)
        m["x"] = np.ascontiguousarray(x[b].T)
        m["w_qkv"] = np.asarray(inputs["L0_attn_w_qkv"], np.float32)
        m["w_out"] = np.asarray(inputs["L0_attn_w_out"], np.float32)
        m["rw"] = np.asarray(inputs["L0_ffn_router_w"], np.float32)
        m["w1"] = np.asarray(inputs["L0_ffn_w1"], np.float32)
        m["w2"] = np.asarray(inputs["L0_ffn_w2"], np.float32)
        m["win"] = np.asarray(inputs["L1_attn_W_in"], np.float32)
        m["wout"] = np.asarray(inputs["L1_attn_W_out"], np.float32)
        m["dtb"] = np.asarray(inputs["L1_attn_dt_bias"], np.float32).reshape(1, 32)
        m["alog"] = np.asarray(inputs["L1_attn_A_log"], np.float32).reshape(1, 32)
        m["dsk"] = np.asarray(inputs["L1_attn_D_skip"], np.float32).reshape(1, 32)
        m["nw"] = np.asarray(inputs["L1_attn_norm_weight"], np.float32).reshape(2048, 1)
        m["k0"] = np.asarray(inputs["L1_ffn_k0"]).view(np.uint8)
        m["k1"] = np.asarray(inputs["L1_ffn_k1"]).view(np.uint8)
        m["k2"] = np.asarray(inputs["L1_ffn_k2"]).view(np.uint8)
        m["s0"] = np.asarray(inputs["L1_ffn_s0"], np.float32).reshape(1, 1)
        m["s1"] = np.asarray(inputs["L1_ffn_s1"], np.float32).reshape(1, 1)
        m["s2"] = np.asarray(inputs["L1_ffn_s2"], np.float32).reshape(1, 1)
        in_maps.append(m)
    res = run_bass_kernel_spmd(nc, in_maps, core_ids=list(range(NCORES)))
    out = np.stack([np.ascontiguousarray(res.results[b]["y"].T)
                    for b in range(NCORES)]).astype(np.float32)
    return out


# version 06
# speedup vs baseline: 1.2359x; 1.0076x over previous
import numpy as np
import ml_dtypes
from contextlib import ExitStack

import concourse.bass as bass
import concourse.mybir as mybir
from concourse.bass_utils import run_bass_kernel_spmd

F32 = mybir.dt.float32
BF16 = mybir.dt.bfloat16
U8 = mybir.dt.uint8
AF = mybir.ActivationFunctionType
ALU = mybir.AluOpType
AX = mybir.AxisListType

D = 1024
T = 4096
NCORES = 8

BF16NP = ml_dtypes.bfloat16


CONST_SPECS = {
    "c_ident_bf": ([128, 128], BF16),
    "c_ident_f32": ([128, 128], F32),
    "c_ones_row": ([1, 128], F32),
    "c_ones_col_bf": ([128, 1], BF16),
    "c_swpq": ([128, 128], F32),
    "c_diagmask": ([128, 896], BF16),
    "c_ut01": ([128, 128], F32),
    "c_sl01": ([128, 128], F32),
    "c_slbig": ([128, 128], F32),
    "c_invfrep": ([128, 512], F32),
    "c_invf16": ([32, 16], F32),
    "c_swpb": ([64, 64], BF16),
}
NFVALS = 64


def host_constants():
    c = {}
    c["c_ident_bf"] = np.eye(128, dtype=BF16NP)
    c["c_ident_f32"] = np.eye(128, dtype=np.float32)
    c["c_ones_row"] = np.ones((1, 128), dtype=np.float32)
    c["c_ones_col_bf"] = np.ones((128, 1), dtype=BF16NP)
    swp = np.zeros((64, 64), np.float32)
    for i in range(32):
        swp[2 * i, 2 * i + 1] = -1.0
        swp[2 * i + 1, 2 * i] = 1.0
    swpT = swp.T
    sw = np.zeros((128, 128), np.float32)
    sw[0:64, 0:64] = swpT
    sw[64:128, 64:128] = swpT
    c["c_swpq"] = sw
    s_i = np.arange(128)[:, None]
    u_i = np.arange(896)[None, :]
    c["c_diagmask"] = (s_i <= u_i - 384).astype(BF16NP)
    k_i = np.arange(128)[:, None]
    m_i = np.arange(128)[None, :]
    c["c_ut01"] = (k_i <= m_i).astype(np.float32)
    c["c_sl01"] = (k_i > m_i).astype(np.float32)
    c["c_slbig"] = (1e30 * (k_i > m_i)).astype(np.float32)
    invf = (1.0 / 10000.0 ** (np.arange(16) / 16.0)).astype(np.float32)
    c["c_invfrep"] = np.broadcast_to(np.tile(invf, 32)[None, :], (128, 512)).copy()
    c["c_invf16"] = np.broadcast_to(invf[None, :], (32, 16)).copy()
    swpb = np.zeros((64, 64), np.float32)
    for i in range(16):
        swpb[i, 16 + i] = -1.0
        swpb[16 + i, i] = 1.0
    c["c_swpb"] = swpb.T.astype(BF16NP)
    return c


def rope_tables():
    inv = 1.0 / 10000.0 ** (np.arange(0, 64, 2, dtype=np.float64) / 64)
    t = np.arange(T, dtype=np.float64)
    fr = np.outer(t, inv)
    c64 = np.repeat(np.cos(fr), 2, axis=1).T
    s64 = np.repeat(np.sin(fr), 2, axis=1).T
    c128 = np.concatenate([c64, c64], 0).astype(np.float32)
    s128 = np.concatenate([s64, s64], 0).astype(np.float32)
    return np.stack([0.125 * c128, 0.125 * s128, c128, s128]).astype(np.float32)


class KB:
    def __init__(self):
        self.nc = bass.Bass("TRN2", target_bir_lowering=False)
        self.stack = ExitStack()
        self.uid = 0
        self.const_sb = {}
        self.store_sem = None
        self.store_cnt = 0

    def name(self, p):
        self.uid += 1
        return f"{p}_{self.uid}"

    def sb(self, p, shape, dt, side="left"):
        return self.stack.enter_context(
            self.nc.sbuf_tensor(self.name(p), shape, dt, side=side))

    def sem(self, p="sem"):
        return self.stack.enter_context(self.nc.semaphore(self.name(p)))

    def din(self, nm, shape, dt):
        return self.nc.dram_tensor(nm, list(shape), dt, kind="ExternalInput").ap()

    def dout(self, nm, shape, dt):
        return self.nc.dram_tensor(nm, list(shape), dt, kind="ExternalOutput").ap()

    def dscr(self, nm, shape, dt):
        return self.nc.dram_tensor(self.name(nm), list(shape), dt).ap()

    def init(self):
        nc = self.nc
        self.store_sem = self.sem("g_store")
        self.ff_sem = self.sem("g_ff")
        self.ff_cnt = 0
        self.chain_init()
        self.fvals = {}
        self.fpool = self.sb("c_fpool_sb", [128, NFVALS], F32)
        for nm, (shape, dt) in CONST_SPECS.items():
            self.const_sb[nm] = self.sb(nm, shape, dt)
        ld = self.sem("c_ld")
        n = 0
        specs = list(CONST_SPECS.items())
        cin = {nm: self.din(nm, shape, dt) for nm, (shape, dt) in specs}
        fv_in = self.din("c_fvals", [128, NFVALS], F32)
        with nc.Block() as blk:
            @blk.sync
            def _(sync):
                nonlocal n
                for nm, _spec in specs:
                    sync.dma_start(self.const_sb[nm][:, :], cin[nm]).then_inc(ld, 16)
                    n += 1
                sync.dma_start(self.fpool[:, :], fv_in).then_inc(ld, 16)
                n += 1
                sync.wait_ge(ld, 16 * n)

    def chain_init(self):
        self.chsem = {"v": self.sem("chv"), "s": self.sem("chs")}
        self.chcnt = {"v": 0, "s": 0}
        self.lnsem = {n: self.sem("ln" + n)
                      for n in ("ld", "ld2", "s1", "s2", "t1", "t2", "t3")}
        self.lncnt = {n: 0 for n in ("ld", "ld2", "s1", "s2", "t1", "t2", "t3")}

    def step(self, key, eng, ins):
        self.chcnt[key] += 1
        ins.then_inc(self.chsem[key], 1)
        eng.wait_ge(self.chsem[key], self.chcnt[key])

    def fc(self, value):
        v = float(value)
        if v not in self.fvals:
            idx = len(self.fvals)
            assert idx < NFVALS, "float const pool exhausted"
            self.fvals[v] = idx
            self.nc.const_aps.register(F32, v, self.fpool[:, idx:idx + 1])
        return v

    def fvals_array(self):
        arr = np.zeros((128, NFVALS), dtype=np.float32)
        for v, i in self.fvals.items():
            arr[:, i] = v
        return arr

    def dma_ff(self, q, dst_ap, src_ap):
        q.dma_start(dst_ap, src_ap).then_inc(self.ff_sem, 16)
        self.ff_cnt += 16

    def ffw(self, eng):
        eng.wait_ge(self.ff_sem, self.ff_cnt)

    def store(self, q, dst_ap, src_ap):
        q.dma_start(dst_ap, src_ap).then_inc(self.store_sem, 16)
        self.store_cnt += 16

    def finalize(self):
        nc = self.nc
        cnt = self.store_cnt
        with nc.Block() as blk:
            @blk.sync
            def _(sync):
                sync.wait_ge(self.store_sem, cnt)
        self.stack.close()
        return nc


def xview(x_hbm):
    return x_hbm.rearrange("(c p) t -> p c t", p=128)


def emit_ln_slab(kb, st, x_sl, xn_bf, scr_bf, nsl, xsrc=None, prefetch=(),
                 ld_wait=None, store_wait=False):
    nc = kb.nc
    onescol = kb.const_sb["c_ones_col_bf"]
    onesrow = kb.const_sb["c_ones_row"]
    nq = nsl // 512
    L, LC = kb.lnsem, kb.lncnt
    ps_sum = st.enter_context(nc.psum_tensor(kb.name("ps_sum"), [1, nsl], F32))
    ps_sq = st.enter_context(nc.psum_tensor(kb.name("ps_sq"), [1, nsl], F32))
    ps_mb = st.enter_context(nc.psum_tensor(kb.name("ps_mb"), [128, nsl], F32))
    ps_rb = st.enter_context(nc.psum_tensor(kb.name("ps_rb"), [128, nsl], F32))
    m_row = st.enter_context(nc.sbuf_tensor(kb.name("m_row"), [1, nsl], F32))
    v_row = st.enter_context(nc.sbuf_tensor(kb.name("v_row"), [1, nsl], F32))
    r_row = st.enter_context(nc.sbuf_tensor(kb.name("r_row"), [1, nsl], F32))
    tmp = st.enter_context(nc.sbuf_tensor(kb.name("ln_tmp"), [128, nsl], F32))
    scr2 = st.enter_context(nc.sbuf_tensor(kb.name("ln_scr2"), [128, 8, nsl], BF16))
    b = {n: LC[n] for n in LC}
    bv, bs = kb.chcnt["v"], kb.chcnt["s"]

    with nc.Block() as blk:
        if xsrc is not None or prefetch:
            @blk.sync
            def _(sync):
                kb.ffw(sync)
                if store_wait and prefetch:
                    sync.wait_ge(kb.store_sem, kb.store_cnt)
                if xsrc is not None:
                    sync.dma_start(x_sl[:, :, :], xsrc).then_inc(L["ld"], 16)
                for dst, srcap, snm in prefetch:
                    sync.dma_start(dst, srcap).then_inc(L[snm], 16)
            if xsrc is not None:
                LC["ld"] += 16
            for dst, srcap, snm in prefetch:
                LC[snm] += 16

        @blk.scalar
        def _(s):
            if xsrc is not None:
                s.wait_ge(L["ld"], b["ld"] + 16)
            elif ld_wait is not None:
                s.wait_ge(L[ld_wait[0]], ld_wait[1])
            kb.ffw(s)
            s.activation(scr_bf[:, :, :], x_sl[:, :, :], AF.Copy).then_inc(L["s1"], 1)
            s.activation(scr2[:, :, :], x_sl[:, :, :], AF.Square).then_inc(L["s2"], 1)
            s.wait_ge(kb.chsem["v"], bv + 4)
            kb.step("s", s, s.activation(v_row[:, :], v_row[:, :], AF.Ln, bias=kb.fc(1e-5)))
            kb.step("s", s, s.activation(r_row[:, :], v_row[:, :], AF.Exp, scale=kb.fc(-0.5)))

        @blk.tensor
        def _(t):
            t.wait_ge(L["s1"], b["s1"] + 1)
            for c in range(8):
                for h in range(nq):
                    t.matmul(ps_sum[0:1, h * 512:(h + 1) * 512], onescol[:, :],
                             scr_bf[:, c, h * 512:(h + 1) * 512],
                             start=(c == 0), stop=(c == 7))
            t.wait_ge(L["s2"], b["s2"] + 1)
            ins = None
            for c in range(8):
                for h in range(nq):
                    ins = t.matmul(ps_sq[0:1, h * 512:(h + 1) * 512], onescol[:, :],
                                   scr2[:, c, h * 512:(h + 1) * 512],
                                   start=(c == 0), stop=(c == 7))
            ins.then_inc(L["t1"], 1)
            t.wait_ge(kb.chsem["s"], bs + 2)
            for h in range(nq):
                sl = slice(h * 512, (h + 1) * 512)
                t.matmul(ps_mb[:, sl], onesrow[:, :], m_row[0:1, sl], start=True, stop=True)
                ins = t.matmul(ps_rb[:, sl], onesrow[:, :], r_row[0:1, sl], start=True, stop=True)
            ins.then_inc(L["t3"], 1)

        @blk.vector
        def _(v):
            v.wait_ge(L["t1"], b["t1"] + 1)
            kb.step("v", v, v.tensor_scalar_mul(m_row[:, :], ps_sum[:, :], kb.fc(1.0 / D)))
            kb.step("v", v, v.tensor_scalar_mul(v_row[:, :], ps_sq[:, :], kb.fc(1.0 / D)))
            kb.step("v", v, v.tensor_mul(r_row[:, :], m_row[:, :], m_row[:, :]))
            kb.step("v", v, v.tensor_sub(v_row[:, :], v_row[:, :], r_row[:, :]))
            v.wait_ge(L["t3"], b["t3"] + 1)
            kb.ffw(v)
            for c in range(8):
                kb.step("v", v, v.tensor_sub(tmp[:, :], x_sl[:, c, :], ps_mb[:, :]))
                kb.step("v", v, v.tensor_mul(xn_bf[:, c, :], tmp[:, :], ps_rb[:, :]))
    LC["s1"] += 1
    LC["s2"] += 1
    LC["t1"] += 1
    LC["t3"] += 1


def emit_scalar_cols2(kb, st, scalar_aps):
    nc = kb.nc
    onesrow = kb.const_sb["c_ones_row"]
    names = list(scalar_aps)
    n = len(names)
    row = st.enter_context(nc.sbuf_tensor(kb.name("sc_row"), [1, n], F32))
    allc = st.enter_context(nc.sbuf_tensor(kb.name("sc_cols"), [128, n], F32))
    ld = kb.sem("sc_ld")
    with nc.Block() as blk:
        @blk.sync
        def _(sync):
            for i, nm in enumerate(names):
                sync.dma_start(row[0:1, i:i + 1], scalar_aps[nm]).then_inc(ld, 16)
            sync.wait_ge(ld, 16 * n)
    with nc.psum_tensor(kb.name("sc_ps"), [128, n], F32) as ps:
        with nc.Block() as blk:
            @blk.tensor
            def _(t):
                t.matmul(ps[:, :], onesrow[:, :], row[:, :], start=True, stop=True)
        with nc.Block() as blk:
            @blk.vector
            def _(v):
                v.tensor_copy(allc[:, :], ps[:, :])
    return {nm: allc[:, i:i + 1] for i, nm in enumerate(names)}


def emit_l1ffn_preconvert(kb, k0_hbm, k1_hbm, k2_hbm, k0e_hbm, k1bf_hbm, k2bf_hbm):
    nc = kb.nc
    with ExitStack() as st:
        ta = st.enter_context(nc.sbuf_tensor(kb.name("pc_a"), [128, 4096], BF16))
        tb = st.enter_context(nc.sbuf_tensor(kb.name("pc_b"), [128, 4096], BF16))
        tc = st.enter_context(nc.sbuf_tensor(kb.name("pc_c"), [128, 4096], BF16))
        ld = kb.sem("pc_ld")
        stc = kb.sem("pc_st")
        nld = 0
        nst = 0
        for c in range(8):
            with nc.Block() as blk:
                @blk.gpsimd
                def _(g):
                    nonlocal nld
                    g.dma_start(ta[:, :], k0_hbm[c * 128:(c + 1) * 128, :]).then_inc(ld, 16)
                    g.dma_start(tb[:, :], k0_hbm[1024 + c * 128:1024 + (c + 1) * 128, :]).then_inc(ld, 16)
                    nld += 32
                    g.wait_ge(ld, nld)
            with nc.Block() as blk:
                @blk.vector
                def _(v):
                    v.tensor_sub(tc[:, :], ta[:, :], tb[:, :])
            with nc.Block() as blk:
                @blk.sync
                def _(sync):
                    nonlocal nst
                    sync.dma_start(k0e_hbm[c * 128:(c + 1) * 128, :], tc[:, :]).then_inc(stc, 16)
                    nst += 16
                    sync.wait_ge(stc, nst)
        for src, dst, nchunks, w in ((k1_hbm, k1bf_hbm, 32, 4096), (k2_hbm, k2bf_hbm, 32, 1024)):
            for c in range(nchunks):
                with nc.Block() as blk:
                    @blk.gpsimd
                    def _(g):
                        nonlocal nld
                        g.dma_start(ta[:, 0:w], src[c * 128:(c + 1) * 128, :]).then_inc(ld, 16)
                        nld += 16
                        g.wait_ge(ld, nld)
                with nc.Block() as blk:
                    @blk.scalar
                    def _(s):
                        s.activation(tc[:, 0:w], ta[:, 0:w], AF.Copy, bias=kb.fc(-1.0), scale=kb.fc(2.0))
                with nc.Block() as blk:
                    @blk.sync
                    def _(sync):
                        nonlocal nst
                        sync.dma_start(dst[c * 128:(c + 1) * 128, :], tc[:, 0:w]).then_inc(stc, 16)
                        nst += 16
                        sync.wait_ge(stc, nst)


class L1FFNSems:
    def __init__(self, kb):
        names = ["ldx", "ldk0", "ldk1", "ldk1b", "ldk2", "mm1", "ev1", "mm2",
                 "ev2", "mm3", "ev3"]
        self.s = {n: kb.sem("l1f_" + n) for n in names}
        self.c = {n: 0 for n in names}


def emit_l1ffn(kb, ffs, x_in_hbm, x_out_hbm, k0e_hbm, k1bf_hbm, k2bf_hbm, scols):
    nc = kb.nc
    S, C = ffs.s, ffs.c
    NSL = 1024
    xin_v = xview(x_in_hbm)
    xout_v = xview(x_out_hbm)
    k0e_v = k0e_hbm.rearrange("(c p) w -> p c w", p=128)
    k1_v = k1bf_hbm.rearrange("(c p) m -> p c m", p=128)
    k2_v = k2bf_hbm.rearrange("(c p) m -> p c m", p=128)

    for isl in range(T // NSL):
        tsl = slice(isl * NSL, (isl + 1) * NSL)
        with ExitStack() as st_xn:
            xn = st_xn.enter_context(nc.sbuf_tensor(kb.name("xn"), [128, 8, NSL], BF16))
            with ExitStack() as st_ln:
                x_sl = st_ln.enter_context(nc.sbuf_tensor(kb.name("x_sl"), [128, 8, NSL], F32))
                scr = st_ln.enter_context(nc.sbuf_tensor(kb.name("scr"), [128, 8, NSL], BF16))
                emit_ln_slab(kb, st_ln, x_sl, xn, scr, NSL, xsrc=xin_v[:, :, tsl])
            with ExitStack() as st_a2:
                a2 = st_a2.enter_context(
                    nc.sbuf_tensor(kb.name("a2"), [128, 32, NSL], BF16, side="right"))
                with ExitStack() as st_a1:
                    a1 = st_a1.enter_context(nc.sbuf_tensor(kb.name("a1"), [128, 32, NSL], BF16))
                    with ExitStack() as st_k0:
                        k0e = st_k0.enter_context(nc.sbuf_tensor(kb.name("k0e"), [128, 8, 2048], BF16))
                        gph = 16 * (NSL // 512)
                        with ExitStack() as st_ps, nc.Block() as blk:
                            psl = [st_ps.enter_context(nc.psum_tensor(kb.name("p1"), [128, 512], F32))
                                   for _ in range(4)]
                            mm0, ev0, ld0 = C["mm1"], C["ev1"], C["ldk0"]

                            @blk.sync
                            def _(sync):
                                for wh in range(2):
                                    if wh == 1:
                                        sync.wait_ge(S["mm1"], mm0 + gph)
                                    sync.dma_start(k0e[:, :, :],
                                                   k0e_v[:, :, wh * 2048:(wh + 1) * 2048]
                                                   ).then_inc(S["ldk0"], 16)

                            @blk.tensor
                            def _(t):
                                g = 0
                                for wh in range(2):
                                    t.wait_ge(S["ldk0"], ld0 + 16 * (wh + 1))
                                    for w in range(16):
                                        for h in range(NSL // 512):
                                            if g >= 4:
                                                t.wait_ge(S["ev1"], ev0 + g - 3)
                                            for k in range(8):
                                                ins = t.matmul(psl[g % 4][:, :],
                                                               k0e[:, k, w * 128:(w + 1) * 128],
                                                               xn[:, k, h * 512:(h + 1) * 512],
                                                               start=(k == 0), stop=(k == 7))
                                            ins.then_inc(S["mm1"], 1)
                                            g += 1

                            @blk.scalar
                            def _(s):
                                g = 0
                                for wh in range(2):
                                    for w in range(16):
                                        for h in range(NSL // 512):
                                            s.wait_ge(S["mm1"], mm0 + g + 1)
                                            s.activation(a1[:, wh * 16 + w, h * 512:(h + 1) * 512],
                                                         psl[g % 4][:, :], AF.Relu,
                                                         scale=scols["s0x2"]).then_inc(S["ev1"], 1)
                                            g += 1
                            C["ldk0"] += 32
                            C["mm1"] += 64
                            C["ev1"] += 64
                    with ExitStack() as st_k1:
                        k1b = [st_k1.enter_context(
                                   nc.sbuf_tensor(kb.name("k1b"), [128, 32, 256], BF16, side="right"))
                               for _ in range(2)]
                        npc = 16
                        gpp = 2 * (NSL // 512)
                        with ExitStack() as st_ps, nc.Block() as blk:
                            psl = [st_ps.enter_context(nc.psum_tensor(kb.name("p2"), [128, 512], F32))
                                   for _ in range(4)]
                            mm0, ev0 = C["mm2"], C["ev2"]
                            ld0 = [C["ldk1"], C["ldk1b"]]
                            k1sem = [S["ldk1"], S["ldk1b"]]

                            @blk.sync
                            def _(sync):
                                for pc in range(npc):
                                    if pc >= 2:
                                        sync.wait_ge(S["mm2"], mm0 + gpp * (pc - 1))
                                    sync.dma_start(k1b[pc % 2][:, :, :],
                                                   k1_v[:, :, pc * 256:(pc + 1) * 256]
                                                   ).then_inc(k1sem[pc % 2], 16)

                            @blk.tensor
                            def _(t):
                                g = 0
                                for pc in range(npc):
                                    t.wait_ge(k1sem[pc % 2], ld0[pc % 2] + 16 * (pc // 2 + 1))
                                    for mc in range(2):
                                        for h in range(NSL // 512):
                                            if g >= 4:
                                                t.wait_ge(S["ev2"], ev0 + g - 3)
                                            for k in range(32):
                                                ins = t.matmul(psl[g % 4][:, :],
                                                               k1b[pc % 2][:, k, mc * 128:(mc + 1) * 128],
                                                               a1[:, k, h * 512:(h + 1) * 512],
                                                               start=(k == 0), stop=(k == 31))
                                            ins.then_inc(S["mm2"], 1)
                                            g += 1

                            @blk.scalar
                            def _(s):
                                g = 0
                                for pc in range(npc):
                                    for mc in range(2):
                                        for h in range(NSL // 512):
                                            s.wait_ge(S["mm2"], mm0 + g + 1)
                                            s.activation(a2[:, pc * 2 + mc, h * 512:(h + 1) * 512],
                                                         psl[g % 4][:, :], AF.Relu,
                                                         scale=scols["s1x2"]).then_inc(S["ev2"], 1)
                                            g += 1
                            C["ldk1"] += 16 * (npc // 2)
                            C["ldk1b"] += 16 * (npc // 2)
                            C["mm2"] += npc * gpp
                            C["ev2"] += npc * gpp
                with ExitStack() as st_m3:
                    k2sb = st_m3.enter_context(nc.sbuf_tensor(kb.name("k2sb"), [128, 32, 1024], BF16))
                    x_sl2 = st_m3.enter_context(nc.sbuf_tensor(kb.name("x_sl2"), [128, 8, NSL], F32))
                    x_out = x_sl2
                    with ExitStack() as st_ps, nc.Block() as blk:
                        psl = [st_ps.enter_context(nc.psum_tensor(kb.name("p3"), [128, 512], F32))
                               for _ in range(4)]
                        mm0, ev0, ld0 = C["mm3"], C["ev3"], C["ldk2"]

                        @blk.sync
                        def _(sync):
                            sync.dma_start(k2sb[:, :, :], k2_v).then_inc(S["ldk2"], 16)
                            sync.dma_start(x_sl2[:, :, :], xin_v[:, :, tsl]).then_inc(S["ldk2"], 16)
                            sync.wait_ge(S["ev3"], ev0 + 8 * (NSL // 512))
                            kb.store(sync, xout_v[:, :, tsl], x_out[:, :, :])

                        @blk.tensor
                        def _(t):
                            t.wait_ge(S["ldk2"], ld0 + 32)
                            g = 0
                            for dout in range(8):
                                for h in range(NSL // 512):
                                    if g >= 4:
                                        t.wait_ge(S["ev3"], ev0 + g - 3)
                                    for k in range(32):
                                        ins = t.matmul(psl[g % 4][:, :],
                                                       k2sb[:, k, dout * 128:(dout + 1) * 128],
                                                       a2[:, k, h * 512:(h + 1) * 512],
                                                       start=(k == 0), stop=(k == 31))
                                    ins.then_inc(S["mm3"], 1)
                                    g += 1

                        @blk.vector
                        def _(v):
                            v.wait_ge(S["ldk2"], ld0 + 32)
                            g = 0
                            for dout in range(8):
                                for h in range(NSL // 512):
                                    v.wait_ge(S["mm3"], mm0 + g + 1)
                                    sl = slice(h * 512, (h + 1) * 512)
                                    v.scalar_tensor_tensor(
                                        x_out[:, dout, sl], psl[g % 4][:, :],
                                        scols["s2h"], x_sl2[:, dout, sl],
                                        ALU.mult, ALU.add).then_inc(S["ev3"], 1)
                                    g += 1
                        C["ldk2"] += 32
                        C["mm3"] += 16
                        C["ev3"] += 16


def emit_xin_copy(kb, x_ext, xres):
    nc = kb.nc
    sem = kb.sem("xc")
    n = 0
    xi = xview(x_ext)
    xo = xview(xres)
    with ExitStack() as st:
        big = st.enter_context(nc.sbuf_tensor(kb.name("xcopy"), [128, 8, T // 2], F32))
        for h in range(2):
            tsl = slice(h * (T // 2), (h + 1) * (T // 2))
            with nc.Block() as blk:
                @blk.sync
                def _(sync):
                    nonlocal n
                    sync.dma_start(big[:, :, :], xi[:, :, tsl]).then_inc(sem, 16)
                    n += 16
                    sync.wait_ge(sem, n)
                    sync.dma_start(xo[:, :, tsl], big[:, :, :]).then_inc(sem, 16)
                    n += 16
                    sync.wait_ge(sem, n)


def emit_moe_preconvert(kb, w1_hbm, w2_hbm, w1bf_hbm, w2bf_hbm):
    nc = kb.nc
    w1v = w1_hbm.rearrange("(c p) m -> p c m", p=128)
    w1bv = w1bf_hbm.rearrange("(c p) m -> p c m", p=128)
    w2v = w2_hbm.rearrange("(c p) d -> p c d", p=128)
    w2bv = w2bf_hbm.rearrange("(c p) d -> p c d", p=128)
    with ExitStack() as st:
        buf = st.enter_context(nc.sbuf_tensor(kb.name("pcw"), [128, 8192], BF16))
        ld = kb.sem("pcw_ld")
        stc = kb.sem("pcw_st")
        nld = nst = 0
        for c in range(8):
            with nc.Block() as blk:
                @blk.gpsimd
                def _(g):
                    nonlocal nld
                    g.dma_start(buf[:, :], w1v[:, c, :]).then_inc(ld, 16)
                    nld += 16
                    g.wait_ge(ld, nld)
            with nc.Block() as blk:
                @blk.sync
                def _(sync):
                    nonlocal nst
                    sync.dma_start(w1bv[:, c, :], buf[:, :]).then_inc(stc, 16)
                    nst += 16
                    sync.wait_ge(stc, nst)
        bview = buf[:, :].rearrange("p (c d) -> p c d", c=8)
        for g8 in range(8):
            with nc.Block() as blk:
                @blk.gpsimd
                def _(g):
                    nonlocal nld
                    g.dma_start(bview, w2v[:, g8 * 8:(g8 + 1) * 8, :]).then_inc(ld, 16)
                    nld += 16
                    g.wait_ge(ld, nld)
            with nc.Block() as blk:
                @blk.sync
                def _(sync):
                    nonlocal nst
                    sync.dma_start(w2bv[:, g8 * 8:(g8 + 1) * 8, :], bview).then_inc(stc, 16)
                    nst += 16
                    sync.wait_ge(stc, nst)


class MoESems:
    def __init__(self, kb):
        names = ["ldx", "ldw1", "ldw1b", "ldw2", "ldw2b", "ldr", "mm1", "rel",
                 "va", "gb", "mm2", "ev2", "lg"]
        self.s = {n: kb.sem("moe_" + n) for n in names}
        self.c = {n: 0 for n in names}


def emit_moe(kb, ms, x_in_hbm, x_out_hbm, w1bf_hbm, w2bf_hbm, rw_hbm, dbg=None, parts="all"):
    nc = kb.nc
    S, C = ms.s, ms.c
    NSL = 512
    ident = kb.const_sb["c_ident_f32"]
    onesrow = kb.const_sb["c_ones_row"]
    xin_v = xview(x_in_hbm)
    xout_v = xview(x_out_hbm)
    w1bv = w1bf_hbm.rearrange("(c p) m -> p c m", p=128)
    w2bv = w2bf_hbm.rearrange("(c p) d -> p c d", p=128)

    g_scr = kb.dscr("g_scr", [8, NSL], F32)
    with ExitStack() as st_call:
        rw_sb = st_call.enter_context(nc.sbuf_tensor(kb.name("rw"), [8, 1024], F32))
        rwT = st_call.enter_context(nc.sbuf_tensor(kb.name("rwT"), [128, 8, 8], BF16))
        with nc.Block() as blk:
            @blk.sync
            def _(sync):
                sync.dma_start(rw_sb[:, :], rw_hbm).then_inc(S["ldr"], 16)
                C["ldr"] += 16
                sync.wait_ge(S["ldr"], C["ldr"])
        with ExitStack() as st_ps:
            ps_rt = st_ps.enter_context(nc.psum_tensor(kb.name("ps_rt"), [128, 8, 8], F32))
            with nc.Block() as blk:
                @blk.tensor
                def _(t):
                    for c in range(8):
                        t.matmul(ps_rt[:, c, :], rw_sb[0:8, c * 128:(c + 1) * 128],
                                 ident[0:8, 0:8], start=True, stop=True)
            with nc.Block() as blk2:
                @blk2.scalar
                def _(s):
                    s.activation(rwT[:, :, :], ps_rt[:, :, :], AF.Copy)

        x_ln2 = [st_call.enter_context(nc.sbuf_tensor(kb.name("x_sl"), [128, 8, NSL], F32))
                 for _ in range(2)]
        base_ld = {"ld": kb.lncnt["ld"], "ld2": kb.lncnt["ld2"]}
        lpn = lambda s: "ld" if s % 2 == 0 else "ld2"
        for isl in range(T // NSL):
            tsl = slice(isl * NSL, (isl + 1) * NSL)
            x_sl = x_ln2[isl % 2]
            with ExitStack() as st_sl:
                xn = st_sl.enter_context(nc.sbuf_tensor(kb.name("xn"), [128, 8, NSL], BF16))
                gT = st_sl.enter_context(nc.sbuf_tensor(kb.name("gT"), [1, 8, NSL], F32))
                with ExitStack() as st_ln:
                    scr = st_ln.enter_context(nc.sbuf_tensor(kb.name("scr"), [128, 8, NSL], BF16))
                    pf = []
                    if isl + 1 < T // NSL:
                        nts = slice((isl + 1) * NSL, (isl + 2) * NSL)
                        pf.append((x_ln2[(isl + 1) % 2][:, :, :], xin_v[:, :, nts],
                                   lpn(isl + 1)))
                    if isl == 0:
                        emit_ln_slab(kb, st_ln, x_sl, xn, scr, NSL,
                                     xsrc=xin_v[:, :, tsl], prefetch=pf,
                                     store_wait=True)
                    else:
                        emit_ln_slab(kb, st_ln, x_sl, xn, scr, NSL, prefetch=pf,
                                     store_wait=True,
                                     ld_wait=(lpn(isl),
                                              base_ld[lpn(isl)] + 16 * (isl // 2 + 1)))
                if parts == "noln":
                    continue
                skip_router = parts in ("norouter",)
                with ExitStack() as st_rt:
                    probs = st_rt.enter_context(nc.sbuf_tensor(kb.name("probs"), [128, 4, 8], F32))
                    ps_lg = st_rt.enter_context(nc.psum_tensor(kb.name("ps_lg"), [128, 4, 8], F32))
                    with nc.Block() as blk:
                        @blk.tensor
                        def _(t):
                            for tg in range(4):
                                for c in range(8):
                                    t.matmul(ps_lg[:, tg, :],
                                             xn[:, c, tg * 128:(tg + 1) * 128],
                                             rwT[:, c, :], start=(c == 0), stop=(c == 7))
                    with nc.Block() as blk:
                        @blk.scalar
                        def _(s):
                            s.activation(probs[:, :, :], ps_lg[:, :, :], AF.Sigmoid)
                    g_tile = st_rt.enter_context(nc.sbuf_tensor(kb.name("gt2"), [128, 4, 8], F32))
                    cols = st_rt.enter_context(nc.sbuf_tensor(kb.name("tcol"), [128, 4, 6], F32))
                    eq1 = st_rt.enter_context(nc.sbuf_tensor(kb.name("eq1"), [128, 4, 8], F32))
                    eq2 = st_rt.enter_context(nc.sbuf_tensor(kb.name("eq2"), [128, 4, 8], F32))
                    p2 = st_rt.enter_context(nc.sbuf_tensor(kb.name("p2t"), [128, 4, 8], F32))
                    with nc.Block() as blk:
                        @blk.vector
                        def _(v):
                            chv = kb.chsem["v"]
                            bvr = kb.chcnt["v"]
                            def mk_ops(tg):
                                pr = probs[:, tg, :]
                                m1, m2, den, rcp, w1c, w2c = [
                                    cols[:, tg, i:i + 1] for i in range(6)]
                                e1, e2, pp = eq1[:, tg, :], eq2[:, tg, :], p2[:, tg, :]
                                return [
                                    lambda: v.reduce_max(m1, pr, axis=AX.X),
                                    lambda: v.tensor_scalar(e1, pr, m1, None, ALU.is_equal),
                                    lambda: v.scalar_tensor_tensor(pp, e1, kb.fc(-1e30), pr,
                                                                   ALU.mult, ALU.add),
                                    lambda: v.reduce_max(m2, pp, axis=AX.X),
                                    lambda: v.tensor_scalar(e2, pp, m2, None, ALU.is_equal),
                                    lambda: v.tensor_add(den, m1, m2),
                                    lambda: v.tensor_scalar(den, den, kb.fc(1e-20), 2.0,
                                                            ALU.add, ALU.mult),
                                    lambda: v.reciprocal(rcp, den),
                                    lambda: v.tensor_mul(w1c, m1, rcp),
                                    lambda: v.tensor_mul(w2c, m2, rcp),
                                    lambda: v.tensor_scalar(g_tile[:, tg, :], e2, w2c,
                                                            None, ALU.mult),
                                    lambda: v.scalar_tensor_tensor(g_tile[:, tg, :], e1, w1c,
                                                                   g_tile[:, tg, :],
                                                                   ALU.mult, ALU.add),
                                ]
                            allops = [mk_ops(tg) for tg in range(4)]
                            for s_ in range(12):
                                for tg in range(4):
                                    if s_ >= 1:
                                        v.wait_ge(chv, bvr + (s_ - 1) * 4 + tg + 1)
                                    kb.chcnt["v"] += 1
                                    allops[tg][s_]().then_inc(chv, 1)
                    gts = st_rt.enter_context(nc.sbuf_tensor(kb.name("gts"), [8, 4, 128], F32))
                    with ExitStack() as st_pt:
                        ps_gt = st_pt.enter_context(nc.psum_tensor(kb.name("ps_gt"), [8, 4, 128], F32))
                        with nc.Block() as blk:
                            @blk.tensor
                            def _(t):
                                for tg in range(4):
                                    t.transpose(ps_gt[0:8, tg, :], g_tile[:, tg, 0:8], ident[:, :])
                        with nc.Block() as blk:
                            @blk.vector
                            def _(v):
                                v.tensor_copy(gts[:, :, :], ps_gt[0:8, :, :])
                    with nc.Block() as blk:
                        @blk.sync
                        def _(sync):
                            sync.wait_ge(S["lg"], C["lg"])
                            sync.dma_start(g_scr, gts[:, :, :].rearrange("e tg t -> e (tg t)")
                                           ).then_inc(S["lg"], 16)
                            C["lg"] += 16
                            sync.wait_ge(S["lg"], C["lg"])
                            sync.dma_start(gT[0:1, :, :],
                                           g_scr.rearrange("e t -> (e t)").rearrange("(o f) -> o f", o=1).rearrange("o (e t) -> o e t", e=8)
                                           ).then_inc(S["lg"], 16)
                            C["lg"] += 16
                    if dbg is not None and isl == 0:
                        with nc.Block() as blk:
                            @blk.sync
                            def _(sync):
                                kb.store(sync, dbg["probs"], probs[:, :, :].rearrange("p tg e -> p (tg e)"))
                                kb.store(sync, dbg["gT"], gT[0:1, :, :].rearrange("o e t -> o (e t)"))
                if skip_router:
                    pass
                if parts == "now1":
                    continue
                a_all = st_sl.enter_context(nc.sbuf_tensor(kb.name("a_all"), [128, 64, NSL], BF16))
                w1b = [st_sl.enter_context(nc.sbuf_tensor(kb.name("w1b"), [128, 8, 1024], BF16, side="right"))
                       for _ in range(2)]
                w2b = [st_sl.enter_context(nc.sbuf_tensor(kb.name("w2b"), [128, 8, 1024], BF16, side="right"))
                       for _ in range(2)]
                rbuf = st_sl.enter_context(nc.sbuf_tensor(kb.name("rbuf"), [128, NSL], F32))
                sqb = [st_sl.enter_context(nc.sbuf_tensor(kb.name("sqb"), [128, NSL], F32))
                       for _ in range(2)]
                with ExitStack() as st_ps, nc.Block() as blk:
                    hps = [st_ps.enter_context(nc.psum_tensor(kb.name("hps"), [128, NSL], F32))
                           for _ in range(2)]
                    gbps = [st_ps.enter_context(nc.psum_tensor(kb.name("gbps"), [128, NSL], F32))
                            for _ in range(2)]
                    b_ldw1 = [C["ldw1"], C["ldw1b"]]
                    b_lg = C["lg"]
                    b_mm1, b_rel, b_va, b_gb = (C["mm1"], C["rel"],
                                                C["va"], C["gb"])
                    w1sem = [S["ldw1"], S["ldw1b"]]

                    @blk.sync
                    def _(sync):
                        for e in range(8):
                            if e >= 2:
                                sync.wait_ge(S["mm1"], b_mm1 + 8 * (e - 1))
                            sync.dma_start(w1b[e % 2][:, :, :],
                                           w1bv[:, :, e * 1024:(e + 1) * 1024]
                                           ).then_inc(w1sem[e % 2], 16)

                    @blk.tensor
                    def _(t):
                        t.wait_ge(S["lg"], b_lg)
                        for e in range(8):
                            t.wait_ge(w1sem[e % 2], b_ldw1[e % 2] + 16 * (e // 2 + 1))
                            if e >= 2:
                                t.wait_ge(S["va"], b_va + 8 * (e - 1))
                            t.matmul(gbps[e % 2][:, :], onesrow[:, :], gT[0:1, e, :],
                                     start=True, stop=True).then_inc(S["gb"], 1)
                            for w in range(8):
                                g = e * 8 + w
                                if g >= 2:
                                    t.wait_ge(S["rel"], b_rel + g - 1)
                                for k in range(8):
                                    ins = t.matmul(hps[g % 2][:, :],
                                                   w1b[e % 2][:, k, w * 128:(w + 1) * 128],
                                                   xn[:, k, :],
                                                   start=(k == 0), stop=(k == 7))
                                ins.then_inc(S["mm1"], 1)

                    @blk.scalar
                    def _(s):
                        for e in range(8):
                            for w in range(8):
                                g = e * 8 + w
                                s.wait_ge(S["mm1"], b_mm1 + g + 1)
                                if g >= 2:
                                    s.wait_ge(S["va"], b_va + g - 1)
                                if g >= 1:
                                    s.wait_ge(S["rel"], b_rel + g)
                                kb.step("s", s, s.activation(rbuf[:, :], hps[g % 2][:, :], AF.Relu))
                                s.activation(sqb[g % 2][:, :], rbuf[:, :],
                                             AF.Square).then_inc(S["rel"], 1)

                    @blk.vector
                    def _(v):
                        for e in range(8):
                            v.wait_ge(S["gb"], b_gb + e + 1)
                            for w in range(8):
                                g = e * 8 + w
                                v.wait_ge(S["rel"], b_rel + g + 1)
                                v.tensor_mul(a_all[:, g, :], sqb[g % 2][:, :],
                                             gbps[e % 2][:, :]).then_inc(S["va"], 1)
                    C["ldw1"] += 16 * 4
                    C["ldw1b"] += 16 * 4
                    C["gb"] += 8
                    C["mm1"] += 64
                    C["rel"] += 64
                    C["va"] += 64
                if dbg is not None and isl == 0:
                    with nc.Block() as blk:
                        @blk.sync
                        def _(sync):
                            kb.store(sync, dbg["xn"], xn[:, :, :].rearrange("p c t -> p (c t)"))
                            kb.store(sync, dbg["a0"], a_all[:, 0:8, :].rearrange("p c t -> p (c t)"))
                if parts == "now2":
                    continue
                with ExitStack() as st_ps, nc.Block() as blk:
                    ops = st_ps.enter_context(nc.psum_tensor(kb.name("ops"), [128, 8, NSL], F32))
                    b_ldw2 = [C["ldw2"], C["ldw2b"]]
                    b_mm2, b_ev2 = C["mm2"], C["ev2"]
                    w2sem = [S["ldw2"], S["ldw2b"]]
                    do_add = parts in ("all", "w2nostore")
                    do_store = parts == "all"

                    @blk.sync
                    def _(sync):
                        for e in range(8):
                            if e >= 2:
                                sync.wait_ge(S["mm2"], b_mm2 + 8 * (e - 1))
                            sync.dma_start(w2b[e % 2][:, :, :],
                                           w2bv[:, e * 8:(e + 1) * 8, :]
                                           ).then_inc(w2sem[e % 2], 16)

                    @blk.tensor
                    def _(t):
                        for e in range(8):
                            t.wait_ge(w2sem[e % 2], b_ldw2[e % 2] + 16 * (e // 2 + 1))
                            for dout in range(8):
                                for k in range(8):
                                    ins = t.matmul(ops[:, dout, :],
                                                   w2b[e % 2][:, k, dout * 128:(dout + 1) * 128],
                                                   a_all[:, e * 8 + k, :],
                                                   start=(e == 0 and k == 0),
                                                   stop=(e == 7 and k == 7))
                                if dout == 7:
                                    ins.then_inc(S["mm2"], 8)

                    @blk.vector
                    def _(v):
                        if do_add:
                            v.wait_ge(S["mm2"], b_mm2 + 64)
                            for dout in range(8):
                                v.tensor_add(x_sl[:, dout, :], x_sl[:, dout, :],
                                             ops[:, dout, :]).then_inc(S["ev2"], 1)
                    C["ldw2"] += 16 * 4
                    C["ldw2b"] += 16 * 4
                    C["mm2"] += 64
                    C["ev2"] += 8 if do_add else 0
                if do_store:
                    with nc.Block() as blk:
                        @blk.sync
                        def _(sync):
                            kb.store(sync, xout_v[:, :, tsl], x_sl[:, :, :])


def build_moe_nc(debug=False, parts="all"):
    kb = KB()
    x_in = kb.din("x", [D, T], F32)
    w1 = kb.din("w1", [1024, 8192], F32)
    w2 = kb.din("w2", [8192, 1024], F32)
    rw = kb.din("rw", [8, 1024], F32)
    y = kb.dout("y", [D, T], F32)
    w1bf = kb.dscr("w1bf", [1024, 8192], BF16)
    w2bf = kb.dscr("w2bf", [8192, 1024], BF16)
    xres = kb.dscr("xres", [D, T], F32)
    dbg = None
    if debug:
        dbg = {"probs": kb.dout("d_probs", [128, 32], F32),
               "gT": kb.dout("d_gT", [1, 8 * 512], F32),
               "xn": kb.dout("d_xn", [128, 8 * 512], BF16),
               "a0": kb.dout("d_a0", [128, 8 * 512], BF16)}
    kb.init()
    emit_xin_copy(kb, x_in, xres)
    if parts in ("all", "pc"):
        emit_moe_preconvert(kb, w1, w2, w1bf, w2bf)
    if parts in ("all", "moe", "norouter", "now2", "now1"):
        ms = MoESems(kb)
        emit_moe(kb, ms, xres, y, w1bf, w2bf, rw, dbg=dbg, parts=parts)
    nc = kb.finalize()
    return nc, {"c_fvals": kb.fvals_array()}


def emit_attn_preconvert(kb, wqkv_hbm, wout_hbm, rope_hbm, wqkv_bf, wout_bf, rope_scr):
    nc = kb.nc
    qv = wqkv_hbm.rearrange("(c p) m -> p c m", p=128)
    qbv = wqkv_bf.rearrange("(c p) m -> p c m", p=128)
    ov = wout_hbm.rearrange("(c p) m -> p c m", p=128)
    obv = wout_bf.rearrange("(c p) m -> p c m", p=128)
    with ExitStack() as st:
        buf = st.enter_context(nc.sbuf_tensor(kb.name("pca"), [128, 8, 3072], BF16))
        fbuf = st.enter_context(nc.sbuf_tensor(kb.name("pcf"), [128, 4096], F32))
        ld = kb.sem("pca_ld")
        stc = kb.sem("pca_st")
        nld = nst = 0
        with nc.Block() as blk:
            @blk.gpsimd
            def _(g):
                nonlocal nld
                g.dma_start(buf[:, :, :], qv).then_inc(ld, 16)
                nld += 16
                g.wait_ge(ld, nld)
        with nc.Block() as blk:
            @blk.sync
            def _(sync):
                nonlocal nst
                sync.dma_start(qbv, buf[:, :, :]).then_inc(stc, 16)
                nst += 16
                sync.wait_ge(stc, nst)
        with nc.Block() as blk:
            @blk.gpsimd
            def _(g):
                nonlocal nld
                g.dma_start(buf[:, 0:8, 0:1024], ov).then_inc(ld, 16)
                nld += 16
                g.wait_ge(ld, nld)
        with nc.Block() as blk:
            @blk.sync
            def _(sync):
                nonlocal nst
                sync.dma_start(obv, buf[:, 0:8, 0:1024]).then_inc(stc, 16)
                nst += 16
                sync.wait_ge(stc, nst)
        for tb in range(4):
            with nc.Block() as blk:
                @blk.gpsimd
                def _(g):
                    nonlocal nld
                    g.dma_start(fbuf[:, :], rope_hbm[tb, :, :]).then_inc(ld, 16)
                    nld += 16
                    g.wait_ge(ld, nld)
            with nc.Block() as blk:
                @blk.sync
                def _(sync):
                    nonlocal nst
                    sync.dma_start(rope_scr[tb, :, :], fbuf[:, :]).then_inc(stc, 16)
                    nst += 16
                    sync.wait_ge(stc, nst)


class AttnSems:
    def __init__(self, kb):
        names = ["ldx", "ldxn", "ldxnb", "ldq", "ldqb", "ldw", "ldtab", "ldtabb",
                 "mm", "sev", "tswp",
                 "vrope", "ttr", "vcp", "smm", "sexp", "vmask", "tpv", "vrcp",
                 "trb", "vfin", "mmo", "evo"]
        self.s = {n: kb.sem("at_" + n) for n in names}
        self.c = {n: 0 for n in names}


def emit_attn(kb, asx, x_in_hbm, x_out_hbm, wqkv_bf, wout_bf, rope_scr,
              xn_scr, q_scr):
    nc = kb.nc
    S, C = asx.s, asx.c
    NSL = 512
    nslabs = T // NSL
    ident_bf = kb.const_sb["c_ident_bf"]
    onesrow = kb.const_sb["c_ones_row"]
    swpq = kb.const_sb["c_swpq"]
    dmask = kb.const_sb["c_diagmask"]
    xin_v = xview(x_in_hbm)
    xout_v = xview(x_out_hbm)
    xn_v = xn_scr.rearrange("(c p) t -> p c t", p=128)
    qbv = wqkv_bf.rearrange("(c p) m -> p c m", p=128)
    obv = wout_bf.rearrange("(c p) m -> p c m", p=128)
    q_v = q_scr.rearrange("pr p t -> p pr t")

    with ExitStack() as st_x2:
        x_sl2 = [st_x2.enter_context(nc.sbuf_tensor(kb.name("x_sl"), [128, 8, NSL], F32))
                 for _ in range(2)]
        base_ld = {"ld": kb.lncnt["ld"], "ld2": kb.lncnt["ld2"]}
        lpn = lambda s: "ld" if s % 2 == 0 else "ld2"
        for isl in range(nslabs):
            tsl = slice(isl * NSL, (isl + 1) * NSL)
            with ExitStack() as st:
                xn = st.enter_context(nc.sbuf_tensor(kb.name("xn"), [128, 8, NSL], BF16))
                scr = st.enter_context(nc.sbuf_tensor(kb.name("scr"), [128, 8, NSL], BF16))
                pf = []
                if isl + 1 < nslabs:
                    nts = slice((isl + 1) * NSL, (isl + 2) * NSL)
                    pf.append((x_sl2[(isl + 1) % 2][:, :, :], xin_v[:, :, nts],
                               lpn(isl + 1)))
                if isl == 0:
                    emit_ln_slab(kb, st, x_sl2[0], xn, scr, NSL,
                                 xsrc=xin_v[:, :, tsl], prefetch=pf)
                else:
                    emit_ln_slab(kb, st, x_sl2[isl % 2], xn, scr, NSL, prefetch=pf,
                                 ld_wait=(lpn(isl),
                                          base_ld[lpn(isl)] + 16 * (isl // 2 + 1)))
                with nc.Block() as blk:
                    @blk.sync
                    def _(sync):
                        kb.dma_ff(sync, xn_v[:, :, tsl], xn[:, :, :])

    with ExitStack() as st_kv:
        k_sb = st_kv.enter_context(nc.sbuf_tensor(kb.name("k_sb"), [128, 8, T], BF16, side="right"))
        vp_sb = st_kv.enter_context(nc.sbuf_tensor(kb.name("vp_sb"), [128, 32, 1040], BF16, side="right"))
        with nc.Block() as blk:
            @blk.vector
            def _(v):
                ones_view = vp_sb[:, :, :].rearrange("p st (h c) -> p st h c", c=65)[:, :, :, 64:65]
                v.memset(ones_view, 1.0)

        for phase in ("q", "k", "v"):
            with ExitStack() as st_ph:
                wpart = st_ph.enter_context(nc.sbuf_tensor(kb.name("wp"), [128, 8, 1024], BF16))
                xn2 = [st_ph.enter_context(nc.sbuf_tensor(kb.name("xns"), [128, 8, NSL], BF16))
                       for _ in range(2)]
                if phase in ("q", "k"):
                    tabs2 = [st_ph.enter_context(nc.sbuf_tensor(kb.name("tabs"), [128, 2, NSL], F32))
                             for _ in range(2)]
                part0 = {"q": 0, "k": 1, "v": 2}[phase]
                ti = {"q": 0, "k": 2, "v": 0}[phase]
                b_xn0 = {"ldxn": C["ldxn"], "ldxnb": C["ldxnb"]}
                b_tab0 = {"ldtab": C["ldtab"], "ldtabb": C["ldtabb"]}
                with nc.Block() as blk:
                    @blk.sync
                    def _(sync):
                        kb.ffw(sync)
                        sync.dma_start(wpart[:, :, :],
                                       qbv[:, :, part0 * 1024:(part0 + 1) * 1024]
                                       ).then_inc(S["ldw"], 16)
                        C["ldw"] += 16
                for isl in range(nslabs):
                    tsl = slice(isl * NSL, (isl + 1) * NSL)
                    xn = xn2[isl % 2]
                    if phase in ("q", "k"):
                        tabs = tabs2[isl % 2]
                    with ExitStack() as st_sl:
                        raw = [st_sl.enter_context(nc.sbuf_tensor(kb.name("raw"), [128, NSL], F32))
                               for _ in range(2)]
                        if phase in ("q", "k"):
                            t1b = st_sl.enter_context(nc.sbuf_tensor(kb.name("t1b"), [128, NSL], F32))
                            t2b = st_sl.enter_context(nc.sbuf_tensor(kb.name("t2b"), [128, NSL], F32))
                        if phase == "q":
                            qtmp = st_sl.enter_context(nc.sbuf_tensor(kb.name("qtmp"), [128, 8, NSL], BF16))
                        if phase == "v":
                            vtmp = [st_sl.enter_context(nc.sbuf_tensor(kb.name("vtmp"), [128, NSL], BF16))
                                    for _ in range(2)]
                        with ExitStack() as st_ps, nc.Block() as blk:
                            mps = [st_ps.enter_context(nc.psum_tensor(kb.name("mps"), [128, NSL], F32))
                                   for _ in range(2)]
                            if phase in ("q", "k"):
                                wps = [st_ps.enter_context(nc.psum_tensor(kb.name("wps"), [128, NSL], F32))
                                       for _ in range(2)]
                            else:
                                tps = [st_ps.enter_context(nc.psum_tensor(kb.name("tps"), [128, 128], BF16))
                                       for _ in range(2)]
                            b = {n: C[n] for n in ("ldxn", "ldtab", "mm", "sev",
                                                   "tswp", "vrope", "ttr", "vcp")}
                            ldw_now = C["ldw"]

                            @blk.sync
                            def _(sync):
                                kb.ffw(sync)
                                for lsl in ([isl, isl + 1] if isl == 0 else [isl + 1]):
                                    if lsl >= nslabs:
                                        continue
                                    lts = slice(lsl * NSL, (lsl + 1) * NSL)
                                    xs = "ldxn" if lsl % 2 == 0 else "ldxnb"
                                    sync.dma_start(xn2[lsl % 2][:, :, :],
                                                   xn_v[:, :, lts]).then_inc(S[xs], 16)
                                    C[xs] += 16
                                    if phase in ("q", "k"):
                                        ts_ = "ldtab" if lsl % 2 == 0 else "ldtabb"
                                        sync.dma_start(tabs2[lsl % 2][:, :, :],
                                                       rope_scr[ti:ti + 2, :, lts].rearrange("tb p t -> p tb t")
                                                       ).then_inc(S[ts_], 16)
                                        C[ts_] += 16

                            @blk.tensor
                            def _(t):
                                t.wait_ge(S["ldw"], ldw_now)
                                xs = "ldxn" if isl % 2 == 0 else "ldxnb"
                                t.wait_ge(S[xs], b_xn0[xs] + 16 * (isl // 2 + 1))
                                for pr in range(8):
                                    if pr >= 2:
                                        t.wait_ge(S["sev"], b["sev"] + pr - 1)
                                    for k in range(8):
                                        ins = t.matmul(mps[pr % 2][:, :],
                                                       wpart[:, k, pr * 128:(pr + 1) * 128],
                                                       xn[:, k, :],
                                                       start=(k == 0), stop=(k == 7))
                                    ins.then_inc(S["mm"], 1)
                                    if phase in ("q", "k"):
                                        t.wait_ge(S["sev"], b["sev"] + pr + 1)
                                        if pr >= 2:
                                            t.wait_ge(S["vrope"], b["vrope"] + pr - 1)
                                        t.matmul(wps[pr % 2][:, :], swpq[:, :], raw[pr % 2][:, :],
                                                 start=True, stop=True).then_inc(S["tswp"], 1)
                                    else:
                                        t.wait_ge(S["sev"], b["sev"] + pr + 1)
                                        for c4 in range(4):
                                            g = pr * 4 + c4
                                            if g >= 2:
                                                t.wait_ge(S["vcp"], b["vcp"] + 2 * (g - 1))
                                            t.transpose(tps[c4 % 2][:, :],
                                                        vtmp[pr % 2][:, c4 * 128:(c4 + 1) * 128],
                                                        ident_bf[:, :]).then_inc(S["ttr"], 1)

                            @blk.scalar
                            def _(s):
                                for pr in range(8):
                                    s.wait_ge(S["mm"], b["mm"] + pr + 1)
                                    if phase in ("q", "k"):
                                        if pr >= 2:
                                            s.wait_ge(S["tswp"], b["tswp"] + pr - 1)
                                            s.wait_ge(S["vrope"], b["vrope"] + pr - 1)
                                        s.activation(raw[pr % 2][:, :], mps[pr % 2][:, :],
                                                     AF.Copy).then_inc(S["sev"], 1)
                                    else:
                                        if pr >= 2:
                                            s.wait_ge(S["ttr"], b["ttr"] + 4 * (pr - 1))
                                        s.activation(vtmp[pr % 2][:, :], mps[pr % 2][:, :],
                                                     AF.Copy).then_inc(S["sev"], 1)

                            @blk.vector
                            def _(v):
                                kb.ffw(v)
                                if phase in ("q", "k"):
                                    ts_ = "ldtab" if isl % 2 == 0 else "ldtabb"
                                    v.wait_ge(S[ts_], b_tab0[ts_] + 16 * (isl // 2 + 1))
                                    for pr in range(8):
                                        v.wait_ge(S["tswp"], b["tswp"] + pr + 1)
                                        if pr >= 1:
                                            v.wait_ge(S["vrope"], b["vrope"] + pr)
                                        kb.step("v", v, v.tensor_mul(t1b[:, :], raw[pr % 2][:, :],
                                                                     tabs[:, 0, :]))
                                        kb.step("v", v, v.tensor_mul(t2b[:, :], wps[pr % 2][:, :],
                                                                     tabs[:, 1, :]))
                                        dst = qtmp[:, pr, :] if phase == "q" else k_sb[:, pr, tsl]
                                        v.tensor_add(dst, t1b[:, :], t2b[:, :]).then_inc(S["vrope"], 1)
                                else:
                                    for pr in range(8):
                                        for c4 in range(4):
                                            v.wait_ge(S["ttr"], b["ttr"] + pr * 4 + c4 + 1)
                                            stile = isl * 4 + c4
                                            hA, hB = 2 * pr, 2 * pr + 1
                                            v.tensor_copy(vp_sb[:, stile, hA * 65:hA * 65 + 64],
                                                          tps[c4 % 2][:, 0:64]).then_inc(S["vcp"], 1)
                                            v.tensor_copy(vp_sb[:, stile, hB * 65:hB * 65 + 64],
                                                          tps[c4 % 2][:, 64:128]).then_inc(S["vcp"], 1)

                            C["mm"] += 8
                            C["sev"] += 8
                            if phase in ("q", "k"):
                                C["tswp"] += 8
                                C["vrope"] += 8
                            else:
                                C["ttr"] += 32
                                C["vcp"] += 64
                        if phase == "q":
                            with nc.Block() as blk:
                                @blk.sync
                                def _(sync):
                                    kb.dma_ff(sync, q_v[:, :, tsl], qtmp[:, :, :])

        with ExitStack() as st_o:
            wout_sb = st_o.enter_context(nc.sbuf_tensor(kb.name("wout"), [128, 8, 1024], BF16))
            q_sl2 = [st_o.enter_context(nc.sbuf_tensor(kb.name("q_sl"), [128, 8, NSL], BF16))
                     for _ in range(2)]
            b_ldq0 = {"ldq": C["ldq"], "ldqb": C["ldqb"]}
            with nc.Block() as blk:
                @blk.sync
                def _(sync):
                    kb.ffw(sync)
                    sync.dma_start(wout_sb[:, :, :], obv).then_inc(S["ldw"], 16)
                    C["ldw"] += 16
            for isl in range(nslabs):
                tsl = slice(isl * NSL, (isl + 1) * NSL)
                jmax = 4 * (isl + 1)
                ndiag = min(4, jmax)
                jd0 = jmax - 4
                q_sl = q_sl2[isl % 2]
                with ExitStack() as st_sl:
                    x_sl = st_sl.enter_context(nc.sbuf_tensor(kb.name("x_slB"), [128, 8, NSL], F32))
                    out_sl = st_sl.enter_context(nc.sbuf_tensor(kb.name("out_sl"), [128, 8, NSL], BF16))
                    pt = st_sl.enter_context(nc.sbuf_tensor(kb.name("pt"), [128, 4, NSL], BF16))
                    rbsb = st_sl.enter_context(nc.sbuf_tensor(kb.name("rbsb"), [64, 2, NSL], F32))
                    rcps = st_sl.enter_context(nc.sbuf_tensor(kb.name("rcps"), [1, 2, NSL], F32))
                    with ExitStack() as st_ps, nc.Block() as blk:
                        spsb = [st_ps.enter_context(nc.psum_tensor(kb.name("sps"), [128, 2, NSL], F32))
                                for _ in range(2)]
                        oA = st_ps.enter_context(nc.psum_tensor(kb.name("oA"), [65, NSL], F32))
                        oB = st_ps.enter_context(nc.psum_tensor(kb.name("oB"), [65, NSL], F32))
                        rbA = st_ps.enter_context(nc.psum_tensor(kb.name("rbA"), [64, NSL], F32))
                        rbB = st_ps.enter_context(nc.psum_tensor(kb.name("rbB"), [64, NSL], F32))
                        b = {n: C[n] for n in ("smm", "sexp", "vmask", "tpv",
                                               "vrcp", "trb", "vfin", "ldq", "ldx")}

                        @blk.sync
                        def _(sync):
                            kb.ffw(sync)
                            for lsl in ([isl, isl + 1] if isl == 0 else [isl + 1]):
                                if lsl >= nslabs:
                                    continue
                                lts = slice(lsl * NSL, (lsl + 1) * NSL)
                                qs = "ldq" if lsl % 2 == 0 else "ldqb"
                                sync.dma_start(q_sl2[lsl % 2][:, :, :],
                                               q_v[:, :, lts]).then_inc(S[qs], 16)
                                C[qs] += 16
                            sync.wait_ge(kb.store_sem, kb.store_cnt)
                            sync.dma_start(x_sl[:, :, :], xin_v[:, :, tsl]).then_inc(S["ldx"], 16)
                        C["ldx"] += 16

                        def pt_ap(j, half):
                            return pt[:, (j % 2) * 2 + half, :]

                        def emit_pv(t, pr, j):
                            hA, hB = 2 * pr, 2 * pr + 1
                            jj = pr * jmax + j
                            if j == 0 and pr >= 1:
                                t.wait_ge(S["vfin"], b["vfin"] + pr)
                            if j >= jd0:
                                t.wait_ge(S["vmask"], b["vmask"] + pr * ndiag + (j - jd0) + 1)
                            else:
                                t.wait_ge(S["sexp"], b["sexp"] + jj + 1)
                            t.matmul(oA[:, :], vp_sb[:, j, hA * 65:(hA + 1) * 65],
                                     pt_ap(j, 0), start=(j == 0), stop=(j == jmax - 1))
                            t.matmul(oB[:, :], vp_sb[:, j, hB * 65:(hB + 1) * 65],
                                     pt_ap(j, 1), start=(j == 0),
                                     stop=(j == jmax - 1)).then_inc(S["tpv"], 1)

                        def emit_rb(t, p):
                            t.wait_ge(S["vrcp"], b["vrcp"] + p + 1)
                            t.matmul(rbA[:, :], onesrow[0:1, 0:64], rcps[0:1, 0, :],
                                     start=True, stop=True)
                            t.matmul(rbB[:, :], onesrow[0:1, 0:64], rcps[0:1, 1, :],
                                     start=True, stop=True).then_inc(S["trb"], 1)

                        @blk.tensor
                        def _(t):
                            qs = "ldq" if isl % 2 == 0 else "ldqb"
                            t.wait_ge(S[qs], b_ldq0[qs] + 16 * (isl // 2 + 1))
                            for pr in range(8):
                                for j in range(jmax):
                                    jj = pr * jmax + j
                                    if jj >= 2:
                                        t.wait_ge(S["sexp"], b["sexp"] + jj - 1)
                                    t.matmul(spsb[j % 2][:, 0, :],
                                             k_sb[0:64, pr, j * 128:(j + 1) * 128],
                                             q_sl[0:64, pr, :],
                                             start=True, stop=True,
                                             tile_position=(0, 0))
                                    t.matmul(spsb[j % 2][:, 1, :],
                                             k_sb[64:128, pr, j * 128:(j + 1) * 128],
                                             q_sl[64:128, pr, :],
                                             start=True, stop=True,
                                             tile_position=(64, 0)).then_inc(S["smm"], 1)
                                    if j == 1 and pr >= 1:
                                        emit_rb(t, pr - 1)
                                    if j >= 1:
                                        emit_pv(t, pr, j - 1)
                                emit_pv(t, pr, jmax - 1)
                            emit_rb(t, 7)

                        @blk.scalar
                        def _(s):
                            for pr in range(8):
                                for j in range(jmax):
                                    jj = pr * jmax + j
                                    s.wait_ge(S["smm"], b["smm"] + jj + 1)
                                    if jj >= 2:
                                        s.wait_ge(S["tpv"], b["tpv"] + jj - 1)
                                    s.activation(pt[:, (j % 2) * 2:(j % 2) * 2 + 2, :],
                                                 spsb[j % 2][:, :, :],
                                                 AF.Exp, bias=kb.fc(-24.0)).then_inc(S["sexp"], 1)

                        @blk.vector
                        def _(v):
                            for pr in range(8):
                                for jd in range(ndiag):
                                    j = jd0 + jd
                                    jj = pr * jmax + j
                                    v.wait_ge(S["sexp"], b["sexp"] + jj + 1)
                                    off = 384 - 128 * jd
                                    v.tensor_mul(pt_ap(j, 0), pt_ap(j, 0),
                                                 dmask[:, off:off + NSL])
                                    v.tensor_mul(pt_ap(j, 1), pt_ap(j, 1),
                                                 dmask[:, off:off + NSL]).then_inc(S["vmask"], 1)
                                v.wait_ge(S["tpv"], b["tpv"] + (pr + 1) * jmax)
                                kb.step("v", v, v.reciprocal(rcps[0:1, 0, :], oA[64:65, :]))
                                kb.step("v", v,
                                        v.reciprocal(rcps[0:1, 1, :], oB[64:65, :]))
                                v.engine_nop().then_inc(S["vrcp"], 1)
                                v.wait_ge(S["trb"], b["trb"] + pr + 1)
                                kb.step("v", v, v.tensor_copy(rbsb[:, 0, :], rbA[:, :]))
                                kb.step("v", v, v.tensor_copy(rbsb[:, 1, :], rbB[:, :]))
                                v.tensor_mul(out_sl[0:64, pr, :], oA[0:64, :], rbsb[:, 0, :])
                                v.tensor_mul(out_sl[64:128, pr, :], oB[0:64, :],
                                             rbsb[:, 1, :]).then_inc(S["vfin"], 1)

                        C["smm"] += 8 * jmax
                        C["sexp"] += 8 * jmax
                        C["vmask"] += 8 * ndiag
                        C["tpv"] += 8 * jmax
                        C["vrcp"] += 8
                        C["trb"] += 8
                        C["vfin"] += 8
                    with ExitStack() as st_ps, nc.Block() as blk:
                        ops = [st_ps.enter_context(nc.psum_tensor(kb.name("opj"), [128, NSL], F32))
                               for _ in range(2)]
                        b_mmo, b_evo = C["mmo"], C["evo"]
                        ldw_o = C["ldw"]

                        @blk.tensor
                        def _(t):
                            t.wait_ge(S["ldw"], ldw_o)
                            for dout in range(8):
                                if dout >= 2:
                                    t.wait_ge(S["evo"], b_evo + dout - 1)
                                for k in range(8):
                                    ins = t.matmul(ops[dout % 2][:, :],
                                                   wout_sb[:, k, dout * 128:(dout + 1) * 128],
                                                   out_sl[:, k, :],
                                                   start=(k == 0), stop=(k == 7))
                                ins.then_inc(S["mmo"], 1)

                        @blk.vector
                        def _(v):
                            v.wait_ge(S["ldx"], b["ldx"] + 16)
                            for dout in range(8):
                                v.wait_ge(S["mmo"], b_mmo + dout + 1)
                                v.tensor_add(x_sl[:, dout, :], x_sl[:, dout, :],
                                             ops[dout % 2][:, :]).then_inc(S["evo"], 1)
                        C["mmo"] += 8
                        C["evo"] += 8
                    with nc.Block() as blk:
                        @blk.sync
                        def _(sync):
                            kb.store(sync, xout_v[:, :, tsl], x_sl[:, :, :])


def build_attn_nc():
    kb = KB()
    x_in = kb.din("x", [D, T], F32)
    wqkv = kb.din("wqkv", [1024, 3072], F32)
    wout = kb.din("wout", [1024, 1024], F32)
    rope = kb.din("c_rope", [4, 128, T], F32)
    y = kb.dout("y", [D, T], F32)
    xres = kb.dscr("xres", [D, T], F32)
    wqkv_bf = kb.dscr("wqkvbf", [1024, 3072], BF16)
    wout_bf = kb.dscr("woutbf", [1024, 1024], BF16)
    rope_scr = kb.dscr("ropescr", [4, 128, T], F32)
    xn_scr = kb.dscr("xnscr", [D, T], BF16)
    q_scr = kb.dscr("qscr", [8, 128, T], BF16)
    kb.init()
    emit_xin_copy(kb, x_in, xres)
    emit_attn_preconvert(kb, wqkv, wout, rope, wqkv_bf, wout_bf, rope_scr)
    asx = AttnSems(kb)
    emit_attn(kb, asx, xres, y, wqkv_bf, wout_bf, rope_scr, xn_scr, q_scr)
    nc = kb.finalize()
    return nc, {"c_fvals": kb.fvals_array(), "c_rope": rope_tables()}


def emit_ssm_preconvert(kb, win_hbm, wout_hbm, nw_hbm, win_bf, wout2_bf):
    nc = kb.nc
    wiv = win_hbm.rearrange("(c p) m -> p c m", p=128)
    wibv = win_bf.rearrange("(c p) m -> p c m", p=128)
    with ExitStack() as st:
        buf = st.enter_context(nc.sbuf_tensor(kb.name("pcs"), [128, 4256], BF16))
        fb = st.enter_context(nc.sbuf_tensor(kb.name("pcsf"), [128, 1024], F32))
        nwc = st.enter_context(nc.sbuf_tensor(kb.name("nwc"), [128, 1], F32))
        ob = st.enter_context(nc.sbuf_tensor(kb.name("pcso"), [128, 1024], BF16))
        ld = kb.sem("pcs_ld")
        stc = kb.sem("pcs_st")
        nld = nst = 0
        for c in range(8):
            with nc.Block() as blk:
                @blk.gpsimd
                def _(g):
                    nonlocal nld
                    g.dma_start(buf[:, :], wiv[:, c, :]).then_inc(ld, 16)
                    nld += 16
                    g.wait_ge(ld, nld)
            with nc.Block() as blk:
                @blk.sync
                def _(sync):
                    nonlocal nst
                    sync.dma_start(wibv[:, c, :], buf[:, :]).then_inc(stc, 16)
                    nst += 16
                    sync.wait_ge(stc, nst)
        for c in range(16):
            with nc.Block() as blk:
                @blk.gpsimd
                def _(g):
                    nonlocal nld
                    g.dma_start(fb[:, :], wout_hbm[c * 128:(c + 1) * 128, :]).then_inc(ld, 16)
                    g.dma_start(nwc[:, :], nw_hbm[c * 128:(c + 1) * 128, :]).then_inc(ld, 16)
                    nld += 32
                    g.wait_ge(ld, nld)
            with nc.Block() as blk:
                @blk.vector
                def _(v):
                    v.tensor_scalar(ob[:, :], fb[:, :], nwc[:, :], None, ALU.mult)
            with nc.Block() as blk:
                @blk.sync
                def _(sync):
                    nonlocal nst
                    sync.dma_start(wout2_bf[c * 128:(c + 1) * 128, :], ob[:, :]).then_inc(stc, 16)
                    nst += 16
                    sync.wait_ge(stc, nst)


class SSMSems:
    def __init__(self, kb):
        names = ["ldx", "ldxn", "ldw", "ldbc", "ldbcb", "ldxs", "ldz", "ldf", "p1mm",
                 "p1ev", "tmsk", "tg", "sl", "slh", "vlu", "vma", "vsd", "typ", "vmg",
                 "tsw", "tsa", "vst", "ttr", "sfe", "mmo", "evo", "gat"]
        self.s = {n: kb.sem("ss_" + n) for n in names}
        self.c = {n: 0 for n in names}


def emit_ssm(kb, sx, x_in_hbm, x_out_hbm, win_bf, wout2_bf, dtb_hbm, alog_hbm,
             dsk_hbm, xn_scr, z_scr, xs_scr, bcd_scr, psf_scr, nch=32, php=9):
    nc = kb.nc
    S, C = sx.s, sx.c
    NSL = 512
    nslabs = T // NSL
    NCH = 32
    ident = kb.const_sb["c_ident_f32"]
    ident_bf = kb.const_sb["c_ident_bf"]
    onesrow = kb.const_sb["c_ones_row"]
    ut01 = kb.const_sb["c_ut01"]
    sl01 = kb.const_sb["c_sl01"]
    invfrep = kb.const_sb["c_invfrep"]
    slbig = kb.const_sb["c_slbig"]
    invf16 = kb.const_sb["c_invf16"]
    swpb = kb.const_sb["c_swpb"]
    xin_v = xview(x_in_hbm)
    xout_v = xview(x_out_hbm)
    xn_v = xn_scr.rearrange("(c p) t -> p c t", p=128)
    wibv = win_bf.rearrange("(c p) m -> p c m", p=128)
    wo2v = wout2_bf.rearrange("(c p) m -> p c m", p=128)
    z_v = z_scr.rearrange("(sl p) m -> p sl m", p=128)
    xs_v = xs_scr.rearrange("(sl p) m -> p sl m", p=128)
    bcd_v = bcd_scr.rearrange("(sl p) m -> p sl m", p=128)
    MAGIC = 12582912.0
    TWO_PI = float(2.0 * np.pi)
    HALF_PI = float(np.pi / 2.0)

    with ExitStack() as st_x2:
        x_sl2 = [st_x2.enter_context(nc.sbuf_tensor(kb.name("x_sl"), [128, 8, NSL], F32))
                 for _ in range(2)]
        base_ld = {"ld": kb.lncnt["ld"], "ld2": kb.lncnt["ld2"]}
        lpn = lambda s: "ld" if s % 2 == 0 else "ld2"
        for isl in range(nslabs):
            tsl = slice(isl * NSL, (isl + 1) * NSL)
            with ExitStack() as st:
                xn = st.enter_context(nc.sbuf_tensor(kb.name("xn"), [128, 8, NSL], BF16))
                scr = st.enter_context(nc.sbuf_tensor(kb.name("scr"), [128, 8, NSL], BF16))
                pf = []
                if isl + 1 < nslabs:
                    nts = slice((isl + 1) * NSL, (isl + 2) * NSL)
                    pf.append((x_sl2[(isl + 1) % 2][:, :, :], xin_v[:, :, nts],
                               lpn(isl + 1)))
                if isl == 0:
                    emit_ln_slab(kb, st, x_sl2[0], xn, scr, NSL,
                                 xsrc=xin_v[:, :, tsl], prefetch=pf)
                else:
                    emit_ln_slab(kb, st, x_sl2[isl % 2], xn, scr, NSL, prefetch=pf,
                                 ld_wait=(lpn(isl),
                                          base_ld[lpn(isl)] + 16 * (isl // 2 + 1)))
                with nc.Block() as blk:
                    @blk.sync
                    def _(sync):
                        kb.dma_ff(sync, xn_v[:, :, tsl], xn[:, :, :])

    with ExitStack() as st_w:
        wi = st_w.enter_context(nc.sbuf_tensor(kb.name("wi"), [128, 8, 4256], BF16))
        with nc.Block() as blk:
            @blk.sync
            def _(sync):
                kb.ffw(sync)
                sync.dma_start(wi[:, :, :], wibv).then_inc(S["ldw"], 16)
                C["ldw"] += 16
        for isl in range(nslabs):
            tsl = slice(isl * NSL, (isl + 1) * NSL)
            with ExitStack() as st_sl:
                xn = st_sl.enter_context(nc.sbuf_tensor(kb.name("xns"), [128, 8, NSL], BF16))
                z_sl = st_sl.enter_context(nc.sbuf_tensor(kb.name("z_sl"), [128, 4, 2048], BF16))
                xs_sl = st_sl.enter_context(nc.sbuf_tensor(kb.name("xs_sl"), [128, 4, 2048], BF16))
                bc_sl = st_sl.enter_context(nc.sbuf_tensor(kb.name("bc_sl"), [128, 4, 160], F32))
                with ExitStack() as st_ps, nc.Block() as blk:
                    pps = [st_ps.enter_context(nc.psum_tensor(kb.name("pps"), [128, 512], F32))
                           for _ in range(4)]
                    b_mm, b_ev, b_ldxn = C["p1mm"], C["p1ev"], C["ldxn"]
                    ldw_now = C["ldw"]

                    @blk.sync
                    def _(sync):
                        kb.ffw(sync)
                        sync.dma_start(xn[:, :, :], xn_v[:, :, tsl]).then_inc(S["ldxn"], 16)
                    C["ldxn"] += 16

                    @blk.tensor
                    def _(t):
                        t.wait_ge(S["ldw"], ldw_now)
                        t.wait_ge(S["ldxn"], b_ldxn + 16)
                        g = 0
                        for tg in range(4):
                            for cc in range(9):
                                ncols = 512 if cc < 8 else 160
                                if g >= 4:
                                    t.wait_ge(S["p1ev"], b_ev + g - 3)
                                for k in range(8):
                                    ins = t.matmul(
                                        pps[g % 4][:, 0:ncols],
                                        xn[:, k, tg * 128:(tg + 1) * 128],
                                        wi[:, k, cc * 512:cc * 512 + ncols],
                                        start=(k == 0), stop=(k == 7))
                                ins.then_inc(S["p1mm"], 1)
                                g += 1

                    @blk.scalar
                    def _(s):
                        kb.ffw(s)
                        g = 0
                        for tg in range(4):
                            for cc in range(9):
                                s.wait_ge(S["p1mm"], b_mm + g + 1)
                                p = pps[g % 4]
                                if cc < 4:
                                    s.activation(z_sl[:, tg, cc * 512:(cc + 1) * 512],
                                                 p[:, 0:512], AF.Copy).then_inc(S["p1ev"], 1)
                                elif cc < 8:
                                    s.activation(xs_sl[:, tg, (cc - 4) * 512:(cc - 3) * 512],
                                                 p[:, 0:512], AF.Copy).then_inc(S["p1ev"], 1)
                                else:
                                    for base in (0, 64):
                                        inr = p[:, base:base + 32].rearrange(
                                            "p (i two) -> p two i", two=2)
                                        outr = bc_sl[:, tg, base:base + 32].rearrange(
                                            "p (two i) -> p two i", two=2)
                                        s.activation(outr, inr, AF.Copy)
                                        s.activation(bc_sl[:, tg, base + 32:base + 64],
                                                     p[:, base + 32:base + 64], AF.Copy)
                                    s.activation(bc_sl[:, tg, 128:160], p[:, 128:160],
                                                 AF.Copy).then_inc(S["p1ev"], 1)
                                g += 1
                    C["p1mm"] += 36
                    C["p1ev"] += 36
                with nc.Block() as blk:
                    @blk.sync
                    def _(sync):
                        kb.dma_ff(sync, z_v[:, isl * 4:(isl + 1) * 4, :], z_sl[:, :, :])
                        kb.dma_ff(sync, xs_v[:, isl * 4:(isl + 1) * 4, :], xs_sl[:, :, :])
                        kb.dma_ff(sync, bcd_v[:, isl * 4:(isl + 1) * 4, :], bc_sl[:, :, :])

    with ExitStack() as st_p2:
        sb = lambda nm, shp, dt, side="left": st_p2.enter_context(
            nc.sbuf_tensor(kb.name(nm), shp, dt, side=side))
        state = sb("state", [64, 2048], F32)
        state_bf = sb("statebf", [128, 2048], BF16)
        wo2 = sb("wo2", [128, 16, 1024], BF16, side="right")
        dtb_b = sb("dtb_b", [128, 32], F32)
        a_b = sb("a_b", [128, 32], F32)
        DI = sb("DI", [128, 32, 128], BF16, side="right")
        rows = sb("rows", [1, 128], F32)
        y_stage = sb("ystage", [128, 4, 2048], F32, side="right")
        bcd2 = [sb("bcd", [128, 160], F32) for _ in range(2)]
        xs_c = sb("xs_c", [128, 2048], BF16)
        dt_t = sb("dt_t", [128, 32], F32)
        tA = sb("tA", [128, 32], F32)
        tB = sb("tB", [128, 32], F32)
        psi = sb("psi", [128, 32], F32)
        psihat = sb("psihat", [128, 32], F32)
        lam = sb("lam", [128, 32], F32)
        bdec = sb("bdec", [128, 32], F32)
        phi = sb("phi", [128, 32], F32)
        psf = sb("psf", [32, 128], F32)
        psrow = sb("psrow", [1, 2048], F32)
        drow = sb("drow", [1, 64], F32)
        bt_f = sb("bt_f", [128, 2048], F32)
        ct_f = sb("ct_f", [128, 2048], F32)
        bt_bf = sb("bt_bf", [128, 2048], BF16)
        ct_bf = sb("ct_bf", [128, 2048], BF16)
        bhat = sb("bhat", [128, 2048], BF16)
        btf_feat = sb("btff", [128, 16, 128], BF16)
        ctf_feat = sb("ctff", [128, 16, 128], BF16)
        lm = sb("lm", [128, 2, 128], F32)
        lmu = sb("lmu", [128, 2, 128], F32)
        sgf = sb("sgf", [128, 2, 128], F32)
        spp = sb("spp", [128, 32, 128], BF16)
        tp16 = sb("tp16", [32, 128], F32)
        ccol_t = sb("ccol_t", [64, 32], F32)
        scol_t = sb("scol_t", [64, 32], F32)
        dec64 = sb("dec64", [64, 32], F32)
        u_bf = sb("u_bf", [64, 2048], BF16)
        y_mg = ct_f
        gsq = bt_f
        z_g = sb("z_g", [128, 4, 2048], BF16, side="right")
        x_g = sb("x_g", [128, 8, NSL], F32, side="right")

        def rep_ap(tile, pcount, pstride, nmid, last_count):
            return bass.AP(tile[:, :].tensor, 0,
                           [[pstride, pcount], [1, nmid], [0, last_count]])

        with nc.Block() as blk:
            @blk.sync
            def _(sync):
                kb.ffw(sync)
                sync.dma_start(rows[0:1, 0:32], dtb_hbm).then_inc(S["ldf"], 16)
                sync.dma_start(rows[0:1, 32:64], alog_hbm).then_inc(S["ldf"], 16)
                sync.dma_start(rows[0:1, 64:96], dsk_hbm).then_inc(S["ldf"], 16)
                sync.dma_start(wo2[:, :, :], wo2v).then_inc(S["ldw"], 16)
                C["ldf"] += 48
                C["ldw"] += 16

            @blk.vector
            def _(v):
                v.memset(state[:, :], 0.0)
                v.memset(state_bf[:, :], 0.0)
        ldw_wo2 = C["ldw"]
        with nc.Block() as blk:
            @blk.scalar
            def _(s):
                s.wait_ge(S["ldf"], C["ldf"])
                s.activation(rows[0:1, 32:64], rows[0:1, 32:64], AF.Exp)
        with ExitStack() as st_ps:
            pb = st_ps.enter_context(nc.psum_tensor(kb.name("pb"), [128, 96], F32))
            with nc.Block() as blk:
                @blk.tensor
                def _(t):
                    t.matmul(pb[:, :], onesrow[:, :], rows[0:1, 0:96], start=True, stop=True)
            with nc.Block() as blk:
                @blk.vector
                def _(v):
                    v.tensor_copy(dtb_b[:, :], pb[:, 0:32])
                    v.tensor_copy(a_b[:, :], pb[:, 32:64])
                    for h in range(32):
                        v.tensor_scalar(DI[:, h, :], ident_bf[:, :], pb[:, 64 + h:65 + h],
                                        None, ALU.mult)

        b_bc0 = {"ldbc": C["ldbc"], "ldbcb": C["ldbcb"]}
        for ch in range(nch):
            zg = ch // 4
            cg = ch % 4
            bcd = bcd2[ch % 2]
            with ExitStack() as st_ps:
                pphi = st_ps.enter_context(nc.psum_tensor(kb.name("pphi"), [128, 32], F32))
                ppsi = st_ps.enter_context(nc.psum_tensor(kb.name("ppsi"), [128, 32], F32))
                pprv = st_ps.enter_context(nc.psum_tensor(kb.name("pprv"), [128, 32], F32))
                ppft = st_ps.enter_context(nc.psum_tensor(kb.name("ppft"), [32, 128], F32))
                pft2 = st_ps.enter_context(nc.psum_tensor(kb.name("pft2"), [32, 128], F32))
                pcc = st_ps.enter_context(nc.psum_tensor(kb.name("pcc"), [128, 32], F32))
                pdd = st_ps.enter_context(nc.psum_tensor(kb.name("pdd"), [64, 32], F32))
                bv0, bs0 = kb.chcnt["v"], kb.chcnt["s"]
                b_t2 = C["typ"]
                b_ff = kb.ff_cnt

                with nc.Block() as blk:
                    @blk.sync
                    def _(sync):
                        kb.ffw(sync)
                        for lch in ([ch, ch + 1] if ch == 0 else [ch + 1]):
                            if lch < nch:
                                bs_ = "ldbc" if lch % 2 == 0 else "ldbcb"
                                sync.dma_start(bcd2[lch % 2][:, :],
                                               bcd_v[:, lch, :]).then_inc(S[bs_], 16)
                                C[bs_] += 16
                        sync.dma_start(xs_c[:, :], xs_v[:, ch, :]).then_inc(S["ldxs"], 16)
                        C["ldxs"] += 16
                        if cg == 0:
                            sync.dma_start(z_g[:, :, :], z_v[:, zg * 4:(zg + 1) * 4, :]
                                           ).then_inc(S["ldz"], 16)
                            sync.wait_ge(kb.store_sem, kb.store_cnt)
                            sync.dma_start(x_g[:, :, :],
                                           xin_v[:, :, zg * 512:(zg + 1) * 512]
                                           ).then_inc(S["ldz"], 16)
                            C["ldz"] += 32
                        sync.wait_ge(kb.chsem["v"], bv0 + 10)
                        kb.dma_ff(sync, psf_scr[0:32, :], psf[:, :])
                        sync.wait_ge(kb.ff_sem, kb.ff_cnt)
                        psfl = psf_scr.rearrange("a b -> (a b)").rearrange(
                            "(o f) -> o f", o=1)
                        sync.dma_start(psrow[0:1, :], psfl[0:1, 0:2048]
                                       ).then_inc(S["slh"], 16)
                    tot_ldz = C["ldz"]
                    C["slh"] += 16

                    @blk.vector
                    def _(v):
                        bs_ = "ldbc" if ch % 2 == 0 else "ldbcb"
                        v.wait_ge(S[bs_], b_bc0[bs_] + 16 * (ch // 2 + 1))
                        if cg == 0:
                            v.wait_ge(S["ldz"], tot_ldz)
                        kb.step("v", v, v.tensor_add(tA[:, :], bcd[:, 128:160], dtb_b[:, :]))
                        v.wait_ge(kb.chsem["s"], bs0 + 3)
                        kb.step("v", v, v.tensor_scalar_max(tA[:, :], tA[:, :], 0.0))
                        kb.step("v", v, v.tensor_add(dt_t[:, :], tA[:, :], tB[:, :]))
                        kb.step("v", v, v.tensor_mul(tA[:, :], dt_t[:, :], a_b[:, :]))
                        v.wait_ge(S["typ"], b_t2 + 1)
                        kb.step("v", v, v.tensor_copy(phi[:, :], pphi[:, :]))
                        kb.step("v", v, v.tensor_copy(psi[:, :], ppsi[:, :]))
                        v.wait_ge(kb.chsem["s"], bs0 + 4)
                        kb.step("v", v, v.tensor_add(psihat[:, :], ppsi[:, :], tB[:, :]))
                        kb.step("v", v, v.tensor_sub(bdec[:, :], tB[:, :], pprv[:, :]))
                        kb.step("v", v, v.tensor_add(drow[0:1, 0:32], pprv[0:1, :], tA[0:1, :]))
                        v.wait_ge(S["typ"], b_t2 + 2)
                        v.wait_ge(kb.ff_sem, b_ff)
                        kb.step("v", v, v.tensor_copy(psf[:, :], ppft[0:32, :]))
                        kb.step("v", v, v.tensor_scalar(tp16[:, 0:16], invf16[:, :],
                                                        pft2[0:32, 127:128], None, ALU.mult))
                        kb.step("v", v, v.tensor_scalar_mul(tp16[:, 96:112], tp16[:, 0:16],
                                                            kb.fc(1.0 / TWO_PI)))
                        kb.step("v", v, v.tensor_scalar_add(tp16[:, 96:112], tp16[:, 96:112],
                                                            kb.fc(MAGIC)))
                        kb.step("v", v, v.tensor_scalar_sub(tp16[:, 96:112], tp16[:, 96:112],
                                                            kb.fc(MAGIC)))
                        kb.step("v", v, v.scalar_tensor_tensor(tp16[:, 0:16], tp16[:, 96:112],
                                                               kb.fc(-TWO_PI), tp16[:, 0:16],
                                                               ALU.mult, ALU.add))
                        v.wait_ge(kb.chsem["s"], bs0 + 10)
                        kb.step("v", v, v.tensor_copy(tp16[:, 16:32], tp16[:, 0:16]))
                        kb.step("v", v, v.tensor_copy(tp16[:, 80:96], tp16[:, 64:80]))
                        kb.step("v", v, v.memset(tp16[:, 32:64], 1.0))
                        kb.step("v", v, v.memset(tp16[:, 96:128], 0.0))
                        v.wait_ge(S["typ"], b_t2 + 3)
                        kb.step("v", v, v.tensor_copy(ccol_t[:, :], pcc[0:64, :]))
                        kb.step("v", v, v.tensor_copy(scol_t[:, :], pcc[64:128, :]))
                        kb.step("v", v, v.tensor_copy(dec64[:, :], pdd[:, :]))

                    @blk.scalar
                    def _(s):
                        s.wait_ge(kb.chsem["v"], bv0 + 1)
                        kb.step("s", s, s.activation(tB[:, :], tA[:, :], AF.Abs))
                        kb.step("s", s, s.activation(tB[:, :], tB[:, :], AF.Exp, scale=-1.0))
                        kb.step("s", s, s.activation(tB[:, :], tB[:, :], AF.Ln, bias=1.0))
                        s.wait_ge(kb.chsem["v"], bv0 + 3)
                        kb.step("s", s, s.activation(tB[:, :], dt_t[:, :], AF.Ln))
                        s.wait_ge(kb.chsem["v"], bv0 + 8)
                        kb.step("s", s, s.activation(bdec[:, :], bdec[:, :], AF.Exp))
                        kb.step("s", s, s.activation(lam[:, :], psi[:, :], AF.Exp, scale=-1.0))
                        s.wait_ge(kb.chsem["v"], bv0 + 9)
                        kb.step("s", s, s.activation(drow[0:1, 0:32], drow[0:1, 0:32],
                                                     AF.Exp, scale=-1.0))
                        s.wait_ge(kb.chsem["v"], bv0 + 15)
                        kb.step("s", s, s.activation(tp16[:, 64:80], tp16[:, 0:16], AF.Sin))
                        kb.step("s", s, s.activation(tp16[:, 96:112], tp16[:, 0:16], AF.Abs))
                        kb.step("s", s, s.activation(tp16[:, 0:16], tp16[:, 96:112], AF.Sin,
                                                     scale=-1.0, bias=kb.fc(HALF_PI)))

                    @blk.tensor
                    def _(t):
                        t.wait_ge(kb.chsem["v"], bv0 + 4)
                        t.matmul(pphi[:, :], ut01[:, :], dt_t[:, :], start=True, stop=True)
                        t.matmul(ppsi[:, :], ut01[:, :], tA[:, :], start=True, stop=True)
                        t.matmul(pprv[:, :], sl01[:, :], tA[:, :],
                                 start=True, stop=True).then_inc(S["typ"], 1)
                        t.wait_ge(kb.chsem["v"], bv0 + 6)
                        t.transpose(ppft[:, :], psi[:, :], ident[:, :])
                        t.transpose(pft2[:, :], phi[:, :],
                                    ident[:, :]).then_inc(S["typ"], 1)
                        t.wait_ge(kb.chsem["v"], bv0 + 19)
                        t.wait_ge(kb.chsem["s"], bs0 + 7)
                        t.transpose(pcc[:, :], tp16[:, :], ident[0:32, 0:32])
                        t.matmul(pdd[:, :], onesrow[0:1, 0:64], drow[0:1, 0:32],
                                 start=True, stop=True).then_inc(S["typ"], 1)
                    C["typ"] += 3

            if php <= 1:
                continue
            st_ph3 = ExitStack()
            angs = st_ph3.enter_context(nc.sbuf_tensor(kb.name("angs"), [128, 512], F32))
            ca = st_ph3.enter_context(nc.sbuf_tensor(kb.name("ca"), [128, 512], F32))
            sa2 = st_ph3.enter_context(nc.sbuf_tensor(kb.name("sa2"), [128, 512], F32))
            vt1 = st_ph3.enter_context(nc.sbuf_tensor(kb.name("vt1"), [128, 512], F32))
            with nc.Block() as blk:
                @blk.vector
                def _(v):
                    ph_b = rep_ap(phi, 128, 32, 32, 16)
                    kb.step("v", v, v.tensor_mul(angs[:, :], ph_b, invfrep[:, :]))
                    kb.step("v", v, v.tensor_scalar_mul(vt1[:, :], angs[:, :],
                                                        kb.fc(1.0 / TWO_PI)))
                    kb.step("v", v, v.tensor_scalar_add(vt1[:, :], vt1[:, :], kb.fc(MAGIC)))
                    kb.step("v", v, v.tensor_scalar_sub(vt1[:, :], vt1[:, :], kb.fc(MAGIC)))
                    kb.step("v", v, v.scalar_tensor_tensor(angs[:, :], vt1[:, :],
                                                           kb.fc(-TWO_PI), angs[:, :],
                                                           ALU.mult, ALU.add))

                @blk.scalar
                def _(s):
                    s.wait_ge(kb.chsem["v"], kb.chcnt["v"])
                    kb.step("s", s, s.activation(sa2[:, :], angs[:, :], AF.Sin))
                    kb.step("s", s, s.activation(ca[:, :], angs[:, :], AF.Abs))
                    kb.step("s", s, s.activation(ca[:, :], ca[:, :], AF.Sin,
                                                 scale=-1.0, bias=kb.fc(HALF_PI)))
            with nc.Block() as blk:
                @blk.vector
                def _(v):
                    th = bcd[:, :].tensor
                    for base, dstf in ((0, bt_f), (64, ct_f)):
                        Be = bass.AP(th, base, [[160, 128], [0, 32], [1, 16]])
                        Bo = bass.AP(th, base + 16, [[160, 128], [0, 32], [1, 16]])
                        Br = bass.AP(th, base + 32, [[160, 128], [0, 32], [1, 32]])
                        dv = dstf[:, :].rearrange("p (h n) -> p h n", n=64)
                        dste, dsto, dstr = dv[:, :, 0:16], dv[:, :, 16:32], dv[:, :, 32:64]
                        kb.step("v", v, v.tensor_mul(vt1[:, :], Be, ca[:, :]))
                        kb.step("v", v, v.tensor_mul(angs[:, :], Bo, sa2[:, :]))
                        kb.step("v", v, v.tensor_add(dste, vt1[:, :], angs[:, :]))
                        kb.step("v", v, v.tensor_mul(vt1[:, :], Bo, ca[:, :]))
                        kb.step("v", v, v.tensor_mul(angs[:, :], Be, sa2[:, :]))
                        kb.step("v", v, v.tensor_sub(dsto, vt1[:, :], angs[:, :]))
                        kb.step("v", v, v.tensor_copy(dstr, Br))
                    bd_rep = rep_ap(bdec, 128, 32, 32, 64)
                    kb.step("v", v, v.tensor_copy(bt_bf[:, :], bt_f[:, :]))
                    kb.step("v", v, v.tensor_copy(ct_bf[:, :], ct_f[:, :]))
                    kb.step("v", v, v.tensor_mul(
                        bhat[:, :].rearrange("p (h n) -> p h n", n=64),
                        bt_f[:, :].rearrange("p (h n) -> p h n", n=64), bd_rep))
            st_ph3.close()
            with ExitStack() as st_ps:
                ptr = [st_ps.enter_context(nc.psum_tensor(kb.name("ptr"), [128, 128], BF16))
                       for _ in range(6)]
                b_ttr, b_sfe = C["ttr"], C["sfe"]
                with nc.Block() as blk:
                    @blk.tensor
                    def _(t):
                        g = 0
                        for srcb, dstf in ((bt_bf, btf_feat), (ct_bf, ctf_feat)):
                            for pt in range(16):
                                if g >= 6:
                                    t.wait_ge(S["sfe"], b_sfe + g - 5)
                                t.transpose(ptr[g % 6][:, :],
                                            srcb[:, (g % 16) * 128:(g % 16 + 1) * 128],
                                            ident_bf[:, :]).then_inc(S["ttr"], 1)
                                g += 1

                    @blk.scalar
                    def _(s):
                        g = 0
                        for srcb, dstf in ((bt_bf, btf_feat), (ct_bf, ctf_feat)):
                            for pt in range(16):
                                s.wait_ge(S["ttr"], b_ttr + g + 1)
                                s.activation(dstf[:, pt, :], ptr[g % 6][:, :],
                                             AF.Copy).then_inc(S["sfe"], 1)
                                g += 1
                    C["ttr"] += 32
                    C["sfe"] += 32

            if php <= 2:
                continue
            with ExitStack() as st_ps:
                pm = [st_ps.enter_context(nc.psum_tensor(kb.name("pm"), [128, 128], F32))
                      for _ in range(2)]
                pg = [st_ps.enter_context(nc.psum_tensor(kb.name("pg"), [128, 128], F32))
                      for _ in range(2)]
                b_tmsk, b_tg, b_sl2, b_vlu = C["tmsk"], C["tg"], C["sl"], C["vlu"]
                b_slh, b_vma = C["slh"], C["vma"]
                with nc.Block() as blk:
                    @blk.sync
                    def _(sync):
                        sync.wait_ge(S["tmsk"], b_tmsk + 16)
                        psfl = psf_scr.rearrange("a b -> (a b)").rearrange(
                            "(o f) -> o f", o=1)
                        sync.dma_start(psrow[0:1, :], psfl[0:1, 2048:4096]
                                       ).then_inc(S["slh"], 16)
                    C["slh"] += 16

                    @blk.tensor
                    def _(t):
                        for h in range(32):
                            t.wait_ge(S["slh"], b_slh - 16 + (16 if h < 16 else 32))
                            if h >= 2:
                                t.wait_ge(S["vma"], b_vma + h - 1)
                            t.matmul(pm[h % 2][:, :], onesrow[:, :],
                                     psrow[0:1, (h % 16) * 128:(h % 16 + 1) * 128],
                                     start=True, stop=True).then_inc(S["tmsk"], 1)
                            if h >= 2:
                                t.wait_ge(S["vlu"], b_vlu + h - 1)
                            pt_i = h // 2
                            r0 = (h % 2) * 64
                            t.matmul(pg[h % 2][:, :],
                                     btf_feat[r0:r0 + 64, pt_i, :],
                                     ctf_feat[r0:r0 + 64, pt_i, :],
                                     start=True, stop=True).then_inc(S["tg"], 1)

                    @blk.scalar
                    def _(s):
                        for h in range(32):
                            s.wait_ge(S["vma"], b_vma + h + 1)
                            if h >= 2:
                                s.wait_ge(S["vlu"], b_vlu + h - 1)
                            s.activation(lm[:, h % 2, :], lmu[:, h % 2, :], AF.Exp,
                                         scale=-1.0, bias=psihat[:, h:h + 1]
                                         ).then_inc(S["sl"], 1)

                    @blk.vector
                    def _(v):
                        for h in range(33):
                            if h < 32:
                                v.wait_ge(S["tmsk"], b_tmsk + h + 1)
                                if h >= 2:
                                    v.wait_ge(S["sl"], b_sl2 + h - 1)
                                v.tensor_add(lmu[:, h % 2, :], pm[h % 2][:, :],
                                             slbig[:, :]).then_inc(S["vma"], 1)
                            if h >= 1:
                                hp = h - 1
                                v.wait_ge(S["sl"], b_sl2 + hp + 1)
                                v.wait_ge(S["tg"], b_tg + hp + 1)
                                kb.step("v", v, v.tensor_mul(sgf[:, hp % 2, :], pg[hp % 2][:, :],
                                                             lm[:, hp % 2, :]))
                                v.tensor_add(spp[:, hp, :], sgf[:, hp % 2, :],
                                             DI[:, hp, :]).then_inc(S["vlu"], 1)
                C["tmsk"] += 32
                C["tg"] += 32
                C["sl"] += 32
                C["vma"] += 32
                C["vlu"] += 32

            if php <= 3:
                continue
            with ExitStack() as st_ps:
                yin = st_ps.enter_context(nc.psum_tensor(kb.name("yin"), [128, 2048], F32))
                yint = st_ps.enter_context(nc.psum_tensor(kb.name("yint"), [128, 2048], F32))
                with nc.Block() as blk:
                    @blk.tensor
                    def _(t):
                        t.wait_ge(S["ldxs"], C["ldxs"])
                        for pt in range(16):
                            hA, hB = 2 * pt, 2 * pt + 1
                            t.matmul(yint[:, hA * 64:(hA + 1) * 64],
                                     ctf_feat[0:64, pt, :],
                                     state_bf[0:64, hA * 64:(hA + 1) * 64],
                                     start=True, stop=True, tile_position=(0, 0))
                            t.matmul(yin[:, hA * 64:(hA + 1) * 64],
                                     spp[:, hA, :], xs_c[:, hA * 64:(hA + 1) * 64],
                                     start=True, stop=True)
                            t.matmul(yint[:, hB * 64:(hB + 1) * 64],
                                     ctf_feat[64:128, pt, :],
                                     state_bf[64:128, hB * 64:(hB + 1) * 64],
                                     start=True, stop=True, tile_position=(64, 0))
                            t.matmul(yin[:, hB * 64:(hB + 1) * 64],
                                     spp[:, hB, :], xs_c[:, hB * 64:(hB + 1) * 64],
                                     start=True, stop=True)
                with nc.Block() as blk:
                    @blk.vector
                    def _(v):
                        lam_rep = rep_ap(lam, 128, 32, 32, 64)
                        kb.step("v", v, v.tensor_mul(
                            y_mg[:, :].rearrange("p (h n) -> p h n", n=64),
                            yint[:, :].rearrange("p (h n) -> p h n", n=64), lam_rep))
                        kb.step("v", v, v.tensor_add(y_stage[:, cg, :], y_mg[:, :],
                                                     yin[:, :]))
            if php <= 4:
                continue
            st_ph6 = ExitStack()
            u_all = st_ph6.enter_context(nc.sbuf_tensor(kb.name("u_all"), [64, 2048], F32))
            w1s = st_ph6.enter_context(nc.sbuf_tensor(kb.name("w1s"), [64, 2048], F32))
            with ExitStack() as st_ps:
                psa = st_ps.enter_context(nc.psum_tensor(kb.name("psa"), [64, 2048], F32))
                with nc.Block() as blk:
                    @blk.tensor
                    def _(t):
                        for h in range(32):
                            t.matmul(psa[:, h * 64:(h + 1) * 64],
                                     bhat[:, h * 64:(h + 1) * 64],
                                     xs_c[:, h * 64:(h + 1) * 64],
                                     start=True, stop=True)
                with nc.Block() as blk:
                    @blk.vector
                    def _(v):
                        dec_rep = rep_ap(dec64, 64, 32, 32, 64)
                        kb.step("v", v, v.tensor_mul(
                            u_all[:, :].rearrange("p (h n) -> p h n", n=64),
                            state[:, :].rearrange("p (h n) -> p h n", n=64), dec_rep))
                        kb.step("v", v, v.tensor_add(u_all[:, :], u_all[:, :], psa[:, :]))
                        kb.step("v", v, v.tensor_copy(u_bf[:, :], u_all[:, :]))
            with ExitStack() as st_ps:
                psw = st_ps.enter_context(nc.psum_tensor(kb.name("psw"), [64, 2048], F32))
                with nc.Block() as blk:
                    @blk.tensor
                    def _(t):
                        for h in range(32):
                            t.matmul(psw[:, h * 64:(h + 1) * 64], swpb[:, :],
                                     u_bf[:, h * 64:(h + 1) * 64], start=True, stop=True)
                with nc.Block() as blk:
                    @blk.vector
                    def _(v):
                        cc_rep = rep_ap(ccol_t, 64, 32, 32, 64)
                        sc_rep = rep_ap(scol_t, 64, 32, 32, 64)
                        kb.step("v", v, v.tensor_mul(
                            w1s[:, :].rearrange("p (h n) -> p h n", n=64),
                            u_all[:, :].rearrange("p (h n) -> p h n", n=64), cc_rep))
                        kb.step("v", v, v.tensor_mul(
                            u_all[:, :].rearrange("p (h n) -> p h n", n=64),
                            psw[:, :].rearrange("p (h n) -> p h n", n=64), sc_rep))
                        kb.step("v", v, v.tensor_add(state[:, :], w1s[:, :], u_all[:, :]))
                        kb.step("v", v, v.tensor_copy(state_bf[0:64, :], state[:, :]))
                        kb.step("v", v, v.tensor_copy(state_bf[64:128, :], state[:, :]))
            st_ph6.close()

            if cg == 3:
                with ExitStack() as st_ps:
                    gz = st_ps.enter_context(nc.sbuf_tensor(kb.name("gz"), [128, 2048], F32))
                    gn_bf = st_ps.enter_context(nc.sbuf_tensor(kb.name("gn_bf"), [128, 2048], BF16))
                    gnf = st_ps.enter_context(nc.sbuf_tensor(kb.name("gnf"), [128, 16, 128], BF16))
                    rcol = st_ps.enter_context(nc.sbuf_tensor(kb.name("rcol"), [128, 2], F32))
                    ptg = [st_ps.enter_context(nc.psum_tensor(kb.name("ptg"), [128, 128], BF16))
                           for _ in range(4)]
                    pout = [st_ps.enter_context(nc.psum_tensor(kb.name("pout"), [128, 128], F32))
                            for _ in range(2)]
                    for cc2 in range(4):
                        with nc.Block() as blk:
                            @blk.scalar
                            def _(s):
                                s.wait_ge(S["ldz"], C["ldz"])
                                kb.step("s", s, s.activation(gz[:, :], z_g[:, cc2, :], AF.Silu))

                            @blk.vector
                            def _(v):
                                v.wait_ge(kb.chsem["s"], kb.chcnt["s"])
                                kb.step("v", v, v.tensor_mul(gz[:, :], gz[:, :],
                                                             y_stage[:, cc2, :]))
                        with nc.Block() as blk:
                            @blk.scalar
                            def _(s):
                                kb.step("s", s, s.activation(gsq[:, :], gz[:, :], AF.Square))

                            @blk.vector
                            def _(v):
                                v.wait_ge(kb.chsem["s"], kb.chcnt["s"])
                                kb.step("v", v, v.reduce_sum(rcol[:, 0:1], gsq[:, :],
                                                             axis=AX.X))
                        with nc.Block() as blk:
                            @blk.scalar
                            def _(s):
                                kb.step("s", s, s.activation(rcol[:, 1:2], rcol[:, 0:1], AF.Ln,
                                                             scale=kb.fc(1.0 / 2048.0),
                                                             bias=kb.fc(1e-5)))
                                kb.step("s", s, s.activation(rcol[:, 1:2], rcol[:, 1:2], AF.Exp,
                                                             scale=kb.fc(-0.5)))
                                kb.step("s", s, s.activation(gn_bf[:, :], gz[:, :], AF.Copy,
                                                             scale=rcol[:, 1:2]))
                        with nc.Block() as blk:
                            b_ttr2, b_sfe2 = C["ttr"], C["sfe"]

                            @blk.tensor
                            def _(t):
                                for pt in range(16):
                                    if pt >= 4:
                                        t.wait_ge(S["sfe"], b_sfe2 + pt - 3)
                                    t.transpose(ptg[pt % 4][:, :],
                                                gn_bf[:, pt * 128:(pt + 1) * 128],
                                                ident_bf[:, :]).then_inc(S["ttr"], 1)

                            @blk.scalar
                            def _(s):
                                for pt in range(16):
                                    s.wait_ge(S["ttr"], b_ttr2 + pt + 1)
                                    s.activation(gnf[:, pt, :], ptg[pt % 4][:, :],
                                                 AF.Copy).then_inc(S["sfe"], 1)
                            C["ttr"] += 16
                            C["sfe"] += 16
                        with nc.Block() as blk:
                            b_mmo, b_evo = C["mmo"], C["evo"]

                            @blk.tensor
                            def _(t):
                                t.wait_ge(S["ldw"], ldw_wo2)
                                for dout in range(8):
                                    if dout >= 2:
                                        t.wait_ge(S["evo"], b_evo + dout - 1)
                                    for k in range(16):
                                        ins = t.matmul(pout[dout % 2][:, :],
                                                       wo2[:, k, dout * 128:(dout + 1) * 128],
                                                       gnf[:, k, :],
                                                       start=(k == 0), stop=(k == 15))
                                    ins.then_inc(S["mmo"], 1)

                            @blk.vector
                            def _(v):
                                for dout in range(8):
                                    v.wait_ge(S["mmo"], b_mmo + dout + 1)
                                    v.tensor_add(x_g[:, dout, cc2 * 128:(cc2 + 1) * 128],
                                                 x_g[:, dout, cc2 * 128:(cc2 + 1) * 128],
                                                 pout[dout % 2][:, :]).then_inc(S["evo"], 1)
                            C["mmo"] += 8
                            C["evo"] += 8
                    with nc.Block() as blk:
                        @blk.sync
                        def _(sync):
                            kb.store(sync, xout_v[:, :, zg * 512:(zg + 1) * 512],
                                     x_g[:, :, :])


def build_ssm_nc(nch=32, php=9):
    kb = KB()
    x_in = kb.din("x", [D, T], F32)
    win = kb.din("win", [1024, 4256], F32)
    wout = kb.din("wout", [2048, 1024], F32)
    dtb = kb.din("dtb", [1, 32], F32)
    alog = kb.din("alog", [1, 32], F32)
    dsk = kb.din("dsk", [1, 32], F32)
    nw = kb.din("nw", [2048, 1], F32)
    y = kb.dout("y", [D, T], F32)
    xres = kb.dscr("xres", [D, T], F32)
    win_bf = kb.dscr("winbf", [1024, 4256], BF16)
    wout2_bf = kb.dscr("wout2bf", [2048, 1024], BF16)
    xn_scr = kb.dscr("xnscr", [D, T], BF16)
    z_scr = kb.dscr("zscr", [T, 2048], BF16)
    xs_scr = kb.dscr("xsscr", [T, 2048], BF16)
    bcd_scr = kb.dscr("bcdscr", [T, 160], F32)
    psf_scr = kb.dscr("psfscr", [32, 128], F32)
    kb.init()
    emit_xin_copy(kb, x_in, xres)
    emit_ssm_preconvert(kb, win, wout, nw, win_bf, wout2_bf)
    sx = SSMSems(kb)
    emit_ssm(kb, sx, xres, y, win_bf, wout2_bf, dtb, alog, dsk,
             xn_scr, z_scr, xs_scr, bcd_scr, psf_scr, nch=nch, php=php)
    nc = kb.finalize()
    return nc, {"c_fvals": kb.fvals_array()}


def build_l1ffn_nc():
    kb = KB()
    x_in = kb.din("x", [D, T], F32)
    k0 = kb.din("k0", [2048, 4096], U8)
    k1 = kb.din("k1", [4096, 4096], U8)
    k2 = kb.din("k2", [4096, 1024], U8)
    s0 = kb.din("s0", [1, 1], F32)
    s1 = kb.din("s1", [1, 1], F32)
    s2 = kb.din("s2", [1, 1], F32)
    y = kb.dout("y", [D, T], F32)
    k0e = kb.dscr("k0e", [1024, 4096], BF16)
    k1bf = kb.dscr("k1bf", [4096, 4096], BF16)
    k2bf = kb.dscr("k2bf", [4096, 1024], BF16)
    xres = kb.dscr("xres", [D, T], F32)
    kb.init()
    emit_xin_copy(kb, x_in, xres)
    st = kb.stack
    raw = emit_scalar_cols2(kb, st, {"s0": s0, "s1": s1, "s2": s2})
    fold = kb.sb("fold", [128, 3], F32)
    with kb.nc.Block() as blk:
        @blk.vector
        def _(v):
            v.tensor_scalar_mul(fold[:, 0:1], raw["s0"], kb.fc(4.0))
            v.tensor_scalar_mul(fold[:, 1:2], raw["s1"], kb.fc(2.0))
            v.tensor_scalar_mul(fold[:, 2:3], raw["s2"], kb.fc(0.5))
    scols = {"s0x2": fold[:, 0:1], "s1x2": fold[:, 1:2], "s2h": fold[:, 2:3]}
    emit_l1ffn_preconvert(kb, k0, k1, k2, k0e, k1bf, k2bf)
    ffs = L1FFNSems(kb)
    emit_l1ffn(kb, ffs, xres, y, k0e, k1bf, k2bf, scols)
    nc = kb.finalize()
    return nc, {"c_fvals": kb.fvals_array()}


def build_full_nc():
    kb = KB()
    x_in = kb.din("x", [D, T], F32)
    w_qkv = kb.din("w_qkv", [1024, 3072], F32)
    w_out = kb.din("w_out", [1024, 1024], F32)
    rw = kb.din("rw", [8, 1024], F32)
    w1 = kb.din("w1", [1024, 8192], F32)
    w2 = kb.din("w2", [8192, 1024], F32)
    win = kb.din("win", [1024, 4256], F32)
    wout = kb.din("wout", [2048, 1024], F32)
    dtb = kb.din("dtb", [1, 32], F32)
    alog = kb.din("alog", [1, 32], F32)
    dsk = kb.din("dsk", [1, 32], F32)
    nw = kb.din("nw", [2048, 1], F32)
    k0 = kb.din("k0", [2048, 4096], U8)
    k1 = kb.din("k1", [4096, 4096], U8)
    k2 = kb.din("k2", [4096, 1024], U8)
    s0 = kb.din("s0", [1, 1], F32)
    s1 = kb.din("s1", [1, 1], F32)
    s2 = kb.din("s2", [1, 1], F32)
    rope = kb.din("c_rope", [4, 128, T], F32)
    y = kb.dout("y", [D, T], F32)
    xres = kb.dscr("xres", [D, T], F32)
    w1bf = kb.dscr("w1bf", [1024, 8192], BF16)
    w2bf = kb.dscr("w2bf", [8192, 1024], BF16)
    wqkv_bf = kb.dscr("wqkvbf", [1024, 3072], BF16)
    wout_bf = kb.dscr("woutbf", [1024, 1024], BF16)
    rope_scr = kb.dscr("ropescr", [4, 128, T], F32)
    k0e = kb.dscr("k0e", [1024, 4096], BF16)
    k1bf = kb.dscr("k1bf", [4096, 4096], BF16)
    k2bf = kb.dscr("k2bf", [4096, 1024], BF16)
    win_bf = kb.dscr("winbf", [1024, 4256], BF16)
    wout2_bf = kb.dscr("wout2bf", [2048, 1024], BF16)
    xn_scr = kb.dscr("xnscr", [D, T], BF16)
    q_scr = kb.dscr("qscr", [8, 128, T], BF16)
    z_scr = kb.dscr("zscr", [T, 2048], BF16)
    xs_scr = kb.dscr("xsscr", [T, 2048], BF16)
    bcd_scr = kb.dscr("bcdscr", [T, 160], F32)
    psf_scr = kb.dscr("psfscr", [32, 128], F32)

    kb.init()
    emit_xin_copy(kb, x_in, xres)
    emit_moe_preconvert(kb, w1, w2, w1bf, w2bf)
    emit_attn_preconvert(kb, w_qkv, w_out, rope, wqkv_bf, wout_bf, rope_scr)
    emit_l1ffn_preconvert(kb, k0, k1, k2, k0e, k1bf, k2bf)
    emit_ssm_preconvert(kb, win, wout, nw, win_bf, wout2_bf)
    st = kb.stack
    raw = emit_scalar_cols2(kb, st, {"s0": s0, "s1": s1, "s2": s2})
    fold = kb.sb("fold", [128, 3], F32)
    with kb.nc.Block() as blk:
        @blk.vector
        def _(v):
            v.tensor_scalar_mul(fold[:, 0:1], raw["s0"], kb.fc(4.0))
            v.tensor_scalar_mul(fold[:, 1:2], raw["s1"], kb.fc(2.0))
            v.tensor_scalar_mul(fold[:, 2:3], raw["s2"], kb.fc(0.5))
    scols = {"s0x2": fold[:, 0:1], "s1x2": fold[:, 1:2], "s2h": fold[:, 2:3]}

    ms = MoESems(kb)
    asx = AttnSems(kb)
    ffs = L1FFNSems(kb)
    sx = SSMSems(kb)
    emit_moe(kb, ms, xres, xres, w1bf, w2bf, rw)
    emit_attn(kb, asx, xres, xres, wqkv_bf, wout_bf, rope_scr, xn_scr, q_scr)
    emit_moe(kb, ms, xres, xres, w1bf, w2bf, rw)
    emit_l1ffn(kb, ffs, xres, xres, k0e, k1bf, k2bf, scols)
    emit_ssm(kb, sx, xres, xres, win_bf, wout2_bf, dtb, alog, dsk,
             xn_scr, z_scr, xs_scr, bcd_scr, psf_scr)
    emit_l1ffn(kb, ffs, xres, y, k0e, k1bf, k2bf, scols)
    nc = kb.finalize()
    return nc, {"c_fvals": kb.fvals_array(), "c_rope": rope_tables()}


def kernel(**inputs):
    consts = host_constants()
    nc, extra = build_full_nc()
    consts.update(extra)
    x = np.asarray(inputs["L0_attn_x"])
    in_maps = []
    for b in range(NCORES):
        m = dict(consts)
        m["x"] = np.ascontiguousarray(x[b].T)
        m["w_qkv"] = np.asarray(inputs["L0_attn_w_qkv"], np.float32)
        m["w_out"] = np.asarray(inputs["L0_attn_w_out"], np.float32)
        m["rw"] = np.asarray(inputs["L0_ffn_router_w"], np.float32)
        m["w1"] = np.asarray(inputs["L0_ffn_w1"], np.float32)
        m["w2"] = np.asarray(inputs["L0_ffn_w2"], np.float32)
        m["win"] = np.asarray(inputs["L1_attn_W_in"], np.float32)
        m["wout"] = np.asarray(inputs["L1_attn_W_out"], np.float32)
        m["dtb"] = np.asarray(inputs["L1_attn_dt_bias"], np.float32).reshape(1, 32)
        m["alog"] = np.asarray(inputs["L1_attn_A_log"], np.float32).reshape(1, 32)
        m["dsk"] = np.asarray(inputs["L1_attn_D_skip"], np.float32).reshape(1, 32)
        m["nw"] = np.asarray(inputs["L1_attn_norm_weight"], np.float32).reshape(2048, 1)
        m["k0"] = np.asarray(inputs["L1_ffn_k0"]).view(np.uint8)
        m["k1"] = np.asarray(inputs["L1_ffn_k1"]).view(np.uint8)
        m["k2"] = np.asarray(inputs["L1_ffn_k2"]).view(np.uint8)
        m["s0"] = np.asarray(inputs["L1_ffn_s0"], np.float32).reshape(1, 1)
        m["s1"] = np.asarray(inputs["L1_ffn_s1"], np.float32).reshape(1, 1)
        m["s2"] = np.asarray(inputs["L1_ffn_s2"], np.float32).reshape(1, 1)
        in_maps.append(m)
    res = run_bass_kernel_spmd(nc, in_maps, core_ids=list(range(NCORES)))
    out = np.stack([np.ascontiguousarray(res.results[b]["y"].T)
                    for b in range(NCORES)]).astype(np.float32)
    return out


# version 07
# speedup vs baseline: 1.2628x; 1.0218x over previous
import numpy as np
import ml_dtypes
from contextlib import ExitStack

import concourse.bass as bass
import concourse.mybir as mybir
from concourse.bass_utils import run_bass_kernel_spmd

F32 = mybir.dt.float32
BF16 = mybir.dt.bfloat16
U8 = mybir.dt.uint8
AF = mybir.ActivationFunctionType
ALU = mybir.AluOpType
AX = mybir.AxisListType

D = 1024
T = 4096
NCORES = 8

BF16NP = ml_dtypes.bfloat16


CONST_SPECS = {
    "c_ident_bf": ([128, 128], BF16),
    "c_ident_f32": ([128, 128], F32),
    "c_ones_row": ([1, 128], F32),
    "c_ones_col_bf": ([128, 1], BF16),
    "c_swpq": ([128, 128], F32),
    "c_diagmask": ([128, 896], BF16),
    "c_ut01": ([128, 128], F32),
    "c_sl01": ([128, 128], F32),
    "c_slbig": ([128, 128], F32),
    "c_invfrep": ([128, 512], F32),
    "c_invf16": ([32, 16], F32),
    "c_swpb": ([64, 64], BF16),
}
NFVALS = 64


def host_constants():
    c = {}
    c["c_ident_bf"] = np.eye(128, dtype=BF16NP)
    c["c_ident_f32"] = np.eye(128, dtype=np.float32)
    c["c_ones_row"] = np.ones((1, 128), dtype=np.float32)
    c["c_ones_col_bf"] = np.ones((128, 1), dtype=BF16NP)
    swp = np.zeros((64, 64), np.float32)
    for i in range(32):
        swp[2 * i, 2 * i + 1] = -1.0
        swp[2 * i + 1, 2 * i] = 1.0
    swpT = swp.T
    sw = np.zeros((128, 128), np.float32)
    sw[0:64, 0:64] = swpT
    sw[64:128, 64:128] = swpT
    c["c_swpq"] = sw
    s_i = np.arange(128)[:, None]
    u_i = np.arange(896)[None, :]
    c["c_diagmask"] = (s_i <= u_i - 384).astype(BF16NP)
    k_i = np.arange(128)[:, None]
    m_i = np.arange(128)[None, :]
    c["c_ut01"] = (k_i <= m_i).astype(np.float32)
    c["c_sl01"] = (k_i > m_i).astype(np.float32)
    c["c_slbig"] = (1e30 * (k_i > m_i)).astype(np.float32)
    invf = (1.0 / 10000.0 ** (np.arange(16) / 16.0)).astype(np.float32)
    c["c_invfrep"] = np.broadcast_to(np.tile(invf, 32)[None, :], (128, 512)).copy()
    c["c_invf16"] = np.broadcast_to(invf[None, :], (32, 16)).copy()
    swpb = np.zeros((64, 64), np.float32)
    for i in range(16):
        swpb[i, 16 + i] = -1.0
        swpb[16 + i, i] = 1.0
    c["c_swpb"] = swpb.T.astype(BF16NP)
    return c


def rope_tables():
    inv = 1.0 / 10000.0 ** (np.arange(0, 64, 2, dtype=np.float64) / 64)
    t = np.arange(T, dtype=np.float64)
    fr = np.outer(t, inv)
    c64 = np.repeat(np.cos(fr), 2, axis=1).T
    s64 = np.repeat(np.sin(fr), 2, axis=1).T
    c128 = np.concatenate([c64, c64], 0).astype(np.float32)
    s128 = np.concatenate([s64, s64], 0).astype(np.float32)
    return np.stack([0.125 * c128, 0.125 * s128, c128, s128]).astype(np.float32)


class KB:
    def __init__(self):
        self.nc = bass.Bass("TRN2", target_bir_lowering=False)
        self.stack = ExitStack()
        self.uid = 0
        self.const_sb = {}
        self.store_sem = None
        self.store_cnt = 0

    def name(self, p):
        self.uid += 1
        return f"{p}_{self.uid}"

    def sb(self, p, shape, dt, side="left"):
        return self.stack.enter_context(
            self.nc.sbuf_tensor(self.name(p), shape, dt, side=side))

    def sem(self, p="sem"):
        return self.stack.enter_context(self.nc.semaphore(self.name(p)))

    def din(self, nm, shape, dt):
        return self.nc.dram_tensor(nm, list(shape), dt, kind="ExternalInput").ap()

    def dout(self, nm, shape, dt):
        return self.nc.dram_tensor(nm, list(shape), dt, kind="ExternalOutput").ap()

    def dscr(self, nm, shape, dt):
        return self.nc.dram_tensor(self.name(nm), list(shape), dt).ap()

    def init(self):
        nc = self.nc
        self.store_sem = self.sem("g_store")
        self.ff_sem = self.sem("g_ff")
        self.ff_cnt = 0
        self.chain_init()
        self.fvals = {}
        self.fpool = self.sb("c_fpool_sb", [128, NFVALS], F32)
        for nm, (shape, dt) in CONST_SPECS.items():
            self.const_sb[nm] = self.sb(nm, shape, dt)
        ld = self.sem("c_ld")
        n = 0
        specs = list(CONST_SPECS.items())
        cin = {nm: self.din(nm, shape, dt) for nm, (shape, dt) in specs}
        fv_in = self.din("c_fvals", [128, NFVALS], F32)
        with nc.Block() as blk:
            @blk.sync
            def _(sync):
                nonlocal n
                for nm, _spec in specs:
                    sync.dma_start(self.const_sb[nm][:, :], cin[nm]).then_inc(ld, 16)
                    n += 1
                sync.dma_start(self.fpool[:, :], fv_in).then_inc(ld, 16)
                n += 1
                sync.wait_ge(ld, 16 * n)

    def chain_init(self):
        self.chsem = {"v": self.sem("chv"), "s": self.sem("chs")}
        self.chcnt = {"v": 0, "s": 0}
        self.lnsem = {n: self.sem("ln" + n)
                      for n in ("ld", "ld2", "ldh", "s1", "s2", "t1", "t2", "t3")}
        self.lncnt = {n: 0 for n in ("ld", "ld2", "ldh", "s1", "s2", "t1", "t2", "t3")}

    def step(self, key, eng, ins):
        self.chcnt[key] += 1
        ins.then_inc(self.chsem[key], 1)
        eng.wait_ge(self.chsem[key], self.chcnt[key])

    def fc(self, value):
        v = float(value)
        if v not in self.fvals:
            idx = len(self.fvals)
            assert idx < NFVALS, "float const pool exhausted"
            self.fvals[v] = idx
            self.nc.const_aps.register(F32, v, self.fpool[:, idx:idx + 1])
        return v

    def fvals_array(self):
        arr = np.zeros((128, NFVALS), dtype=np.float32)
        for v, i in self.fvals.items():
            arr[:, i] = v
        return arr

    def dma_ff(self, q, dst_ap, src_ap):
        q.dma_start(dst_ap, src_ap).then_inc(self.ff_sem, 16)
        self.ff_cnt += 16

    def ffw(self, eng):
        eng.wait_ge(self.ff_sem, self.ff_cnt)

    def store(self, q, dst_ap, src_ap):
        q.dma_start(dst_ap, src_ap).then_inc(self.store_sem, 16)
        self.store_cnt += 16

    def finalize(self):
        nc = self.nc
        cnt = self.store_cnt
        with nc.Block() as blk:
            @blk.sync
            def _(sync):
                sync.wait_ge(self.store_sem, cnt)
        self.stack.close()
        return nc


def xview(x_hbm):
    return x_hbm.rearrange("(c p) t -> p c t", p=128)


def emit_ln_slab(kb, st, x_sl, xn_bf, scr_bf, nsl, xsrc=None, prefetch=(),
                 ld_wait=None, store_wait=False):
    nc = kb.nc
    onescol = kb.const_sb["c_ones_col_bf"]
    onesrow = kb.const_sb["c_ones_row"]
    nq = nsl // 512
    L, LC = kb.lnsem, kb.lncnt
    ps_sum = st.enter_context(nc.psum_tensor(kb.name("ps_sum"), [1, nsl], F32))
    ps_sq = st.enter_context(nc.psum_tensor(kb.name("ps_sq"), [1, nsl], F32))
    ps_mb = st.enter_context(nc.psum_tensor(kb.name("ps_mb"), [128, nsl], F32))
    ps_rb = st.enter_context(nc.psum_tensor(kb.name("ps_rb"), [128, nsl], F32))
    m_row = st.enter_context(nc.sbuf_tensor(kb.name("m_row"), [1, nsl], F32))
    v_row = st.enter_context(nc.sbuf_tensor(kb.name("v_row"), [1, nsl], F32))
    r_row = st.enter_context(nc.sbuf_tensor(kb.name("r_row"), [1, nsl], F32))
    tmp = st.enter_context(nc.sbuf_tensor(kb.name("ln_tmp"), [128, nsl], F32))
    scr2 = st.enter_context(nc.sbuf_tensor(kb.name("ln_scr2"), [128, 8, nsl], BF16))
    b = {n: LC[n] for n in LC}
    bv, bs = kb.chcnt["v"], kb.chcnt["s"]

    with nc.Block() as blk:
        if xsrc is not None or prefetch:
            @blk.sync
            def _(sync):
                kb.ffw(sync)
                if store_wait and prefetch:
                    sync.wait_ge(kb.store_sem, kb.store_cnt)
                if xsrc is not None:
                    sync.dma_start(x_sl[:, 0:4, :], xsrc[:, 0:4, :]).then_inc(L["ld"], 16)
                    sync.dma_start(x_sl[:, 4:8, :], xsrc[:, 4:8, :]).then_inc(L["ldh"], 16)
                for dst, srcap, snm in prefetch:
                    sync.dma_start(dst, srcap).then_inc(L[snm], 16)
            if xsrc is not None:
                LC["ld"] += 16
                LC["ldh"] += 16
            for dst, srcap, snm in prefetch:
                LC[snm] += 16

        @blk.scalar
        def _(s):
            kb.ffw(s)
            for half in range(2):
                cs = slice(half * 4, half * 4 + 4)
                if xsrc is not None:
                    hs = "ld" if half == 0 else "ldh"
                    s.wait_ge(L[hs], b[hs] + 16)
                elif ld_wait is not None and half == 0:
                    s.wait_ge(L[ld_wait[0]], ld_wait[1])
                s.activation(scr2[:, cs, :], x_sl[:, cs, :],
                             AF.Square).then_inc(L["s2"], 1)
            s.wait_ge(kb.chsem["v"], bv + 4)
            kb.step("s", s, s.activation(v_row[:, :], v_row[:, :], AF.Ln, bias=kb.fc(1e-5)))
            kb.step("s", s, s.activation(r_row[:, :], v_row[:, :], AF.Exp, scale=kb.fc(-0.5)))

        @blk.tensor
        def _(t):
            for half in range(2):
                t.wait_ge(L["s1"], b["s1"] + half + 1)
                for c in range(half * 4, half * 4 + 4):
                    for h in range(nq):
                        t.matmul(ps_sum[0:1, h * 512:(h + 1) * 512], onescol[:, :],
                                 scr_bf[:, c, h * 512:(h + 1) * 512],
                                 start=(c == 0), stop=(c == 7))
                t.wait_ge(L["s2"], b["s2"] + half + 1)
                ins = None
                for c in range(half * 4, half * 4 + 4):
                    for h in range(nq):
                        ins = t.matmul(ps_sq[0:1, h * 512:(h + 1) * 512], onescol[:, :],
                                       scr2[:, c, h * 512:(h + 1) * 512],
                                       start=(c == 0), stop=(c == 7))
            ins.then_inc(L["t1"], 1)
            t.wait_ge(kb.chsem["s"], bs + 2)
            for h in range(nq):
                sl = slice(h * 512, (h + 1) * 512)
                t.matmul(ps_mb[:, sl], onesrow[:, :], m_row[0:1, sl], start=True, stop=True)
                ins = t.matmul(ps_rb[:, sl], onesrow[:, :], r_row[0:1, sl], start=True, stop=True)
            ins.then_inc(L["t3"], 1)

        @blk.vector
        def _(v):
            kb.ffw(v)
            for half in range(2):
                cs = slice(half * 4, half * 4 + 4)
                if xsrc is not None:
                    hs = "ld" if half == 0 else "ldh"
                    v.wait_ge(L[hs], b[hs] + 16)
                elif ld_wait is not None and half == 0:
                    v.wait_ge(L[ld_wait[0]], ld_wait[1])
                v.tensor_copy(scr_bf[:, cs, :], x_sl[:, cs, :]).then_inc(L["s1"], 1)
            v.wait_ge(L["t1"], b["t1"] + 1)
            kb.step("v", v, v.tensor_scalar_mul(m_row[:, :], ps_sum[:, :], kb.fc(1.0 / D)))
            kb.step("v", v, v.tensor_scalar_mul(v_row[:, :], ps_sq[:, :], kb.fc(1.0 / D)))
            kb.step("v", v, v.tensor_mul(r_row[:, :], m_row[:, :], m_row[:, :]))
            kb.step("v", v, v.tensor_sub(v_row[:, :], v_row[:, :], r_row[:, :]))
            v.wait_ge(L["t3"], b["t3"] + 1)
            for c in range(8):
                kb.step("v", v, v.tensor_sub(tmp[:, :], x_sl[:, c, :], ps_mb[:, :]))
                kb.step("v", v, v.tensor_mul(xn_bf[:, c, :], tmp[:, :], ps_rb[:, :]))
    LC["s1"] += 2
    LC["s2"] += 2
    LC["t1"] += 1
    LC["t3"] += 1


def emit_scalar_cols2(kb, st, scalar_aps):
    nc = kb.nc
    onesrow = kb.const_sb["c_ones_row"]
    names = list(scalar_aps)
    n = len(names)
    row = st.enter_context(nc.sbuf_tensor(kb.name("sc_row"), [1, n], F32))
    allc = st.enter_context(nc.sbuf_tensor(kb.name("sc_cols"), [128, n], F32))
    ld = kb.sem("sc_ld")
    with nc.Block() as blk:
        @blk.sync
        def _(sync):
            for i, nm in enumerate(names):
                sync.dma_start(row[0:1, i:i + 1], scalar_aps[nm]).then_inc(ld, 16)
            sync.wait_ge(ld, 16 * n)
    with nc.psum_tensor(kb.name("sc_ps"), [128, n], F32) as ps:
        with nc.Block() as blk:
            @blk.tensor
            def _(t):
                t.matmul(ps[:, :], onesrow[:, :], row[:, :], start=True, stop=True)
        with nc.Block() as blk:
            @blk.vector
            def _(v):
                v.tensor_copy(allc[:, :], ps[:, :])
    return {nm: allc[:, i:i + 1] for i, nm in enumerate(names)}


def emit_l1ffn_preconvert(kb, k0_hbm, k1_hbm, k2_hbm, k0e_hbm, k1bf_hbm, k2bf_hbm):
    nc = kb.nc
    with ExitStack() as st:
        ta = st.enter_context(nc.sbuf_tensor(kb.name("pc_a"), [128, 4096], BF16))
        tb = st.enter_context(nc.sbuf_tensor(kb.name("pc_b"), [128, 4096], BF16))
        tc = st.enter_context(nc.sbuf_tensor(kb.name("pc_c"), [128, 4096], BF16))
        ld = kb.sem("pc_ld")
        stc = kb.sem("pc_st")
        nld = 0
        nst = 0
        for c in range(8):
            with nc.Block() as blk:
                @blk.gpsimd
                def _(g):
                    nonlocal nld
                    g.dma_start(ta[:, :], k0_hbm[c * 128:(c + 1) * 128, :]).then_inc(ld, 16)
                    g.dma_start(tb[:, :], k0_hbm[1024 + c * 128:1024 + (c + 1) * 128, :]).then_inc(ld, 16)
                    nld += 32
                    g.wait_ge(ld, nld)
            with nc.Block() as blk:
                @blk.vector
                def _(v):
                    v.tensor_sub(tc[:, :], ta[:, :], tb[:, :])
            with nc.Block() as blk:
                @blk.sync
                def _(sync):
                    nonlocal nst
                    sync.dma_start(k0e_hbm[c * 128:(c + 1) * 128, :], tc[:, :]).then_inc(stc, 16)
                    nst += 16
                    sync.wait_ge(stc, nst)
        for src, dst, nchunks, w in ((k1_hbm, k1bf_hbm, 32, 4096), (k2_hbm, k2bf_hbm, 32, 1024)):
            for c in range(nchunks):
                with nc.Block() as blk:
                    @blk.gpsimd
                    def _(g):
                        nonlocal nld
                        g.dma_start(ta[:, 0:w], src[c * 128:(c + 1) * 128, :]).then_inc(ld, 16)
                        nld += 16
                        g.wait_ge(ld, nld)
                with nc.Block() as blk:
                    @blk.scalar
                    def _(s):
                        s.activation(tc[:, 0:w], ta[:, 0:w], AF.Copy, bias=kb.fc(-1.0), scale=kb.fc(2.0))
                with nc.Block() as blk:
                    @blk.sync
                    def _(sync):
                        nonlocal nst
                        sync.dma_start(dst[c * 128:(c + 1) * 128, :], tc[:, 0:w]).then_inc(stc, 16)
                        nst += 16
                        sync.wait_ge(stc, nst)


class L1FFNSems:
    def __init__(self, kb):
        names = ["ldx", "ldk0", "ldk1", "ldk1b", "ldk2", "mm1", "ev1", "mm2",
                 "ev2", "mm3", "ev3"]
        self.s = {n: kb.sem("l1f_" + n) for n in names}
        self.c = {n: 0 for n in names}


def emit_l1ffn(kb, ffs, x_in_hbm, x_out_hbm, k0e_hbm, k1bf_hbm, k2bf_hbm, scols):
    nc = kb.nc
    S, C = ffs.s, ffs.c
    NSL = 1024
    xin_v = xview(x_in_hbm)
    xout_v = xview(x_out_hbm)
    k0e_v = k0e_hbm.rearrange("(c p) w -> p c w", p=128)
    k1_v = k1bf_hbm.rearrange("(c p) m -> p c m", p=128)
    k2_v = k2bf_hbm.rearrange("(c p) m -> p c m", p=128)

    for isl in range(T // NSL):
        tsl = slice(isl * NSL, (isl + 1) * NSL)
        with ExitStack() as st_xn:
            xn = st_xn.enter_context(nc.sbuf_tensor(kb.name("xn"), [128, 8, NSL], BF16))
            with ExitStack() as st_ln:
                x_sl = st_ln.enter_context(nc.sbuf_tensor(kb.name("x_sl"), [128, 8, NSL], F32))
                scr = st_ln.enter_context(nc.sbuf_tensor(kb.name("scr"), [128, 8, NSL], BF16))
                emit_ln_slab(kb, st_ln, x_sl, xn, scr, NSL, xsrc=xin_v[:, :, tsl])
            with ExitStack() as st_a2:
                a2 = st_a2.enter_context(
                    nc.sbuf_tensor(kb.name("a2"), [128, 32, NSL], BF16, side="right"))
                with ExitStack() as st_a1:
                    a1 = st_a1.enter_context(nc.sbuf_tensor(kb.name("a1"), [128, 32, NSL], BF16))
                    with ExitStack() as st_k0:
                        k0e = st_k0.enter_context(nc.sbuf_tensor(kb.name("k0e"), [128, 8, 2048], BF16))
                        gph = 16 * (NSL // 512)
                        with ExitStack() as st_ps, nc.Block() as blk:
                            psl = [st_ps.enter_context(nc.psum_tensor(kb.name("p1"), [128, 512], F32))
                                   for _ in range(4)]
                            mm0, ev0, ld0 = C["mm1"], C["ev1"], C["ldk0"]

                            @blk.sync
                            def _(sync):
                                for wh in range(2):
                                    if wh == 1:
                                        sync.wait_ge(S["mm1"], mm0 + gph)
                                    sync.dma_start(k0e[:, :, :],
                                                   k0e_v[:, :, wh * 2048:(wh + 1) * 2048]
                                                   ).then_inc(S["ldk0"], 16)

                            @blk.tensor
                            def _(t):
                                g = 0
                                for wh in range(2):
                                    t.wait_ge(S["ldk0"], ld0 + 16 * (wh + 1))
                                    for w in range(16):
                                        for h in range(NSL // 512):
                                            if g >= 4:
                                                t.wait_ge(S["ev1"], ev0 + g - 3)
                                            for k in range(8):
                                                ins = t.matmul(psl[g % 4][:, :],
                                                               k0e[:, k, w * 128:(w + 1) * 128],
                                                               xn[:, k, h * 512:(h + 1) * 512],
                                                               start=(k == 0), stop=(k == 7))
                                            ins.then_inc(S["mm1"], 1)
                                            g += 1

                            @blk.scalar
                            def _(s):
                                g = 0
                                for wh in range(2):
                                    for w in range(16):
                                        for h in range(NSL // 512):
                                            s.wait_ge(S["mm1"], mm0 + g + 1)
                                            s.activation(a1[:, wh * 16 + w, h * 512:(h + 1) * 512],
                                                         psl[g % 4][:, :], AF.Relu,
                                                         scale=scols["s0x2"]).then_inc(S["ev1"], 1)
                                            g += 1
                            C["ldk0"] += 32
                            C["mm1"] += 64
                            C["ev1"] += 64
                    with ExitStack() as st_k1:
                        k1b = [st_k1.enter_context(
                                   nc.sbuf_tensor(kb.name("k1b"), [128, 32, 256], BF16, side="right"))
                               for _ in range(2)]
                        npc = 16
                        gpp = 2 * (NSL // 512)
                        with ExitStack() as st_ps, nc.Block() as blk:
                            psl = [st_ps.enter_context(nc.psum_tensor(kb.name("p2"), [128, 512], F32))
                                   for _ in range(4)]
                            mm0, ev0 = C["mm2"], C["ev2"]
                            ld0 = [C["ldk1"], C["ldk1b"]]
                            k1sem = [S["ldk1"], S["ldk1b"]]

                            @blk.sync
                            def _(sync):
                                for pc in range(npc):
                                    if pc >= 2:
                                        sync.wait_ge(S["mm2"], mm0 + gpp * (pc - 1))
                                    sync.dma_start(k1b[pc % 2][:, :, :],
                                                   k1_v[:, :, pc * 256:(pc + 1) * 256]
                                                   ).then_inc(k1sem[pc % 2], 16)

                            @blk.tensor
                            def _(t):
                                g = 0
                                for pc in range(npc):
                                    t.wait_ge(k1sem[pc % 2], ld0[pc % 2] + 16 * (pc // 2 + 1))
                                    for mc in range(2):
                                        for h in range(NSL // 512):
                                            if g >= 4:
                                                t.wait_ge(S["ev2"], ev0 + g - 3)
                                            for k in range(32):
                                                ins = t.matmul(psl[g % 4][:, :],
                                                               k1b[pc % 2][:, k, mc * 128:(mc + 1) * 128],
                                                               a1[:, k, h * 512:(h + 1) * 512],
                                                               start=(k == 0), stop=(k == 31))
                                            ins.then_inc(S["mm2"], 1)
                                            g += 1

                            @blk.scalar
                            def _(s):
                                g = 0
                                for pc in range(npc):
                                    for mc in range(2):
                                        for h in range(NSL // 512):
                                            s.wait_ge(S["mm2"], mm0 + g + 1)
                                            s.activation(a2[:, pc * 2 + mc, h * 512:(h + 1) * 512],
                                                         psl[g % 4][:, :], AF.Relu,
                                                         scale=scols["s1x2"]).then_inc(S["ev2"], 1)
                                            g += 1
                            C["ldk1"] += 16 * (npc // 2)
                            C["ldk1b"] += 16 * (npc // 2)
                            C["mm2"] += npc * gpp
                            C["ev2"] += npc * gpp
                with ExitStack() as st_m3:
                    k2sb = st_m3.enter_context(nc.sbuf_tensor(kb.name("k2sb"), [128, 32, 1024], BF16))
                    x_sl2 = st_m3.enter_context(nc.sbuf_tensor(kb.name("x_sl2"), [128, 8, NSL], F32))
                    x_out = x_sl2
                    with ExitStack() as st_ps, nc.Block() as blk:
                        psl = [st_ps.enter_context(nc.psum_tensor(kb.name("p3"), [128, 512], F32))
                               for _ in range(4)]
                        mm0, ev0, ld0 = C["mm3"], C["ev3"], C["ldk2"]

                        @blk.sync
                        def _(sync):
                            sync.dma_start(k2sb[:, :, :], k2_v).then_inc(S["ldk2"], 16)
                            sync.dma_start(x_sl2[:, :, :], xin_v[:, :, tsl]).then_inc(S["ldk2"], 16)
                            sync.wait_ge(S["ev3"], ev0 + 8 * (NSL // 512))
                            kb.store(sync, xout_v[:, :, tsl], x_out[:, :, :])

                        @blk.tensor
                        def _(t):
                            t.wait_ge(S["ldk2"], ld0 + 32)
                            g = 0
                            for dout in range(8):
                                for h in range(NSL // 512):
                                    if g >= 4:
                                        t.wait_ge(S["ev3"], ev0 + g - 3)
                                    for k in range(32):
                                        ins = t.matmul(psl[g % 4][:, :],
                                                       k2sb[:, k, dout * 128:(dout + 1) * 128],
                                                       a2[:, k, h * 512:(h + 1) * 512],
                                                       start=(k == 0), stop=(k == 31))
                                    ins.then_inc(S["mm3"], 1)
                                    g += 1

                        @blk.vector
                        def _(v):
                            v.wait_ge(S["ldk2"], ld0 + 32)
                            g = 0
                            for dout in range(8):
                                for h in range(NSL // 512):
                                    v.wait_ge(S["mm3"], mm0 + g + 1)
                                    sl = slice(h * 512, (h + 1) * 512)
                                    v.scalar_tensor_tensor(
                                        x_out[:, dout, sl], psl[g % 4][:, :],
                                        scols["s2h"], x_sl2[:, dout, sl],
                                        ALU.mult, ALU.add).then_inc(S["ev3"], 1)
                                    g += 1
                        C["ldk2"] += 32
                        C["mm3"] += 16
                        C["ev3"] += 16


def emit_xin_copy(kb, x_ext, xres):
    nc = kb.nc
    sem = kb.sem("xc")
    n = 0
    xi = xview(x_ext)
    xo = xview(xres)
    with ExitStack() as st:
        big = st.enter_context(nc.sbuf_tensor(kb.name("xcopy"), [128, 8, T // 2], F32))
        for h in range(2):
            tsl = slice(h * (T // 2), (h + 1) * (T // 2))
            with nc.Block() as blk:
                @blk.sync
                def _(sync):
                    nonlocal n
                    sync.dma_start(big[:, :, :], xi[:, :, tsl]).then_inc(sem, 16)
                    n += 16
                    sync.wait_ge(sem, n)
                    sync.dma_start(xo[:, :, tsl], big[:, :, :]).then_inc(sem, 16)
                    n += 16
                    sync.wait_ge(sem, n)


def emit_moe_preconvert(kb, w1_hbm, w2_hbm, w1bf_hbm, w2bf_hbm):
    nc = kb.nc
    w1v = w1_hbm.rearrange("(c p) m -> p c m", p=128)
    w1bv = w1bf_hbm.rearrange("(c p) m -> p c m", p=128)
    w2v = w2_hbm.rearrange("(c p) d -> p c d", p=128)
    w2bv = w2bf_hbm.rearrange("(c p) d -> p c d", p=128)
    with ExitStack() as st:
        buf = st.enter_context(nc.sbuf_tensor(kb.name("pcw"), [128, 8192], BF16))
        ld = kb.sem("pcw_ld")
        stc = kb.sem("pcw_st")
        nld = nst = 0
        for c in range(8):
            with nc.Block() as blk:
                @blk.gpsimd
                def _(g):
                    nonlocal nld
                    g.dma_start(buf[:, :], w1v[:, c, :]).then_inc(ld, 16)
                    nld += 16
                    g.wait_ge(ld, nld)
            with nc.Block() as blk:
                @blk.sync
                def _(sync):
                    nonlocal nst
                    sync.dma_start(w1bv[:, c, :], buf[:, :]).then_inc(stc, 16)
                    nst += 16
                    sync.wait_ge(stc, nst)
        bview = buf[:, :].rearrange("p (c d) -> p c d", c=8)
        for g8 in range(8):
            with nc.Block() as blk:
                @blk.gpsimd
                def _(g):
                    nonlocal nld
                    g.dma_start(bview, w2v[:, g8 * 8:(g8 + 1) * 8, :]).then_inc(ld, 16)
                    nld += 16
                    g.wait_ge(ld, nld)
            with nc.Block() as blk:
                @blk.sync
                def _(sync):
                    nonlocal nst
                    sync.dma_start(w2bv[:, g8 * 8:(g8 + 1) * 8, :], bview).then_inc(stc, 16)
                    nst += 16
                    sync.wait_ge(stc, nst)


class MoESems:
    def __init__(self, kb):
        names = ["ldx", "ldw1", "ldw1b", "ldw2", "ldw2b", "ldr", "mm1", "rel",
                 "va", "gb", "mm2", "ev2", "lg"]
        self.s = {n: kb.sem("moe_" + n) for n in names}
        self.c = {n: 0 for n in names}


def emit_moe(kb, ms, x_in_hbm, x_out_hbm, w1bf_hbm, w2bf_hbm, rw_hbm, dbg=None, parts="all"):
    nc = kb.nc
    S, C = ms.s, ms.c
    NSL = 512
    ident = kb.const_sb["c_ident_f32"]
    onesrow = kb.const_sb["c_ones_row"]
    xin_v = xview(x_in_hbm)
    xout_v = xview(x_out_hbm)
    w1bv = w1bf_hbm.rearrange("(c p) m -> p c m", p=128)
    w2bv = w2bf_hbm.rearrange("(c p) d -> p c d", p=128)

    g_scr = kb.dscr("g_scr", [8, NSL], F32)
    with ExitStack() as st_call:
        rw_sb = st_call.enter_context(nc.sbuf_tensor(kb.name("rw"), [8, 1024], F32))
        rwT = st_call.enter_context(nc.sbuf_tensor(kb.name("rwT"), [128, 8, 8], BF16))
        with nc.Block() as blk:
            @blk.sync
            def _(sync):
                sync.dma_start(rw_sb[:, :], rw_hbm).then_inc(S["ldr"], 16)
                C["ldr"] += 16
                sync.wait_ge(S["ldr"], C["ldr"])
        with ExitStack() as st_ps:
            ps_rt = st_ps.enter_context(nc.psum_tensor(kb.name("ps_rt"), [128, 8, 8], F32))
            with nc.Block() as blk:
                @blk.tensor
                def _(t):
                    for c in range(8):
                        t.matmul(ps_rt[:, c, :], rw_sb[0:8, c * 128:(c + 1) * 128],
                                 ident[0:8, 0:8], start=True, stop=True)
            with nc.Block() as blk2:
                @blk2.scalar
                def _(s):
                    s.activation(rwT[:, :, :], ps_rt[:, :, :], AF.Copy)

        x_ln2 = [st_call.enter_context(nc.sbuf_tensor(kb.name("x_sl"), [128, 8, NSL], F32))
                 for _ in range(2)]
        base_ld = {"ld": kb.lncnt["ld"], "ld2": kb.lncnt["ld2"]}
        lpn = lambda s: "ld" if s % 2 == 0 else "ld2"
        for isl in range(T // NSL):
            tsl = slice(isl * NSL, (isl + 1) * NSL)
            x_sl = x_ln2[isl % 2]
            with ExitStack() as st_sl:
                xn = st_sl.enter_context(nc.sbuf_tensor(kb.name("xn"), [128, 8, NSL], BF16))
                gT = st_sl.enter_context(nc.sbuf_tensor(kb.name("gT"), [1, 8, NSL], F32))
                with ExitStack() as st_ln:
                    scr = st_ln.enter_context(nc.sbuf_tensor(kb.name("scr"), [128, 8, NSL], BF16))
                    pf = []
                    if isl + 1 < T // NSL:
                        nts = slice((isl + 1) * NSL, (isl + 2) * NSL)
                        pf.append((x_ln2[(isl + 1) % 2][:, :, :], xin_v[:, :, nts],
                                   lpn(isl + 1)))
                    if isl == 0:
                        emit_ln_slab(kb, st_ln, x_sl, xn, scr, NSL,
                                     xsrc=xin_v[:, :, tsl], prefetch=pf,
                                     store_wait=True)
                    else:
                        emit_ln_slab(kb, st_ln, x_sl, xn, scr, NSL, prefetch=pf,
                                     store_wait=True,
                                     ld_wait=(lpn(isl),
                                              base_ld[lpn(isl)] + 16 * (isl // 2 + 1)))
                if parts == "noln":
                    continue
                skip_router = parts in ("norouter",)
                with ExitStack() as st_rt:
                    probs = st_rt.enter_context(nc.sbuf_tensor(kb.name("probs"), [128, 4, 8], F32))
                    ps_lg = st_rt.enter_context(nc.psum_tensor(kb.name("ps_lg"), [128, 4, 8], F32))
                    with nc.Block() as blk:
                        @blk.tensor
                        def _(t):
                            for tg in range(4):
                                for c in range(8):
                                    t.matmul(ps_lg[:, tg, :],
                                             xn[:, c, tg * 128:(tg + 1) * 128],
                                             rwT[:, c, :], start=(c == 0), stop=(c == 7))
                    with nc.Block() as blk:
                        @blk.scalar
                        def _(s):
                            s.activation(probs[:, :, :], ps_lg[:, :, :], AF.Sigmoid)
                    g_tile = st_rt.enter_context(nc.sbuf_tensor(kb.name("gt2"), [128, 4, 8], F32))
                    cols = st_rt.enter_context(nc.sbuf_tensor(kb.name("tcol"), [128, 4, 6], F32))
                    eq1 = st_rt.enter_context(nc.sbuf_tensor(kb.name("eq1"), [128, 4, 8], F32))
                    eq2 = st_rt.enter_context(nc.sbuf_tensor(kb.name("eq2"), [128, 4, 8], F32))
                    p2 = st_rt.enter_context(nc.sbuf_tensor(kb.name("p2t"), [128, 4, 8], F32))
                    with nc.Block() as blk:
                        @blk.vector
                        def _(v):
                            chv = kb.chsem["v"]
                            bvr = kb.chcnt["v"]
                            def mk_ops(tg):
                                pr = probs[:, tg, :]
                                m1, m2, den, rcp, w1c, w2c = [
                                    cols[:, tg, i:i + 1] for i in range(6)]
                                e1, e2, pp = eq1[:, tg, :], eq2[:, tg, :], p2[:, tg, :]
                                return [
                                    lambda: v.reduce_max(m1, pr, axis=AX.X),
                                    lambda: v.tensor_scalar(e1, pr, m1, None, ALU.is_equal),
                                    lambda: v.scalar_tensor_tensor(pp, e1, kb.fc(-1e30), pr,
                                                                   ALU.mult, ALU.add),
                                    lambda: v.reduce_max(m2, pp, axis=AX.X),
                                    lambda: v.tensor_scalar(e2, pp, m2, None, ALU.is_equal),
                                    lambda: v.tensor_add(den, m1, m2),
                                    lambda: v.tensor_scalar(den, den, kb.fc(1e-20), 2.0,
                                                            ALU.add, ALU.mult),
                                    lambda: v.reciprocal(rcp, den),
                                    lambda: v.tensor_mul(w1c, m1, rcp),
                                    lambda: v.tensor_mul(w2c, m2, rcp),
                                    lambda: v.tensor_scalar(g_tile[:, tg, :], e2, w2c,
                                                            None, ALU.mult),
                                    lambda: v.scalar_tensor_tensor(g_tile[:, tg, :], e1, w1c,
                                                                   g_tile[:, tg, :],
                                                                   ALU.mult, ALU.add),
                                ]
                            allops = [mk_ops(tg) for tg in range(4)]
                            for s_ in range(12):
                                for tg in range(4):
                                    if s_ >= 1:
                                        v.wait_ge(chv, bvr + (s_ - 1) * 4 + tg + 1)
                                    kb.chcnt["v"] += 1
                                    allops[tg][s_]().then_inc(chv, 1)
                    gts = st_rt.enter_context(nc.sbuf_tensor(kb.name("gts"), [8, 4, 128], F32))
                    with ExitStack() as st_pt:
                        ps_gt = st_pt.enter_context(nc.psum_tensor(kb.name("ps_gt"), [8, 4, 128], F32))
                        with nc.Block() as blk:
                            @blk.tensor
                            def _(t):
                                for tg in range(4):
                                    t.transpose(ps_gt[0:8, tg, :], g_tile[:, tg, 0:8], ident[:, :])
                        with nc.Block() as blk:
                            @blk.vector
                            def _(v):
                                v.tensor_copy(gts[:, :, :], ps_gt[0:8, :, :])
                    with nc.Block() as blk:
                        @blk.sync
                        def _(sync):
                            sync.wait_ge(S["lg"], C["lg"])
                            sync.dma_start(g_scr, gts[:, :, :].rearrange("e tg t -> e (tg t)")
                                           ).then_inc(S["lg"], 16)
                            C["lg"] += 16
                            sync.wait_ge(S["lg"], C["lg"])
                            sync.dma_start(gT[0:1, :, :],
                                           g_scr.rearrange("e t -> (e t)").rearrange("(o f) -> o f", o=1).rearrange("o (e t) -> o e t", e=8)
                                           ).then_inc(S["lg"], 16)
                            C["lg"] += 16
                    if dbg is not None and isl == 0:
                        with nc.Block() as blk:
                            @blk.sync
                            def _(sync):
                                kb.store(sync, dbg["probs"], probs[:, :, :].rearrange("p tg e -> p (tg e)"))
                                kb.store(sync, dbg["gT"], gT[0:1, :, :].rearrange("o e t -> o (e t)"))
                if skip_router:
                    pass
                if parts == "now1":
                    continue
                a_all = st_sl.enter_context(nc.sbuf_tensor(kb.name("a_all"), [128, 64, NSL], BF16))
                w1b = [st_sl.enter_context(nc.sbuf_tensor(kb.name("w1b"), [128, 8, 1024], BF16, side="right"))
                       for _ in range(2)]
                w2b = [st_sl.enter_context(nc.sbuf_tensor(kb.name("w2b"), [128, 8, 1024], BF16, side="right"))
                       for _ in range(2)]
                rbuf = st_sl.enter_context(nc.sbuf_tensor(kb.name("rbuf"), [128, NSL], F32))
                sqb = [st_sl.enter_context(nc.sbuf_tensor(kb.name("sqb"), [128, NSL], F32))
                       for _ in range(2)]
                with ExitStack() as st_ps, nc.Block() as blk:
                    hps = [st_ps.enter_context(nc.psum_tensor(kb.name("hps"), [128, NSL], F32))
                           for _ in range(2)]
                    gbps = [st_ps.enter_context(nc.psum_tensor(kb.name("gbps"), [128, NSL], F32))
                            for _ in range(2)]
                    b_ldw1 = [C["ldw1"], C["ldw1b"]]
                    b_lg = C["lg"]
                    b_mm1, b_rel, b_va, b_gb = (C["mm1"], C["rel"],
                                                C["va"], C["gb"])
                    w1sem = [S["ldw1"], S["ldw1b"]]

                    @blk.sync
                    def _(sync):
                        for e in range(8):
                            if e >= 2:
                                sync.wait_ge(S["mm1"], b_mm1 + 8 * (e - 1))
                            sync.dma_start(w1b[e % 2][:, :, :],
                                           w1bv[:, :, e * 1024:(e + 1) * 1024]
                                           ).then_inc(w1sem[e % 2], 16)

                    @blk.tensor
                    def _(t):
                        t.wait_ge(S["lg"], b_lg)
                        for e in range(8):
                            t.wait_ge(w1sem[e % 2], b_ldw1[e % 2] + 16 * (e // 2 + 1))
                            if e >= 2:
                                t.wait_ge(S["va"], b_va + 8 * (e - 1))
                            t.matmul(gbps[e % 2][:, :], onesrow[:, :], gT[0:1, e, :],
                                     start=True, stop=True).then_inc(S["gb"], 1)
                            for w in range(8):
                                g = e * 8 + w
                                if g >= 2:
                                    t.wait_ge(S["rel"], b_rel + g - 1)
                                for k in range(8):
                                    ins = t.matmul(hps[g % 2][:, :],
                                                   w1b[e % 2][:, k, w * 128:(w + 1) * 128],
                                                   xn[:, k, :],
                                                   start=(k == 0), stop=(k == 7))
                                ins.then_inc(S["mm1"], 1)

                    @blk.scalar
                    def _(s):
                        for e in range(8):
                            for w in range(8):
                                g = e * 8 + w
                                s.wait_ge(S["mm1"], b_mm1 + g + 1)
                                if g >= 2:
                                    s.wait_ge(S["va"], b_va + g - 1)
                                if g >= 1:
                                    s.wait_ge(S["rel"], b_rel + g)
                                kb.step("s", s, s.activation(rbuf[:, :], hps[g % 2][:, :], AF.Relu))
                                s.activation(sqb[g % 2][:, :], rbuf[:, :],
                                             AF.Square).then_inc(S["rel"], 1)

                    @blk.vector
                    def _(v):
                        for e in range(8):
                            v.wait_ge(S["gb"], b_gb + e + 1)
                            for w in range(8):
                                g = e * 8 + w
                                v.wait_ge(S["rel"], b_rel + g + 1)
                                v.tensor_mul(a_all[:, g, :], sqb[g % 2][:, :],
                                             gbps[e % 2][:, :]).then_inc(S["va"], 1)
                    C["ldw1"] += 16 * 4
                    C["ldw1b"] += 16 * 4
                    C["gb"] += 8
                    C["mm1"] += 64
                    C["rel"] += 64
                    C["va"] += 64
                if dbg is not None and isl == 0:
                    with nc.Block() as blk:
                        @blk.sync
                        def _(sync):
                            kb.store(sync, dbg["xn"], xn[:, :, :].rearrange("p c t -> p (c t)"))
                            kb.store(sync, dbg["a0"], a_all[:, 0:8, :].rearrange("p c t -> p (c t)"))
                if parts == "now2":
                    continue
                with ExitStack() as st_ps, nc.Block() as blk:
                    ops = st_ps.enter_context(nc.psum_tensor(kb.name("ops"), [128, 8, NSL], F32))
                    b_ldw2 = [C["ldw2"], C["ldw2b"]]
                    b_mm2, b_ev2 = C["mm2"], C["ev2"]
                    w2sem = [S["ldw2"], S["ldw2b"]]
                    do_add = parts in ("all", "w2nostore")
                    do_store = parts == "all"

                    @blk.sync
                    def _(sync):
                        for e in range(8):
                            if e >= 2:
                                sync.wait_ge(S["mm2"], b_mm2 + 8 * (e - 1))
                            sync.dma_start(w2b[e % 2][:, :, :],
                                           w2bv[:, e * 8:(e + 1) * 8, :]
                                           ).then_inc(w2sem[e % 2], 16)

                    @blk.tensor
                    def _(t):
                        for e in range(8):
                            t.wait_ge(w2sem[e % 2], b_ldw2[e % 2] + 16 * (e // 2 + 1))
                            for dout in range(8):
                                for k in range(8):
                                    ins = t.matmul(ops[:, dout, :],
                                                   w2b[e % 2][:, k, dout * 128:(dout + 1) * 128],
                                                   a_all[:, e * 8 + k, :],
                                                   start=(e == 0 and k == 0),
                                                   stop=(e == 7 and k == 7))
                                if dout == 7:
                                    ins.then_inc(S["mm2"], 8)

                    @blk.vector
                    def _(v):
                        if do_add:
                            v.wait_ge(S["mm2"], b_mm2 + 64)
                            for dout in range(8):
                                v.tensor_add(x_sl[:, dout, :], x_sl[:, dout, :],
                                             ops[:, dout, :]).then_inc(S["ev2"], 1)
                    C["ldw2"] += 16 * 4
                    C["ldw2b"] += 16 * 4
                    C["mm2"] += 64
                    C["ev2"] += 8 if do_add else 0
                if do_store:
                    with nc.Block() as blk:
                        @blk.sync
                        def _(sync):
                            kb.store(sync, xout_v[:, :, tsl], x_sl[:, :, :])


def build_moe_nc(debug=False, parts="all"):
    kb = KB()
    x_in = kb.din("x", [D, T], F32)
    w1 = kb.din("w1", [1024, 8192], F32)
    w2 = kb.din("w2", [8192, 1024], F32)
    rw = kb.din("rw", [8, 1024], F32)
    y = kb.dout("y", [D, T], F32)
    w1bf = kb.dscr("w1bf", [1024, 8192], BF16)
    w2bf = kb.dscr("w2bf", [8192, 1024], BF16)
    xres = kb.dscr("xres", [D, T], F32)
    dbg = None
    if debug:
        dbg = {"probs": kb.dout("d_probs", [128, 32], F32),
               "gT": kb.dout("d_gT", [1, 8 * 512], F32),
               "xn": kb.dout("d_xn", [128, 8 * 512], BF16),
               "a0": kb.dout("d_a0", [128, 8 * 512], BF16)}
    kb.init()
    emit_xin_copy(kb, x_in, xres)
    if parts in ("all", "pc"):
        emit_moe_preconvert(kb, w1, w2, w1bf, w2bf)
    if parts in ("all", "moe", "norouter", "now2", "now1"):
        ms = MoESems(kb)
        emit_moe(kb, ms, xres, y, w1bf, w2bf, rw, dbg=dbg, parts=parts)
    nc = kb.finalize()
    return nc, {"c_fvals": kb.fvals_array()}


def emit_attn_preconvert(kb, wqkv_hbm, wout_hbm, rope_hbm, wqkv_bf, wout_bf, rope_scr):
    nc = kb.nc
    qv = wqkv_hbm.rearrange("(c p) m -> p c m", p=128)
    qbv = wqkv_bf.rearrange("(c p) m -> p c m", p=128)
    ov = wout_hbm.rearrange("(c p) m -> p c m", p=128)
    obv = wout_bf.rearrange("(c p) m -> p c m", p=128)
    with ExitStack() as st:
        buf = st.enter_context(nc.sbuf_tensor(kb.name("pca"), [128, 8, 3072], BF16))
        fbuf = st.enter_context(nc.sbuf_tensor(kb.name("pcf"), [128, 4096], F32))
        ld = kb.sem("pca_ld")
        stc = kb.sem("pca_st")
        nld = nst = 0
        with nc.Block() as blk:
            @blk.gpsimd
            def _(g):
                nonlocal nld
                g.dma_start(buf[:, :, :], qv).then_inc(ld, 16)
                nld += 16
                g.wait_ge(ld, nld)
        with nc.Block() as blk:
            @blk.sync
            def _(sync):
                nonlocal nst
                sync.dma_start(qbv, buf[:, :, :]).then_inc(stc, 16)
                nst += 16
                sync.wait_ge(stc, nst)
        with nc.Block() as blk:
            @blk.gpsimd
            def _(g):
                nonlocal nld
                g.dma_start(buf[:, 0:8, 0:1024], ov).then_inc(ld, 16)
                nld += 16
                g.wait_ge(ld, nld)
        with nc.Block() as blk:
            @blk.sync
            def _(sync):
                nonlocal nst
                sync.dma_start(obv, buf[:, 0:8, 0:1024]).then_inc(stc, 16)
                nst += 16
                sync.wait_ge(stc, nst)
        for tb in range(4):
            with nc.Block() as blk:
                @blk.gpsimd
                def _(g):
                    nonlocal nld
                    g.dma_start(fbuf[:, :], rope_hbm[tb, :, :]).then_inc(ld, 16)
                    nld += 16
                    g.wait_ge(ld, nld)
            with nc.Block() as blk:
                @blk.sync
                def _(sync):
                    nonlocal nst
                    sync.dma_start(rope_scr[tb, :, :], fbuf[:, :]).then_inc(stc, 16)
                    nst += 16
                    sync.wait_ge(stc, nst)


class AttnSems:
    def __init__(self, kb):
        names = ["ldx", "ldxn", "ldxnb", "ldq", "ldqb", "ldw", "ldtab", "ldtabb",
                 "mm", "sev", "tswp",
                 "vrope", "ttr", "vcp", "smm", "sexp", "vmask", "tpv", "vrcp",
                 "trb", "vfin", "mmo", "evo"]
        self.s = {n: kb.sem("at_" + n) for n in names}
        self.c = {n: 0 for n in names}


def emit_attn(kb, asx, x_in_hbm, x_out_hbm, wqkv_bf, wout_bf, rope_scr,
              xn_scr, q_scr):
    nc = kb.nc
    S, C = asx.s, asx.c
    NSL = 512
    nslabs = T // NSL
    ident_bf = kb.const_sb["c_ident_bf"]
    onesrow = kb.const_sb["c_ones_row"]
    swpq = kb.const_sb["c_swpq"]
    dmask = kb.const_sb["c_diagmask"]
    xin_v = xview(x_in_hbm)
    xout_v = xview(x_out_hbm)
    xn_v = xn_scr.rearrange("(c p) t -> p c t", p=128)
    qbv = wqkv_bf.rearrange("(c p) m -> p c m", p=128)
    obv = wout_bf.rearrange("(c p) m -> p c m", p=128)
    q_v = q_scr.rearrange("pr p t -> p pr t")

    with ExitStack() as st_x2:
        x_sl2 = [st_x2.enter_context(nc.sbuf_tensor(kb.name("x_sl"), [128, 8, NSL], F32))
                 for _ in range(2)]
        base_ld = {"ld": kb.lncnt["ld"], "ld2": kb.lncnt["ld2"]}
        lpn = lambda s: "ld" if s % 2 == 0 else "ld2"
        for isl in range(nslabs):
            tsl = slice(isl * NSL, (isl + 1) * NSL)
            with ExitStack() as st:
                xn = st.enter_context(nc.sbuf_tensor(kb.name("xn"), [128, 8, NSL], BF16))
                scr = st.enter_context(nc.sbuf_tensor(kb.name("scr"), [128, 8, NSL], BF16))
                pf = []
                if isl + 1 < nslabs:
                    nts = slice((isl + 1) * NSL, (isl + 2) * NSL)
                    pf.append((x_sl2[(isl + 1) % 2][:, :, :], xin_v[:, :, nts],
                               lpn(isl + 1)))
                if isl == 0:
                    emit_ln_slab(kb, st, x_sl2[0], xn, scr, NSL,
                                 xsrc=xin_v[:, :, tsl], prefetch=pf)
                else:
                    emit_ln_slab(kb, st, x_sl2[isl % 2], xn, scr, NSL, prefetch=pf,
                                 ld_wait=(lpn(isl),
                                          base_ld[lpn(isl)] + 16 * (isl // 2 + 1)))
                with nc.Block() as blk:
                    @blk.sync
                    def _(sync):
                        kb.dma_ff(sync, xn_v[:, :, tsl], xn[:, :, :])

    with ExitStack() as st_kv:
        k_sb = st_kv.enter_context(nc.sbuf_tensor(kb.name("k_sb"), [128, 8, T], BF16, side="right"))
        vp_sb = st_kv.enter_context(nc.sbuf_tensor(kb.name("vp_sb"), [128, 32, 1040], BF16, side="right"))
        with nc.Block() as blk:
            @blk.vector
            def _(v):
                ones_view = vp_sb[:, :, :].rearrange("p st (h c) -> p st h c", c=65)[:, :, :, 64:65]
                v.memset(ones_view, 1.0)

        for phase in ("q", "k", "v"):
            with ExitStack() as st_ph:
                wpart = st_ph.enter_context(nc.sbuf_tensor(kb.name("wp"), [128, 8, 1024], BF16))
                xn2 = [st_ph.enter_context(nc.sbuf_tensor(kb.name("xns"), [128, 8, NSL], BF16))
                       for _ in range(2)]
                if phase in ("q", "k"):
                    tabs2 = [st_ph.enter_context(nc.sbuf_tensor(kb.name("tabs"), [128, 2, NSL], F32))
                             for _ in range(2)]
                part0 = {"q": 0, "k": 1, "v": 2}[phase]
                ti = {"q": 0, "k": 2, "v": 0}[phase]
                b_xn0 = {"ldxn": C["ldxn"], "ldxnb": C["ldxnb"]}
                b_tab0 = {"ldtab": C["ldtab"], "ldtabb": C["ldtabb"]}
                with nc.Block() as blk:
                    @blk.sync
                    def _(sync):
                        kb.ffw(sync)
                        sync.dma_start(wpart[:, :, :],
                                       qbv[:, :, part0 * 1024:(part0 + 1) * 1024]
                                       ).then_inc(S["ldw"], 16)
                        C["ldw"] += 16
                for isl in range(nslabs):
                    tsl = slice(isl * NSL, (isl + 1) * NSL)
                    xn = xn2[isl % 2]
                    if phase in ("q", "k"):
                        tabs = tabs2[isl % 2]
                    with ExitStack() as st_sl:
                        raw = [st_sl.enter_context(nc.sbuf_tensor(kb.name("raw"), [128, NSL], F32))
                               for _ in range(2)]
                        if phase in ("q", "k"):
                            t1b = st_sl.enter_context(nc.sbuf_tensor(kb.name("t1b"), [128, NSL], F32))
                            t2b = st_sl.enter_context(nc.sbuf_tensor(kb.name("t2b"), [128, NSL], F32))
                        if phase == "q":
                            qtmp = st_sl.enter_context(nc.sbuf_tensor(kb.name("qtmp"), [128, 8, NSL], BF16))
                        if phase == "v":
                            vtmp = [st_sl.enter_context(nc.sbuf_tensor(kb.name("vtmp"), [128, NSL], BF16))
                                    for _ in range(2)]
                        with ExitStack() as st_ps, nc.Block() as blk:
                            mps = [st_ps.enter_context(nc.psum_tensor(kb.name("mps"), [128, NSL], F32))
                                   for _ in range(2)]
                            if phase in ("q", "k"):
                                wps = [st_ps.enter_context(nc.psum_tensor(kb.name("wps"), [128, NSL], F32))
                                       for _ in range(2)]
                            else:
                                tps = [st_ps.enter_context(nc.psum_tensor(kb.name("tps"), [128, 128], BF16))
                                       for _ in range(2)]
                            b = {n: C[n] for n in ("ldxn", "ldtab", "mm", "sev",
                                                   "tswp", "vrope", "ttr", "vcp")}
                            ldw_now = C["ldw"]

                            @blk.sync
                            def _(sync):
                                kb.ffw(sync)
                                for lsl in ([isl, isl + 1] if isl == 0 else [isl + 1]):
                                    if lsl >= nslabs:
                                        continue
                                    lts = slice(lsl * NSL, (lsl + 1) * NSL)
                                    xs = "ldxn" if lsl % 2 == 0 else "ldxnb"
                                    sync.dma_start(xn2[lsl % 2][:, :, :],
                                                   xn_v[:, :, lts]).then_inc(S[xs], 16)
                                    C[xs] += 16
                                    if phase in ("q", "k"):
                                        ts_ = "ldtab" if lsl % 2 == 0 else "ldtabb"
                                        sync.dma_start(tabs2[lsl % 2][:, :, :],
                                                       rope_scr[ti:ti + 2, :, lts].rearrange("tb p t -> p tb t")
                                                       ).then_inc(S[ts_], 16)
                                        C[ts_] += 16

                            @blk.tensor
                            def _(t):
                                t.wait_ge(S["ldw"], ldw_now)
                                xs = "ldxn" if isl % 2 == 0 else "ldxnb"
                                t.wait_ge(S[xs], b_xn0[xs] + 16 * (isl // 2 + 1))
                                for pr in range(8):
                                    if pr >= 2:
                                        t.wait_ge(S["sev"], b["sev"] + pr - 1)
                                    for k in range(8):
                                        ins = t.matmul(mps[pr % 2][:, :],
                                                       wpart[:, k, pr * 128:(pr + 1) * 128],
                                                       xn[:, k, :],
                                                       start=(k == 0), stop=(k == 7))
                                    ins.then_inc(S["mm"], 1)
                                    if phase in ("q", "k"):
                                        t.wait_ge(S["sev"], b["sev"] + pr + 1)
                                        if pr >= 2:
                                            t.wait_ge(S["vrope"], b["vrope"] + pr - 1)
                                        t.matmul(wps[pr % 2][:, :], swpq[:, :], raw[pr % 2][:, :],
                                                 start=True, stop=True).then_inc(S["tswp"], 1)
                                    else:
                                        t.wait_ge(S["sev"], b["sev"] + pr + 1)
                                        for c4 in range(4):
                                            g = pr * 4 + c4
                                            if g >= 2:
                                                t.wait_ge(S["vcp"], b["vcp"] + 2 * (g - 1))
                                            t.transpose(tps[c4 % 2][:, :],
                                                        vtmp[pr % 2][:, c4 * 128:(c4 + 1) * 128],
                                                        ident_bf[:, :]).then_inc(S["ttr"], 1)

                            @blk.scalar
                            def _(s):
                                for pr in range(8):
                                    s.wait_ge(S["mm"], b["mm"] + pr + 1)
                                    if phase in ("q", "k"):
                                        if pr >= 2:
                                            s.wait_ge(S["tswp"], b["tswp"] + pr - 1)
                                            s.wait_ge(S["vrope"], b["vrope"] + pr - 1)
                                        s.activation(raw[pr % 2][:, :], mps[pr % 2][:, :],
                                                     AF.Copy).then_inc(S["sev"], 1)
                                    else:
                                        if pr >= 2:
                                            s.wait_ge(S["ttr"], b["ttr"] + 4 * (pr - 1))
                                        s.activation(vtmp[pr % 2][:, :], mps[pr % 2][:, :],
                                                     AF.Copy).then_inc(S["sev"], 1)

                            @blk.vector
                            def _(v):
                                kb.ffw(v)
                                if phase in ("q", "k"):
                                    ts_ = "ldtab" if isl % 2 == 0 else "ldtabb"
                                    v.wait_ge(S[ts_], b_tab0[ts_] + 16 * (isl // 2 + 1))
                                    for pr in range(8):
                                        v.wait_ge(S["tswp"], b["tswp"] + pr + 1)
                                        if pr >= 1:
                                            v.wait_ge(S["vrope"], b["vrope"] + pr)
                                        kb.step("v", v, v.tensor_mul(t1b[:, :], raw[pr % 2][:, :],
                                                                     tabs[:, 0, :]))
                                        kb.step("v", v, v.tensor_mul(t2b[:, :], wps[pr % 2][:, :],
                                                                     tabs[:, 1, :]))
                                        dst = qtmp[:, pr, :] if phase == "q" else k_sb[:, pr, tsl]
                                        v.tensor_add(dst, t1b[:, :], t2b[:, :]).then_inc(S["vrope"], 1)
                                else:
                                    for pr in range(8):
                                        for c4 in range(4):
                                            v.wait_ge(S["ttr"], b["ttr"] + pr * 4 + c4 + 1)
                                            stile = isl * 4 + c4
                                            hA, hB = 2 * pr, 2 * pr + 1
                                            v.tensor_copy(vp_sb[:, stile, hA * 65:hA * 65 + 64],
                                                          tps[c4 % 2][:, 0:64]).then_inc(S["vcp"], 1)
                                            v.tensor_copy(vp_sb[:, stile, hB * 65:hB * 65 + 64],
                                                          tps[c4 % 2][:, 64:128]).then_inc(S["vcp"], 1)

                            C["mm"] += 8
                            C["sev"] += 8
                            if phase in ("q", "k"):
                                C["tswp"] += 8
                                C["vrope"] += 8
                            else:
                                C["ttr"] += 32
                                C["vcp"] += 64
                        if phase == "q":
                            with nc.Block() as blk:
                                @blk.sync
                                def _(sync):
                                    kb.dma_ff(sync, q_v[:, :, tsl], qtmp[:, :, :])

        with ExitStack() as st_o:
            wout_sb = st_o.enter_context(nc.sbuf_tensor(kb.name("wout"), [128, 8, 1024], BF16))
            q_sl2 = [st_o.enter_context(nc.sbuf_tensor(kb.name("q_sl"), [128, 8, NSL], BF16))
                     for _ in range(2)]
            b_ldq0 = {"ldq": C["ldq"], "ldqb": C["ldqb"]}
            with nc.Block() as blk:
                @blk.sync
                def _(sync):
                    kb.ffw(sync)
                    sync.dma_start(wout_sb[:, :, :], obv).then_inc(S["ldw"], 16)
                    C["ldw"] += 16
            for isl in range(nslabs):
                tsl = slice(isl * NSL, (isl + 1) * NSL)
                jmax = 4 * (isl + 1)
                ndiag = min(4, jmax)
                jd0 = jmax - 4
                q_sl = q_sl2[isl % 2]
                with ExitStack() as st_sl:
                    x_sl = st_sl.enter_context(nc.sbuf_tensor(kb.name("x_slB"), [128, 8, NSL], F32))
                    out_sl = st_sl.enter_context(nc.sbuf_tensor(kb.name("out_sl"), [128, 8, NSL], BF16))
                    pt = st_sl.enter_context(nc.sbuf_tensor(kb.name("pt"), [128, 4, NSL], BF16))
                    rbsb = st_sl.enter_context(nc.sbuf_tensor(kb.name("rbsb"), [64, 2, NSL], F32))
                    rcps = st_sl.enter_context(nc.sbuf_tensor(kb.name("rcps"), [1, 2, NSL], F32))
                    with ExitStack() as st_ps, nc.Block() as blk:
                        spsb = [st_ps.enter_context(nc.psum_tensor(kb.name("sps"), [128, 2, NSL], F32))
                                for _ in range(2)]
                        oA = st_ps.enter_context(nc.psum_tensor(kb.name("oA"), [65, NSL], F32))
                        oB = st_ps.enter_context(nc.psum_tensor(kb.name("oB"), [65, NSL], F32))
                        rbA = st_ps.enter_context(nc.psum_tensor(kb.name("rbA"), [64, NSL], F32))
                        rbB = st_ps.enter_context(nc.psum_tensor(kb.name("rbB"), [64, NSL], F32))
                        b = {n: C[n] for n in ("smm", "sexp", "vmask", "tpv",
                                               "vrcp", "trb", "vfin", "ldq", "ldx")}

                        @blk.sync
                        def _(sync):
                            kb.ffw(sync)
                            for lsl in ([isl, isl + 1] if isl == 0 else [isl + 1]):
                                if lsl >= nslabs:
                                    continue
                                lts = slice(lsl * NSL, (lsl + 1) * NSL)
                                qs = "ldq" if lsl % 2 == 0 else "ldqb"
                                sync.dma_start(q_sl2[lsl % 2][:, :, :],
                                               q_v[:, :, lts]).then_inc(S[qs], 16)
                                C[qs] += 16
                            sync.wait_ge(kb.store_sem, kb.store_cnt)
                            sync.dma_start(x_sl[:, :, :], xin_v[:, :, tsl]).then_inc(S["ldx"], 16)
                        C["ldx"] += 16

                        def pt_ap(j, half):
                            return pt[:, (j % 2) * 2 + half, :]

                        def emit_pv(t, pr, j):
                            hA, hB = 2 * pr, 2 * pr + 1
                            jj = pr * jmax + j
                            if j == 0 and pr >= 1:
                                t.wait_ge(S["vfin"], b["vfin"] + pr)
                            if j >= jd0:
                                t.wait_ge(S["vmask"], b["vmask"] + pr * ndiag + (j - jd0) + 1)
                            else:
                                t.wait_ge(S["sexp"], b["sexp"] + jj + 1)
                            t.matmul(oA[:, :], vp_sb[:, j, hA * 65:(hA + 1) * 65],
                                     pt_ap(j, 0), start=(j == 0), stop=(j == jmax - 1))
                            t.matmul(oB[:, :], vp_sb[:, j, hB * 65:(hB + 1) * 65],
                                     pt_ap(j, 1), start=(j == 0),
                                     stop=(j == jmax - 1)).then_inc(S["tpv"], 1)

                        def emit_rb(t, p):
                            t.wait_ge(S["vrcp"], b["vrcp"] + p + 1)
                            t.matmul(rbA[:, :], onesrow[0:1, 0:64], rcps[0:1, 0, :],
                                     start=True, stop=True)
                            t.matmul(rbB[:, :], onesrow[0:1, 0:64], rcps[0:1, 1, :],
                                     start=True, stop=True).then_inc(S["trb"], 1)

                        @blk.tensor
                        def _(t):
                            qs = "ldq" if isl % 2 == 0 else "ldqb"
                            t.wait_ge(S[qs], b_ldq0[qs] + 16 * (isl // 2 + 1))
                            for pr in range(8):
                                for j in range(jmax):
                                    jj = pr * jmax + j
                                    if jj >= 2:
                                        t.wait_ge(S["sexp"], b["sexp"] + jj - 1)
                                    t.matmul(spsb[j % 2][:, 0, :],
                                             k_sb[0:64, pr, j * 128:(j + 1) * 128],
                                             q_sl[0:64, pr, :],
                                             start=True, stop=True,
                                             tile_position=(0, 0))
                                    t.matmul(spsb[j % 2][:, 1, :],
                                             k_sb[64:128, pr, j * 128:(j + 1) * 128],
                                             q_sl[64:128, pr, :],
                                             start=True, stop=True,
                                             tile_position=(64, 0)).then_inc(S["smm"], 1)
                                    if j == 1 and pr >= 1:
                                        emit_rb(t, pr - 1)
                                    if j >= 1:
                                        emit_pv(t, pr, j - 1)
                                emit_pv(t, pr, jmax - 1)
                            emit_rb(t, 7)

                        @blk.scalar
                        def _(s):
                            for pr in range(8):
                                for j in range(jmax):
                                    jj = pr * jmax + j
                                    s.wait_ge(S["smm"], b["smm"] + jj + 1)
                                    if jj >= 2:
                                        s.wait_ge(S["tpv"], b["tpv"] + jj - 1)
                                    s.activation(pt[:, (j % 2) * 2:(j % 2) * 2 + 2, :],
                                                 spsb[j % 2][:, :, :],
                                                 AF.Exp, bias=kb.fc(-24.0)).then_inc(S["sexp"], 1)

                        @blk.vector
                        def _(v):
                            for pr in range(8):
                                for jd in range(ndiag):
                                    j = jd0 + jd
                                    jj = pr * jmax + j
                                    v.wait_ge(S["sexp"], b["sexp"] + jj + 1)
                                    off = 384 - 128 * jd
                                    v.tensor_mul(pt_ap(j, 0), pt_ap(j, 0),
                                                 dmask[:, off:off + NSL])
                                    v.tensor_mul(pt_ap(j, 1), pt_ap(j, 1),
                                                 dmask[:, off:off + NSL]).then_inc(S["vmask"], 1)
                                v.wait_ge(S["tpv"], b["tpv"] + (pr + 1) * jmax)
                                kb.step("v", v, v.reciprocal(rcps[0:1, 0, :], oA[64:65, :]))
                                kb.step("v", v,
                                        v.reciprocal(rcps[0:1, 1, :], oB[64:65, :]))
                                v.engine_nop().then_inc(S["vrcp"], 1)
                                v.wait_ge(S["trb"], b["trb"] + pr + 1)
                                kb.step("v", v, v.tensor_copy(rbsb[:, 0, :], rbA[:, :]))
                                kb.step("v", v, v.tensor_copy(rbsb[:, 1, :], rbB[:, :]))
                                v.tensor_mul(out_sl[0:64, pr, :], oA[0:64, :], rbsb[:, 0, :])
                                v.tensor_mul(out_sl[64:128, pr, :], oB[0:64, :],
                                             rbsb[:, 1, :]).then_inc(S["vfin"], 1)

                        C["smm"] += 8 * jmax
                        C["sexp"] += 8 * jmax
                        C["vmask"] += 8 * ndiag
                        C["tpv"] += 8 * jmax
                        C["vrcp"] += 8
                        C["trb"] += 8
                        C["vfin"] += 8
                    with ExitStack() as st_ps, nc.Block() as blk:
                        ops = [st_ps.enter_context(nc.psum_tensor(kb.name("opj"), [128, NSL], F32))
                               for _ in range(2)]
                        b_mmo, b_evo = C["mmo"], C["evo"]
                        ldw_o = C["ldw"]

                        @blk.tensor
                        def _(t):
                            t.wait_ge(S["ldw"], ldw_o)
                            for dout in range(8):
                                if dout >= 2:
                                    t.wait_ge(S["evo"], b_evo + dout - 1)
                                for k in range(8):
                                    ins = t.matmul(ops[dout % 2][:, :],
                                                   wout_sb[:, k, dout * 128:(dout + 1) * 128],
                                                   out_sl[:, k, :],
                                                   start=(k == 0), stop=(k == 7))
                                ins.then_inc(S["mmo"], 1)

                        @blk.vector
                        def _(v):
                            v.wait_ge(S["ldx"], b["ldx"] + 16)
                            for dout in range(8):
                                v.wait_ge(S["mmo"], b_mmo + dout + 1)
                                v.tensor_add(x_sl[:, dout, :], x_sl[:, dout, :],
                                             ops[dout % 2][:, :]).then_inc(S["evo"], 1)
                        C["mmo"] += 8
                        C["evo"] += 8
                    with nc.Block() as blk:
                        @blk.sync
                        def _(sync):
                            kb.store(sync, xout_v[:, :, tsl], x_sl[:, :, :])


def build_attn_nc():
    kb = KB()
    x_in = kb.din("x", [D, T], F32)
    wqkv = kb.din("wqkv", [1024, 3072], F32)
    wout = kb.din("wout", [1024, 1024], F32)
    rope = kb.din("c_rope", [4, 128, T], F32)
    y = kb.dout("y", [D, T], F32)
    xres = kb.dscr("xres", [D, T], F32)
    wqkv_bf = kb.dscr("wqkvbf", [1024, 3072], BF16)
    wout_bf = kb.dscr("woutbf", [1024, 1024], BF16)
    rope_scr = kb.dscr("ropescr", [4, 128, T], F32)
    xn_scr = kb.dscr("xnscr", [D, T], BF16)
    q_scr = kb.dscr("qscr", [8, 128, T], BF16)
    kb.init()
    emit_xin_copy(kb, x_in, xres)
    emit_attn_preconvert(kb, wqkv, wout, rope, wqkv_bf, wout_bf, rope_scr)
    asx = AttnSems(kb)
    emit_attn(kb, asx, xres, y, wqkv_bf, wout_bf, rope_scr, xn_scr, q_scr)
    nc = kb.finalize()
    return nc, {"c_fvals": kb.fvals_array(), "c_rope": rope_tables()}


def emit_ssm_preconvert(kb, win_hbm, wout_hbm, nw_hbm, win_bf, wout2_bf):
    nc = kb.nc
    wiv = win_hbm.rearrange("(c p) m -> p c m", p=128)
    wibv = win_bf.rearrange("(c p) m -> p c m", p=128)
    with ExitStack() as st:
        buf = st.enter_context(nc.sbuf_tensor(kb.name("pcs"), [128, 4256], BF16))
        fb = st.enter_context(nc.sbuf_tensor(kb.name("pcsf"), [128, 1024], F32))
        nwc = st.enter_context(nc.sbuf_tensor(kb.name("nwc"), [128, 1], F32))
        ob = st.enter_context(nc.sbuf_tensor(kb.name("pcso"), [128, 1024], BF16))
        ld = kb.sem("pcs_ld")
        stc = kb.sem("pcs_st")
        nld = nst = 0
        for c in range(8):
            with nc.Block() as blk:
                @blk.gpsimd
                def _(g):
                    nonlocal nld
                    g.dma_start(buf[:, :], wiv[:, c, :]).then_inc(ld, 16)
                    nld += 16
                    g.wait_ge(ld, nld)
            with nc.Block() as blk:
                @blk.sync
                def _(sync):
                    nonlocal nst
                    sync.dma_start(wibv[:, c, :], buf[:, :]).then_inc(stc, 16)
                    nst += 16
                    sync.wait_ge(stc, nst)
        for c in range(16):
            with nc.Block() as blk:
                @blk.gpsimd
                def _(g):
                    nonlocal nld
                    g.dma_start(fb[:, :], wout_hbm[c * 128:(c + 1) * 128, :]).then_inc(ld, 16)
                    g.dma_start(nwc[:, :], nw_hbm[c * 128:(c + 1) * 128, :]).then_inc(ld, 16)
                    nld += 32
                    g.wait_ge(ld, nld)
            with nc.Block() as blk:
                @blk.vector
                def _(v):
                    v.tensor_scalar(ob[:, :], fb[:, :], nwc[:, :], None, ALU.mult)
            with nc.Block() as blk:
                @blk.sync
                def _(sync):
                    nonlocal nst
                    sync.dma_start(wout2_bf[c * 128:(c + 1) * 128, :], ob[:, :]).then_inc(stc, 16)
                    nst += 16
                    sync.wait_ge(stc, nst)


class SSMSems:
    def __init__(self, kb):
        names = ["ldx", "ldxn", "ldw", "ldbc", "ldbcb", "ldxs", "ldz", "ldf", "p1mm",
                 "p1ev", "tmsk", "tg", "sl", "slh", "vlu", "vma", "vsd", "typ", "vmg",
                 "tsw", "tsa", "vst", "ttr", "sfe", "mmo", "evo", "gat"]
        self.s = {n: kb.sem("ss_" + n) for n in names}
        self.c = {n: 0 for n in names}


def emit_ssm(kb, sx, x_in_hbm, x_out_hbm, win_bf, wout2_bf, dtb_hbm, alog_hbm,
             dsk_hbm, xn_scr, z_scr, xs_scr, bcd_scr, psf_scr, nch=32, php=9):
    nc = kb.nc
    S, C = sx.s, sx.c
    NSL = 512
    nslabs = T // NSL
    NCH = 32
    ident = kb.const_sb["c_ident_f32"]
    ident_bf = kb.const_sb["c_ident_bf"]
    onesrow = kb.const_sb["c_ones_row"]
    ut01 = kb.const_sb["c_ut01"]
    sl01 = kb.const_sb["c_sl01"]
    invfrep = kb.const_sb["c_invfrep"]
    slbig = kb.const_sb["c_slbig"]
    invf16 = kb.const_sb["c_invf16"]
    swpb = kb.const_sb["c_swpb"]
    xin_v = xview(x_in_hbm)
    xout_v = xview(x_out_hbm)
    xn_v = xn_scr.rearrange("(c p) t -> p c t", p=128)
    wibv = win_bf.rearrange("(c p) m -> p c m", p=128)
    wo2v = wout2_bf.rearrange("(c p) m -> p c m", p=128)
    z_v = z_scr.rearrange("(sl p) m -> p sl m", p=128)
    xs_v = xs_scr.rearrange("(sl p) m -> p sl m", p=128)
    bcd_v = bcd_scr.rearrange("(sl p) m -> p sl m", p=128)
    MAGIC = 12582912.0
    TWO_PI = float(2.0 * np.pi)
    HALF_PI = float(np.pi / 2.0)

    with ExitStack() as st_x2:
        x_sl2 = [st_x2.enter_context(nc.sbuf_tensor(kb.name("x_sl"), [128, 8, NSL], F32))
                 for _ in range(2)]
        base_ld = {"ld": kb.lncnt["ld"], "ld2": kb.lncnt["ld2"]}
        lpn = lambda s: "ld" if s % 2 == 0 else "ld2"
        for isl in range(nslabs):
            tsl = slice(isl * NSL, (isl + 1) * NSL)
            with ExitStack() as st:
                xn = st.enter_context(nc.sbuf_tensor(kb.name("xn"), [128, 8, NSL], BF16))
                scr = st.enter_context(nc.sbuf_tensor(kb.name("scr"), [128, 8, NSL], BF16))
                pf = []
                if isl + 1 < nslabs:
                    nts = slice((isl + 1) * NSL, (isl + 2) * NSL)
                    pf.append((x_sl2[(isl + 1) % 2][:, :, :], xin_v[:, :, nts],
                               lpn(isl + 1)))
                if isl == 0:
                    emit_ln_slab(kb, st, x_sl2[0], xn, scr, NSL,
                                 xsrc=xin_v[:, :, tsl], prefetch=pf)
                else:
                    emit_ln_slab(kb, st, x_sl2[isl % 2], xn, scr, NSL, prefetch=pf,
                                 ld_wait=(lpn(isl),
                                          base_ld[lpn(isl)] + 16 * (isl // 2 + 1)))
                with nc.Block() as blk:
                    @blk.sync
                    def _(sync):
                        kb.dma_ff(sync, xn_v[:, :, tsl], xn[:, :, :])

    with ExitStack() as st_w:
        wi = st_w.enter_context(nc.sbuf_tensor(kb.name("wi"), [128, 8, 4256], BF16))
        with nc.Block() as blk:
            @blk.sync
            def _(sync):
                kb.ffw(sync)
                sync.dma_start(wi[:, :, :], wibv).then_inc(S["ldw"], 16)
                C["ldw"] += 16
        for isl in range(nslabs):
            tsl = slice(isl * NSL, (isl + 1) * NSL)
            with ExitStack() as st_sl:
                xn = st_sl.enter_context(nc.sbuf_tensor(kb.name("xns"), [128, 8, NSL], BF16))
                z_sl = st_sl.enter_context(nc.sbuf_tensor(kb.name("z_sl"), [128, 4, 2048], BF16))
                xs_sl = st_sl.enter_context(nc.sbuf_tensor(kb.name("xs_sl"), [128, 4, 2048], BF16))
                bc_sl = st_sl.enter_context(nc.sbuf_tensor(kb.name("bc_sl"), [128, 4, 160], F32))
                with ExitStack() as st_ps, nc.Block() as blk:
                    pps = [st_ps.enter_context(nc.psum_tensor(kb.name("pps"), [128, 512], F32))
                           for _ in range(4)]
                    b_mm, b_ev, b_ldxn = C["p1mm"], C["p1ev"], C["ldxn"]
                    ldw_now = C["ldw"]

                    @blk.sync
                    def _(sync):
                        kb.ffw(sync)
                        sync.dma_start(xn[:, :, :], xn_v[:, :, tsl]).then_inc(S["ldxn"], 16)
                    C["ldxn"] += 16

                    @blk.tensor
                    def _(t):
                        t.wait_ge(S["ldw"], ldw_now)
                        t.wait_ge(S["ldxn"], b_ldxn + 16)
                        g = 0
                        for tg in range(4):
                            for cc in range(9):
                                ncols = 512 if cc < 8 else 160
                                if g >= 4:
                                    t.wait_ge(S["p1ev"], b_ev + g - 3)
                                for k in range(8):
                                    ins = t.matmul(
                                        pps[g % 4][:, 0:ncols],
                                        xn[:, k, tg * 128:(tg + 1) * 128],
                                        wi[:, k, cc * 512:cc * 512 + ncols],
                                        start=(k == 0), stop=(k == 7))
                                ins.then_inc(S["p1mm"], 1)
                                g += 1

                    @blk.scalar
                    def _(s):
                        kb.ffw(s)
                        g = 0
                        for tg in range(4):
                            for cc in range(9):
                                s.wait_ge(S["p1mm"], b_mm + g + 1)
                                p = pps[g % 4]
                                if cc < 4:
                                    s.activation(z_sl[:, tg, cc * 512:(cc + 1) * 512],
                                                 p[:, 0:512], AF.Copy).then_inc(S["p1ev"], 1)
                                elif cc < 8:
                                    s.activation(xs_sl[:, tg, (cc - 4) * 512:(cc - 3) * 512],
                                                 p[:, 0:512], AF.Copy).then_inc(S["p1ev"], 1)
                                else:
                                    for base in (0, 64):
                                        inr = p[:, base:base + 32].rearrange(
                                            "p (i two) -> p two i", two=2)
                                        outr = bc_sl[:, tg, base:base + 32].rearrange(
                                            "p (two i) -> p two i", two=2)
                                        s.activation(outr, inr, AF.Copy)
                                        s.activation(bc_sl[:, tg, base + 32:base + 64],
                                                     p[:, base + 32:base + 64], AF.Copy)
                                    s.activation(bc_sl[:, tg, 128:160], p[:, 128:160],
                                                 AF.Copy).then_inc(S["p1ev"], 1)
                                g += 1
                    C["p1mm"] += 36
                    C["p1ev"] += 36
                with nc.Block() as blk:
                    @blk.sync
                    def _(sync):
                        kb.dma_ff(sync, z_v[:, isl * 4:(isl + 1) * 4, :], z_sl[:, :, :])
                        kb.dma_ff(sync, xs_v[:, isl * 4:(isl + 1) * 4, :], xs_sl[:, :, :])
                        kb.dma_ff(sync, bcd_v[:, isl * 4:(isl + 1) * 4, :], bc_sl[:, :, :])

    with ExitStack() as st_p2:
        sb = lambda nm, shp, dt, side="left": st_p2.enter_context(
            nc.sbuf_tensor(kb.name(nm), shp, dt, side=side))
        state = sb("state", [64, 2048], F32)
        state_bf = sb("statebf", [128, 2048], BF16)
        wo2 = sb("wo2", [128, 16, 1024], BF16, side="right")
        dtb_b = sb("dtb_b", [128, 32], F32)
        a_b = sb("a_b", [128, 32], F32)
        DI = sb("DI", [128, 32, 128], BF16, side="right")
        rows = sb("rows", [1, 128], F32)
        y_stage = sb("ystage", [128, 4, 2048], F32, side="right")
        bcd2 = [sb("bcd", [128, 160], F32) for _ in range(2)]
        xs_c = sb("xs_c", [128, 2048], BF16)
        dt_t = sb("dt_t", [128, 32], F32)
        tA = sb("tA", [128, 32], F32)
        tB = sb("tB", [128, 32], F32)
        psi = sb("psi", [128, 32], F32)
        psihat = sb("psihat", [128, 32], F32)
        lam = sb("lam", [128, 32], F32)
        bdec = sb("bdec", [128, 32], F32)
        phi = sb("phi", [128, 32], F32)
        psf = sb("psf", [32, 128], F32)
        psrow = sb("psrow", [1, 2048], F32)
        drow = sb("drow", [1, 64], F32)
        bt_f = sb("bt_f", [128, 2048], F32)
        ct_f = sb("ct_f", [128, 2048], F32)
        bt_bf = sb("bt_bf", [128, 2048], BF16)
        ct_bf = sb("ct_bf", [128, 2048], BF16)
        bhat = sb("bhat", [128, 2048], BF16)
        btf_feat = sb("btff", [128, 16, 128], BF16)
        ctf_feat = sb("ctff", [128, 16, 128], BF16)
        lm = sb("lm", [128, 2, 128], F32)
        lmu = sb("lmu", [128, 2, 128], F32)
        sgf = sb("sgf", [128, 2, 128], F32)
        spp = sb("spp", [128, 32, 128], BF16)
        tp16 = sb("tp16", [32, 128], F32)
        ccol_t = sb("ccol_t", [64, 32], F32)
        scol_t = sb("scol_t", [64, 32], F32)
        dec64 = sb("dec64", [64, 32], F32)
        u_bf = sb("u_bf", [64, 2048], BF16)
        y_mg = ct_f
        gsq = bt_f
        z_g = sb("z_g", [128, 4, 2048], BF16, side="right")
        x_g = sb("x_g", [128, 8, NSL], F32, side="right")

        def rep_ap(tile, pcount, pstride, nmid, last_count):
            return bass.AP(tile[:, :].tensor, 0,
                           [[pstride, pcount], [1, nmid], [0, last_count]])

        with nc.Block() as blk:
            @blk.sync
            def _(sync):
                kb.ffw(sync)
                sync.dma_start(rows[0:1, 0:32], dtb_hbm).then_inc(S["ldf"], 16)
                sync.dma_start(rows[0:1, 32:64], alog_hbm).then_inc(S["ldf"], 16)
                sync.dma_start(rows[0:1, 64:96], dsk_hbm).then_inc(S["ldf"], 16)
                sync.dma_start(wo2[:, :, :], wo2v).then_inc(S["ldw"], 16)
                C["ldf"] += 48
                C["ldw"] += 16

            @blk.vector
            def _(v):
                v.memset(state[:, :], 0.0)
                v.memset(state_bf[:, :], 0.0)
        ldw_wo2 = C["ldw"]
        with nc.Block() as blk:
            @blk.scalar
            def _(s):
                s.wait_ge(S["ldf"], C["ldf"])
                s.activation(rows[0:1, 32:64], rows[0:1, 32:64], AF.Exp)
        with ExitStack() as st_ps:
            pb = st_ps.enter_context(nc.psum_tensor(kb.name("pb"), [128, 96], F32))
            with nc.Block() as blk:
                @blk.tensor
                def _(t):
                    t.matmul(pb[:, :], onesrow[:, :], rows[0:1, 0:96], start=True, stop=True)
            with nc.Block() as blk:
                @blk.vector
                def _(v):
                    v.tensor_copy(dtb_b[:, :], pb[:, 0:32])
                    v.tensor_copy(a_b[:, :], pb[:, 32:64])
                    for h in range(32):
                        v.tensor_scalar(DI[:, h, :], ident_bf[:, :], pb[:, 64 + h:65 + h],
                                        None, ALU.mult)

        b_bc0 = {"ldbc": C["ldbc"], "ldbcb": C["ldbcb"]}
        for ch in range(nch):
            zg = ch // 4
            cg = ch % 4
            bcd = bcd2[ch % 2]
            with ExitStack() as st_ps:
                pphi = st_ps.enter_context(nc.psum_tensor(kb.name("pphi"), [128, 32], F32))
                ppsi = st_ps.enter_context(nc.psum_tensor(kb.name("ppsi"), [128, 32], F32))
                pprv = st_ps.enter_context(nc.psum_tensor(kb.name("pprv"), [128, 32], F32))
                ppft = st_ps.enter_context(nc.psum_tensor(kb.name("ppft"), [32, 128], F32))
                pft2 = st_ps.enter_context(nc.psum_tensor(kb.name("pft2"), [32, 128], F32))
                pcc = st_ps.enter_context(nc.psum_tensor(kb.name("pcc"), [128, 32], F32))
                pdd = st_ps.enter_context(nc.psum_tensor(kb.name("pdd"), [64, 32], F32))
                bv0, bs0 = kb.chcnt["v"], kb.chcnt["s"]
                b_t2 = C["typ"]
                b_ff = kb.ff_cnt

                with nc.Block() as blk:
                    @blk.sync
                    def _(sync):
                        kb.ffw(sync)
                        for lch in ([ch, ch + 1] if ch == 0 else [ch + 1]):
                            if lch < nch:
                                bs_ = "ldbc" if lch % 2 == 0 else "ldbcb"
                                sync.dma_start(bcd2[lch % 2][:, :],
                                               bcd_v[:, lch, :]).then_inc(S[bs_], 16)
                                C[bs_] += 16
                        sync.dma_start(xs_c[:, :], xs_v[:, ch, :]).then_inc(S["ldxs"], 16)
                        C["ldxs"] += 16
                        if cg == 0:
                            sync.dma_start(z_g[:, :, :], z_v[:, zg * 4:(zg + 1) * 4, :]
                                           ).then_inc(S["ldz"], 16)
                            sync.wait_ge(kb.store_sem, kb.store_cnt)
                            sync.dma_start(x_g[:, :, :],
                                           xin_v[:, :, zg * 512:(zg + 1) * 512]
                                           ).then_inc(S["ldz"], 16)
                            C["ldz"] += 32
                        sync.wait_ge(kb.chsem["v"], bv0 + 10)
                        kb.dma_ff(sync, psf_scr[0:32, :], psf[:, :])
                        sync.wait_ge(kb.ff_sem, kb.ff_cnt)
                        psfl = psf_scr.rearrange("a b -> (a b)").rearrange(
                            "(o f) -> o f", o=1)
                        sync.dma_start(psrow[0:1, :], psfl[0:1, 0:2048]
                                       ).then_inc(S["slh"], 16)
                    tot_ldz = C["ldz"]
                    C["slh"] += 16

                    @blk.vector
                    def _(v):
                        bs_ = "ldbc" if ch % 2 == 0 else "ldbcb"
                        v.wait_ge(S[bs_], b_bc0[bs_] + 16 * (ch // 2 + 1))
                        if cg == 0:
                            v.wait_ge(S["ldz"], tot_ldz)
                        kb.step("v", v, v.tensor_add(tA[:, :], bcd[:, 128:160], dtb_b[:, :]))
                        v.wait_ge(kb.chsem["s"], bs0 + 3)
                        kb.step("v", v, v.tensor_scalar_max(tA[:, :], tA[:, :], 0.0))
                        kb.step("v", v, v.tensor_add(dt_t[:, :], tA[:, :], tB[:, :]))
                        kb.step("v", v, v.tensor_mul(tA[:, :], dt_t[:, :], a_b[:, :]))
                        v.wait_ge(S["typ"], b_t2 + 1)
                        kb.step("v", v, v.tensor_copy(phi[:, :], pphi[:, :]))
                        kb.step("v", v, v.tensor_copy(psi[:, :], ppsi[:, :]))
                        v.wait_ge(kb.chsem["s"], bs0 + 4)
                        kb.step("v", v, v.tensor_add(psihat[:, :], ppsi[:, :], tB[:, :]))
                        kb.step("v", v, v.tensor_sub(bdec[:, :], tB[:, :], pprv[:, :]))
                        kb.step("v", v, v.tensor_add(drow[0:1, 0:32], pprv[0:1, :], tA[0:1, :]))
                        v.wait_ge(S["typ"], b_t2 + 2)
                        v.wait_ge(kb.ff_sem, b_ff)
                        kb.step("v", v, v.tensor_copy(psf[:, :], ppft[0:32, :]))
                        kb.step("v", v, v.tensor_scalar(tp16[:, 0:16], invf16[:, :],
                                                        pft2[0:32, 127:128], None, ALU.mult))
                        kb.step("v", v, v.tensor_scalar_mul(tp16[:, 96:112], tp16[:, 0:16],
                                                            kb.fc(1.0 / TWO_PI)))
                        kb.step("v", v, v.tensor_scalar_add(tp16[:, 96:112], tp16[:, 96:112],
                                                            kb.fc(MAGIC)))
                        kb.step("v", v, v.tensor_scalar_sub(tp16[:, 96:112], tp16[:, 96:112],
                                                            kb.fc(MAGIC)))
                        kb.step("v", v, v.scalar_tensor_tensor(tp16[:, 0:16], tp16[:, 96:112],
                                                               kb.fc(-TWO_PI), tp16[:, 0:16],
                                                               ALU.mult, ALU.add))
                        v.wait_ge(kb.chsem["s"], bs0 + 10)
                        kb.step("v", v, v.tensor_copy(tp16[:, 16:32], tp16[:, 0:16]))
                        kb.step("v", v, v.tensor_copy(tp16[:, 80:96], tp16[:, 64:80]))
                        kb.step("v", v, v.memset(tp16[:, 32:64], 1.0))
                        kb.step("v", v, v.memset(tp16[:, 96:128], 0.0))
                        v.wait_ge(S["typ"], b_t2 + 3)
                        kb.step("v", v, v.tensor_copy(ccol_t[:, :], pcc[0:64, :]))
                        kb.step("v", v, v.tensor_copy(scol_t[:, :], pcc[64:128, :]))
                        kb.step("v", v, v.tensor_copy(dec64[:, :], pdd[:, :]))

                    @blk.scalar
                    def _(s):
                        s.wait_ge(kb.chsem["v"], bv0 + 1)
                        kb.step("s", s, s.activation(tB[:, :], tA[:, :], AF.Abs))
                        kb.step("s", s, s.activation(tB[:, :], tB[:, :], AF.Exp, scale=-1.0))
                        kb.step("s", s, s.activation(tB[:, :], tB[:, :], AF.Ln, bias=1.0))
                        s.wait_ge(kb.chsem["v"], bv0 + 3)
                        kb.step("s", s, s.activation(tB[:, :], dt_t[:, :], AF.Ln))
                        s.wait_ge(kb.chsem["v"], bv0 + 8)
                        kb.step("s", s, s.activation(bdec[:, :], bdec[:, :], AF.Exp))
                        kb.step("s", s, s.activation(lam[:, :], psi[:, :], AF.Exp, scale=-1.0))
                        s.wait_ge(kb.chsem["v"], bv0 + 9)
                        kb.step("s", s, s.activation(drow[0:1, 0:32], drow[0:1, 0:32],
                                                     AF.Exp, scale=-1.0))
                        s.wait_ge(kb.chsem["v"], bv0 + 15)
                        kb.step("s", s, s.activation(tp16[:, 64:80], tp16[:, 0:16], AF.Sin))
                        kb.step("s", s, s.activation(tp16[:, 96:112], tp16[:, 0:16], AF.Abs))
                        kb.step("s", s, s.activation(tp16[:, 0:16], tp16[:, 96:112], AF.Sin,
                                                     scale=-1.0, bias=kb.fc(HALF_PI)))

                    @blk.tensor
                    def _(t):
                        t.wait_ge(kb.chsem["v"], bv0 + 4)
                        t.matmul(pphi[:, :], ut01[:, :], dt_t[:, :], start=True, stop=True)
                        t.matmul(ppsi[:, :], ut01[:, :], tA[:, :], start=True, stop=True)
                        t.matmul(pprv[:, :], sl01[:, :], tA[:, :],
                                 start=True, stop=True).then_inc(S["typ"], 1)
                        t.wait_ge(kb.chsem["v"], bv0 + 6)
                        t.transpose(ppft[:, :], psi[:, :], ident[:, :])
                        t.transpose(pft2[:, :], phi[:, :],
                                    ident[:, :]).then_inc(S["typ"], 1)
                        t.wait_ge(kb.chsem["v"], bv0 + 19)
                        t.wait_ge(kb.chsem["s"], bs0 + 7)
                        t.transpose(pcc[:, :], tp16[:, :], ident[0:32, 0:32])
                        t.matmul(pdd[:, :], onesrow[0:1, 0:64], drow[0:1, 0:32],
                                 start=True, stop=True).then_inc(S["typ"], 1)
                    C["typ"] += 3

            if php <= 1:
                continue
            st_ph3 = ExitStack()
            angs = st_ph3.enter_context(nc.sbuf_tensor(kb.name("angs"), [128, 512], F32))
            ca = st_ph3.enter_context(nc.sbuf_tensor(kb.name("ca"), [128, 512], F32))
            sa2 = st_ph3.enter_context(nc.sbuf_tensor(kb.name("sa2"), [128, 512], F32))
            vt1 = st_ph3.enter_context(nc.sbuf_tensor(kb.name("vt1"), [128, 512], F32))
            with nc.Block() as blk:
                @blk.vector
                def _(v):
                    ph_b = rep_ap(phi, 128, 32, 32, 16)
                    kb.step("v", v, v.tensor_mul(angs[:, :], ph_b, invfrep[:, :]))
                    kb.step("v", v, v.tensor_scalar_mul(vt1[:, :], angs[:, :],
                                                        kb.fc(1.0 / TWO_PI)))
                    kb.step("v", v, v.tensor_scalar_add(vt1[:, :], vt1[:, :], kb.fc(MAGIC)))
                    kb.step("v", v, v.tensor_scalar_sub(vt1[:, :], vt1[:, :], kb.fc(MAGIC)))
                    kb.step("v", v, v.scalar_tensor_tensor(angs[:, :], vt1[:, :],
                                                           kb.fc(-TWO_PI), angs[:, :],
                                                           ALU.mult, ALU.add))

                @blk.scalar
                def _(s):
                    s.wait_ge(kb.chsem["v"], kb.chcnt["v"])
                    kb.step("s", s, s.activation(sa2[:, :], angs[:, :], AF.Sin))
                    kb.step("s", s, s.activation(ca[:, :], angs[:, :], AF.Abs))
                    kb.step("s", s, s.activation(ca[:, :], ca[:, :], AF.Sin,
                                                 scale=-1.0, bias=kb.fc(HALF_PI)))
            with nc.Block() as blk:
                @blk.vector
                def _(v):
                    th = bcd[:, :].tensor
                    for base, dstf in ((0, bt_f), (64, ct_f)):
                        Be = bass.AP(th, base, [[160, 128], [0, 32], [1, 16]])
                        Bo = bass.AP(th, base + 16, [[160, 128], [0, 32], [1, 16]])
                        Br = bass.AP(th, base + 32, [[160, 128], [0, 32], [1, 32]])
                        dv = dstf[:, :].rearrange("p (h n) -> p h n", n=64)
                        dste, dsto, dstr = dv[:, :, 0:16], dv[:, :, 16:32], dv[:, :, 32:64]
                        kb.step("v", v, v.tensor_mul(vt1[:, :], Be, ca[:, :]))
                        kb.step("v", v, v.tensor_mul(angs[:, :], Bo, sa2[:, :]))
                        kb.step("v", v, v.tensor_add(dste, vt1[:, :], angs[:, :]))
                        kb.step("v", v, v.tensor_mul(vt1[:, :], Bo, ca[:, :]))
                        kb.step("v", v, v.tensor_mul(angs[:, :], Be, sa2[:, :]))
                        kb.step("v", v, v.tensor_sub(dsto, vt1[:, :], angs[:, :]))
                        kb.step("v", v, v.tensor_copy(dstr, Br))
                    bd_rep = rep_ap(bdec, 128, 32, 32, 64)
                    kb.step("v", v, v.tensor_copy(bt_bf[:, :], bt_f[:, :]))
                    kb.step("v", v, v.tensor_copy(ct_bf[:, :], ct_f[:, :]))
                    kb.step("v", v, v.tensor_mul(
                        bhat[:, :].rearrange("p (h n) -> p h n", n=64),
                        bt_f[:, :].rearrange("p (h n) -> p h n", n=64), bd_rep))
            st_ph3.close()
            with ExitStack() as st_ps:
                ptr = [st_ps.enter_context(nc.psum_tensor(kb.name("ptr"), [128, 128], BF16))
                       for _ in range(6)]
                b_ttr, b_sfe = C["ttr"], C["sfe"]
                with nc.Block() as blk:
                    @blk.tensor
                    def _(t):
                        g = 0
                        for srcb, dstf in ((bt_bf, btf_feat), (ct_bf, ctf_feat)):
                            for pt in range(16):
                                if g >= 6:
                                    t.wait_ge(S["sfe"], b_sfe + g - 5)
                                t.transpose(ptr[g % 6][:, :],
                                            srcb[:, (g % 16) * 128:(g % 16 + 1) * 128],
                                            ident_bf[:, :]).then_inc(S["ttr"], 1)
                                g += 1

                    @blk.scalar
                    def _(s):
                        g = 0
                        for srcb, dstf in ((bt_bf, btf_feat), (ct_bf, ctf_feat)):
                            for pt in range(16):
                                s.wait_ge(S["ttr"], b_ttr + g + 1)
                                s.activation(dstf[:, pt, :], ptr[g % 6][:, :],
                                             AF.Copy).then_inc(S["sfe"], 1)
                                g += 1
                    C["ttr"] += 32
                    C["sfe"] += 32

            if php <= 2:
                continue
            with ExitStack() as st_ps:
                pm = [st_ps.enter_context(nc.psum_tensor(kb.name("pm"), [128, 128], F32))
                      for _ in range(2)]
                pg = [st_ps.enter_context(nc.psum_tensor(kb.name("pg"), [128, 128], F32))
                      for _ in range(2)]
                b_tmsk, b_tg, b_sl2, b_vlu = C["tmsk"], C["tg"], C["sl"], C["vlu"]
                b_slh, b_vma = C["slh"], C["vma"]
                with nc.Block() as blk:
                    @blk.sync
                    def _(sync):
                        sync.wait_ge(S["tmsk"], b_tmsk + 16)
                        psfl = psf_scr.rearrange("a b -> (a b)").rearrange(
                            "(o f) -> o f", o=1)
                        sync.dma_start(psrow[0:1, :], psfl[0:1, 2048:4096]
                                       ).then_inc(S["slh"], 16)
                    C["slh"] += 16

                    @blk.tensor
                    def _(t):
                        for h in range(32):
                            t.wait_ge(S["slh"], b_slh - 16 + (16 if h < 16 else 32))
                            if h >= 2:
                                t.wait_ge(S["vma"], b_vma + h - 1)
                            t.matmul(pm[h % 2][:, :], onesrow[:, :],
                                     psrow[0:1, (h % 16) * 128:(h % 16 + 1) * 128],
                                     start=True, stop=True).then_inc(S["tmsk"], 1)
                            if h >= 2:
                                t.wait_ge(S["vlu"], b_vlu + h - 1)
                            pt_i = h // 2
                            r0 = (h % 2) * 64
                            t.matmul(pg[h % 2][:, :],
                                     btf_feat[r0:r0 + 64, pt_i, :],
                                     ctf_feat[r0:r0 + 64, pt_i, :],
                                     start=True, stop=True).then_inc(S["tg"], 1)

                    @blk.scalar
                    def _(s):
                        for h in range(32):
                            s.wait_ge(S["vma"], b_vma + h + 1)
                            if h >= 2:
                                s.wait_ge(S["vlu"], b_vlu + h - 1)
                            s.activation(lm[:, h % 2, :], lmu[:, h % 2, :], AF.Exp,
                                         scale=-1.0, bias=psihat[:, h:h + 1]
                                         ).then_inc(S["sl"], 1)

                    @blk.vector
                    def _(v):
                        for h in range(33):
                            if h < 32:
                                v.wait_ge(S["tmsk"], b_tmsk + h + 1)
                                if h >= 2:
                                    v.wait_ge(S["sl"], b_sl2 + h - 1)
                                v.tensor_add(lmu[:, h % 2, :], pm[h % 2][:, :],
                                             slbig[:, :]).then_inc(S["vma"], 1)
                            if h >= 1:
                                hp = h - 1
                                v.wait_ge(S["sl"], b_sl2 + hp + 1)
                                v.wait_ge(S["tg"], b_tg + hp + 1)
                                kb.step("v", v, v.tensor_mul(sgf[:, hp % 2, :], pg[hp % 2][:, :],
                                                             lm[:, hp % 2, :]))
                                v.tensor_add(spp[:, hp, :], sgf[:, hp % 2, :],
                                             DI[:, hp, :]).then_inc(S["vlu"], 1)
                C["tmsk"] += 32
                C["tg"] += 32
                C["sl"] += 32
                C["vma"] += 32
                C["vlu"] += 32

            if php <= 3:
                continue
            with ExitStack() as st_ps:
                yin = st_ps.enter_context(nc.psum_tensor(kb.name("yin"), [128, 2048], F32))
                yint = st_ps.enter_context(nc.psum_tensor(kb.name("yint"), [128, 2048], F32))
                with nc.Block() as blk:
                    @blk.tensor
                    def _(t):
                        t.wait_ge(S["ldxs"], C["ldxs"])
                        for pt in range(16):
                            hA, hB = 2 * pt, 2 * pt + 1
                            t.matmul(yint[:, hA * 64:(hA + 1) * 64],
                                     ctf_feat[0:64, pt, :],
                                     state_bf[0:64, hA * 64:(hA + 1) * 64],
                                     start=True, stop=True, tile_position=(0, 0))
                            t.matmul(yin[:, hA * 64:(hA + 1) * 64],
                                     spp[:, hA, :], xs_c[:, hA * 64:(hA + 1) * 64],
                                     start=True, stop=True)
                            t.matmul(yint[:, hB * 64:(hB + 1) * 64],
                                     ctf_feat[64:128, pt, :],
                                     state_bf[64:128, hB * 64:(hB + 1) * 64],
                                     start=True, stop=True, tile_position=(64, 0))
                            t.matmul(yin[:, hB * 64:(hB + 1) * 64],
                                     spp[:, hB, :], xs_c[:, hB * 64:(hB + 1) * 64],
                                     start=True, stop=True)
                with nc.Block() as blk:
                    @blk.vector
                    def _(v):
                        lam_rep = rep_ap(lam, 128, 32, 32, 64)
                        kb.step("v", v, v.tensor_mul(
                            y_mg[:, :].rearrange("p (h n) -> p h n", n=64),
                            yint[:, :].rearrange("p (h n) -> p h n", n=64), lam_rep))
                        kb.step("v", v, v.tensor_add(y_stage[:, cg, :], y_mg[:, :],
                                                     yin[:, :]))
            if php <= 4:
                continue
            st_ph6 = ExitStack()
            u_all = st_ph6.enter_context(nc.sbuf_tensor(kb.name("u_all"), [64, 2048], F32))
            w1s = st_ph6.enter_context(nc.sbuf_tensor(kb.name("w1s"), [64, 2048], F32))
            with ExitStack() as st_ps:
                psa = st_ps.enter_context(nc.psum_tensor(kb.name("psa"), [64, 2048], F32))
                with nc.Block() as blk:
                    @blk.tensor
                    def _(t):
                        for h in range(32):
                            t.matmul(psa[:, h * 64:(h + 1) * 64],
                                     bhat[:, h * 64:(h + 1) * 64],
                                     xs_c[:, h * 64:(h + 1) * 64],
                                     start=True, stop=True)
                with nc.Block() as blk:
                    @blk.vector
                    def _(v):
                        dec_rep = rep_ap(dec64, 64, 32, 32, 64)
                        kb.step("v", v, v.tensor_mul(
                            u_all[:, :].rearrange("p (h n) -> p h n", n=64),
                            state[:, :].rearrange("p (h n) -> p h n", n=64), dec_rep))
                        kb.step("v", v, v.tensor_add(u_all[:, :], u_all[:, :], psa[:, :]))
                        kb.step("v", v, v.tensor_copy(u_bf[:, :], u_all[:, :]))
            with ExitStack() as st_ps:
                psw = st_ps.enter_context(nc.psum_tensor(kb.name("psw"), [64, 2048], F32))
                with nc.Block() as blk:
                    @blk.tensor
                    def _(t):
                        for h in range(32):
                            t.matmul(psw[:, h * 64:(h + 1) * 64], swpb[:, :],
                                     u_bf[:, h * 64:(h + 1) * 64], start=True, stop=True)
                with nc.Block() as blk:
                    @blk.vector
                    def _(v):
                        cc_rep = rep_ap(ccol_t, 64, 32, 32, 64)
                        sc_rep = rep_ap(scol_t, 64, 32, 32, 64)
                        kb.step("v", v, v.tensor_mul(
                            w1s[:, :].rearrange("p (h n) -> p h n", n=64),
                            u_all[:, :].rearrange("p (h n) -> p h n", n=64), cc_rep))
                        kb.step("v", v, v.tensor_mul(
                            u_all[:, :].rearrange("p (h n) -> p h n", n=64),
                            psw[:, :].rearrange("p (h n) -> p h n", n=64), sc_rep))
                        kb.step("v", v, v.tensor_add(state[:, :], w1s[:, :], u_all[:, :]))
                        kb.step("v", v, v.tensor_copy(state_bf[0:64, :], state[:, :]))
                        kb.step("v", v, v.tensor_copy(state_bf[64:128, :], state[:, :]))
            st_ph6.close()

            if cg == 3:
                with ExitStack() as st_ps:
                    gz = st_ps.enter_context(nc.sbuf_tensor(kb.name("gz"), [128, 2048], F32))
                    gn_bf = st_ps.enter_context(nc.sbuf_tensor(kb.name("gn_bf"), [128, 2048], BF16))
                    gnf = st_ps.enter_context(nc.sbuf_tensor(kb.name("gnf"), [128, 16, 128], BF16))
                    rcol = st_ps.enter_context(nc.sbuf_tensor(kb.name("rcol"), [128, 2], F32))
                    ptg = [st_ps.enter_context(nc.psum_tensor(kb.name("ptg"), [128, 128], BF16))
                           for _ in range(4)]
                    pout = [st_ps.enter_context(nc.psum_tensor(kb.name("pout"), [128, 128], F32))
                            for _ in range(2)]
                    for cc2 in range(4):
                        with nc.Block() as blk:
                            @blk.scalar
                            def _(s):
                                s.wait_ge(S["ldz"], C["ldz"])
                                kb.step("s", s, s.activation(gz[:, :], z_g[:, cc2, :], AF.Silu))

                            @blk.vector
                            def _(v):
                                v.wait_ge(kb.chsem["s"], kb.chcnt["s"])
                                kb.step("v", v, v.tensor_mul(gz[:, :], gz[:, :],
                                                             y_stage[:, cc2, :]))
                        with nc.Block() as blk:
                            @blk.scalar
                            def _(s):
                                kb.step("s", s, s.activation(gsq[:, :], gz[:, :], AF.Square))

                            @blk.vector
                            def _(v):
                                v.wait_ge(kb.chsem["s"], kb.chcnt["s"])
                                kb.step("v", v, v.reduce_sum(rcol[:, 0:1], gsq[:, :],
                                                             axis=AX.X))
                        with nc.Block() as blk:
                            @blk.scalar
                            def _(s):
                                kb.step("s", s, s.activation(rcol[:, 1:2], rcol[:, 0:1], AF.Ln,
                                                             scale=kb.fc(1.0 / 2048.0),
                                                             bias=kb.fc(1e-5)))
                                kb.step("s", s, s.activation(rcol[:, 1:2], rcol[:, 1:2], AF.Exp,
                                                             scale=kb.fc(-0.5)))
                                kb.step("s", s, s.activation(gn_bf[:, :], gz[:, :], AF.Copy,
                                                             scale=rcol[:, 1:2]))
                        with nc.Block() as blk:
                            b_ttr2, b_sfe2 = C["ttr"], C["sfe"]

                            @blk.tensor
                            def _(t):
                                for pt in range(16):
                                    if pt >= 4:
                                        t.wait_ge(S["sfe"], b_sfe2 + pt - 3)
                                    t.transpose(ptg[pt % 4][:, :],
                                                gn_bf[:, pt * 128:(pt + 1) * 128],
                                                ident_bf[:, :]).then_inc(S["ttr"], 1)

                            @blk.scalar
                            def _(s):
                                for pt in range(16):
                                    s.wait_ge(S["ttr"], b_ttr2 + pt + 1)
                                    s.activation(gnf[:, pt, :], ptg[pt % 4][:, :],
                                                 AF.Copy).then_inc(S["sfe"], 1)
                            C["ttr"] += 16
                            C["sfe"] += 16
                        with nc.Block() as blk:
                            b_mmo, b_evo = C["mmo"], C["evo"]

                            @blk.tensor
                            def _(t):
                                t.wait_ge(S["ldw"], ldw_wo2)
                                for dout in range(8):
                                    if dout >= 2:
                                        t.wait_ge(S["evo"], b_evo + dout - 1)
                                    for k in range(16):
                                        ins = t.matmul(pout[dout % 2][:, :],
                                                       wo2[:, k, dout * 128:(dout + 1) * 128],
                                                       gnf[:, k, :],
                                                       start=(k == 0), stop=(k == 15))
                                    ins.then_inc(S["mmo"], 1)

                            @blk.vector
                            def _(v):
                                for dout in range(8):
                                    v.wait_ge(S["mmo"], b_mmo + dout + 1)
                                    v.tensor_add(x_g[:, dout, cc2 * 128:(cc2 + 1) * 128],
                                                 x_g[:, dout, cc2 * 128:(cc2 + 1) * 128],
                                                 pout[dout % 2][:, :]).then_inc(S["evo"], 1)
                            C["mmo"] += 8
                            C["evo"] += 8
                    with nc.Block() as blk:
                        @blk.sync
                        def _(sync):
                            kb.store(sync, xout_v[:, :, zg * 512:(zg + 1) * 512],
                                     x_g[:, :, :])


def build_ssm_nc(nch=32, php=9):
    kb = KB()
    x_in = kb.din("x", [D, T], F32)
    win = kb.din("win", [1024, 4256], F32)
    wout = kb.din("wout", [2048, 1024], F32)
    dtb = kb.din("dtb", [1, 32], F32)
    alog = kb.din("alog", [1, 32], F32)
    dsk = kb.din("dsk", [1, 32], F32)
    nw = kb.din("nw", [2048, 1], F32)
    y = kb.dout("y", [D, T], F32)
    xres = kb.dscr("xres", [D, T], F32)
    win_bf = kb.dscr("winbf", [1024, 4256], BF16)
    wout2_bf = kb.dscr("wout2bf", [2048, 1024], BF16)
    xn_scr = kb.dscr("xnscr", [D, T], BF16)
    z_scr = kb.dscr("zscr", [T, 2048], BF16)
    xs_scr = kb.dscr("xsscr", [T, 2048], BF16)
    bcd_scr = kb.dscr("bcdscr", [T, 160], F32)
    psf_scr = kb.dscr("psfscr", [32, 128], F32)
    kb.init()
    emit_xin_copy(kb, x_in, xres)
    emit_ssm_preconvert(kb, win, wout, nw, win_bf, wout2_bf)
    sx = SSMSems(kb)
    emit_ssm(kb, sx, xres, y, win_bf, wout2_bf, dtb, alog, dsk,
             xn_scr, z_scr, xs_scr, bcd_scr, psf_scr, nch=nch, php=php)
    nc = kb.finalize()
    return nc, {"c_fvals": kb.fvals_array()}


def build_l1ffn_nc():
    kb = KB()
    x_in = kb.din("x", [D, T], F32)
    k0 = kb.din("k0", [2048, 4096], U8)
    k1 = kb.din("k1", [4096, 4096], U8)
    k2 = kb.din("k2", [4096, 1024], U8)
    s0 = kb.din("s0", [1, 1], F32)
    s1 = kb.din("s1", [1, 1], F32)
    s2 = kb.din("s2", [1, 1], F32)
    y = kb.dout("y", [D, T], F32)
    k0e = kb.dscr("k0e", [1024, 4096], BF16)
    k1bf = kb.dscr("k1bf", [4096, 4096], BF16)
    k2bf = kb.dscr("k2bf", [4096, 1024], BF16)
    xres = kb.dscr("xres", [D, T], F32)
    kb.init()
    emit_xin_copy(kb, x_in, xres)
    st = kb.stack
    raw = emit_scalar_cols2(kb, st, {"s0": s0, "s1": s1, "s2": s2})
    fold = kb.sb("fold", [128, 3], F32)
    with kb.nc.Block() as blk:
        @blk.vector
        def _(v):
            v.tensor_scalar_mul(fold[:, 0:1], raw["s0"], kb.fc(4.0))
            v.tensor_scalar_mul(fold[:, 1:2], raw["s1"], kb.fc(2.0))
            v.tensor_scalar_mul(fold[:, 2:3], raw["s2"], kb.fc(0.5))
    scols = {"s0x2": fold[:, 0:1], "s1x2": fold[:, 1:2], "s2h": fold[:, 2:3]}
    emit_l1ffn_preconvert(kb, k0, k1, k2, k0e, k1bf, k2bf)
    ffs = L1FFNSems(kb)
    emit_l1ffn(kb, ffs, xres, y, k0e, k1bf, k2bf, scols)
    nc = kb.finalize()
    return nc, {"c_fvals": kb.fvals_array()}


def build_full_nc():
    kb = KB()
    x_in = kb.din("x", [D, T], F32)
    w_qkv = kb.din("w_qkv", [1024, 3072], F32)
    w_out = kb.din("w_out", [1024, 1024], F32)
    rw = kb.din("rw", [8, 1024], F32)
    w1 = kb.din("w1", [1024, 8192], F32)
    w2 = kb.din("w2", [8192, 1024], F32)
    win = kb.din("win", [1024, 4256], F32)
    wout = kb.din("wout", [2048, 1024], F32)
    dtb = kb.din("dtb", [1, 32], F32)
    alog = kb.din("alog", [1, 32], F32)
    dsk = kb.din("dsk", [1, 32], F32)
    nw = kb.din("nw", [2048, 1], F32)
    k0 = kb.din("k0", [2048, 4096], U8)
    k1 = kb.din("k1", [4096, 4096], U8)
    k2 = kb.din("k2", [4096, 1024], U8)
    s0 = kb.din("s0", [1, 1], F32)
    s1 = kb.din("s1", [1, 1], F32)
    s2 = kb.din("s2", [1, 1], F32)
    rope = kb.din("c_rope", [4, 128, T], F32)
    y = kb.dout("y", [D, T], F32)
    xres = kb.dscr("xres", [D, T], F32)
    w1bf = kb.dscr("w1bf", [1024, 8192], BF16)
    w2bf = kb.dscr("w2bf", [8192, 1024], BF16)
    wqkv_bf = kb.dscr("wqkvbf", [1024, 3072], BF16)
    wout_bf = kb.dscr("woutbf", [1024, 1024], BF16)
    rope_scr = kb.dscr("ropescr", [4, 128, T], F32)
    k0e = kb.dscr("k0e", [1024, 4096], BF16)
    k1bf = kb.dscr("k1bf", [4096, 4096], BF16)
    k2bf = kb.dscr("k2bf", [4096, 1024], BF16)
    win_bf = kb.dscr("winbf", [1024, 4256], BF16)
    wout2_bf = kb.dscr("wout2bf", [2048, 1024], BF16)
    xn_scr = kb.dscr("xnscr", [D, T], BF16)
    q_scr = kb.dscr("qscr", [8, 128, T], BF16)
    z_scr = kb.dscr("zscr", [T, 2048], BF16)
    xs_scr = kb.dscr("xsscr", [T, 2048], BF16)
    bcd_scr = kb.dscr("bcdscr", [T, 160], F32)
    psf_scr = kb.dscr("psfscr", [32, 128], F32)

    kb.init()
    emit_xin_copy(kb, x_in, xres)
    emit_moe_preconvert(kb, w1, w2, w1bf, w2bf)
    emit_attn_preconvert(kb, w_qkv, w_out, rope, wqkv_bf, wout_bf, rope_scr)
    emit_l1ffn_preconvert(kb, k0, k1, k2, k0e, k1bf, k2bf)
    emit_ssm_preconvert(kb, win, wout, nw, win_bf, wout2_bf)
    st = kb.stack
    raw = emit_scalar_cols2(kb, st, {"s0": s0, "s1": s1, "s2": s2})
    fold = kb.sb("fold", [128, 3], F32)
    with kb.nc.Block() as blk:
        @blk.vector
        def _(v):
            v.tensor_scalar_mul(fold[:, 0:1], raw["s0"], kb.fc(4.0))
            v.tensor_scalar_mul(fold[:, 1:2], raw["s1"], kb.fc(2.0))
            v.tensor_scalar_mul(fold[:, 2:3], raw["s2"], kb.fc(0.5))
    scols = {"s0x2": fold[:, 0:1], "s1x2": fold[:, 1:2], "s2h": fold[:, 2:3]}

    ms = MoESems(kb)
    asx = AttnSems(kb)
    ffs = L1FFNSems(kb)
    sx = SSMSems(kb)
    emit_moe(kb, ms, xres, xres, w1bf, w2bf, rw)
    emit_attn(kb, asx, xres, xres, wqkv_bf, wout_bf, rope_scr, xn_scr, q_scr)
    emit_moe(kb, ms, xres, xres, w1bf, w2bf, rw)
    emit_l1ffn(kb, ffs, xres, xres, k0e, k1bf, k2bf, scols)
    emit_ssm(kb, sx, xres, xres, win_bf, wout2_bf, dtb, alog, dsk,
             xn_scr, z_scr, xs_scr, bcd_scr, psf_scr)
    emit_l1ffn(kb, ffs, xres, y, k0e, k1bf, k2bf, scols)
    nc = kb.finalize()
    return nc, {"c_fvals": kb.fvals_array(), "c_rope": rope_tables()}


def kernel(**inputs):
    consts = host_constants()
    nc, extra = build_full_nc()
    consts.update(extra)
    x = np.asarray(inputs["L0_attn_x"])
    in_maps = []
    for b in range(NCORES):
        m = dict(consts)
        m["x"] = np.ascontiguousarray(x[b].T)
        m["w_qkv"] = np.asarray(inputs["L0_attn_w_qkv"], np.float32)
        m["w_out"] = np.asarray(inputs["L0_attn_w_out"], np.float32)
        m["rw"] = np.asarray(inputs["L0_ffn_router_w"], np.float32)
        m["w1"] = np.asarray(inputs["L0_ffn_w1"], np.float32)
        m["w2"] = np.asarray(inputs["L0_ffn_w2"], np.float32)
        m["win"] = np.asarray(inputs["L1_attn_W_in"], np.float32)
        m["wout"] = np.asarray(inputs["L1_attn_W_out"], np.float32)
        m["dtb"] = np.asarray(inputs["L1_attn_dt_bias"], np.float32).reshape(1, 32)
        m["alog"] = np.asarray(inputs["L1_attn_A_log"], np.float32).reshape(1, 32)
        m["dsk"] = np.asarray(inputs["L1_attn_D_skip"], np.float32).reshape(1, 32)
        m["nw"] = np.asarray(inputs["L1_attn_norm_weight"], np.float32).reshape(2048, 1)
        m["k0"] = np.asarray(inputs["L1_ffn_k0"]).view(np.uint8)
        m["k1"] = np.asarray(inputs["L1_ffn_k1"]).view(np.uint8)
        m["k2"] = np.asarray(inputs["L1_ffn_k2"]).view(np.uint8)
        m["s0"] = np.asarray(inputs["L1_ffn_s0"], np.float32).reshape(1, 1)
        m["s1"] = np.asarray(inputs["L1_ffn_s1"], np.float32).reshape(1, 1)
        m["s2"] = np.asarray(inputs["L1_ffn_s2"], np.float32).reshape(1, 1)
        in_maps.append(m)
    res = run_bass_kernel_spmd(nc, in_maps, core_ids=list(range(NCORES)))
    out = np.stack([np.ascontiguousarray(res.results[b]["y"].T)
                    for b in range(NCORES)]).astype(np.float32)
    return out
